# version 13
# baseline (speedup 1.0000x reference)
import numpy as np
import ml_dtypes

import concourse.bass as bass
import concourse.mybir as mybir
from concourse.bass_utils import run_bass_kernel_spmd

F32 = mybir.dt.float32
BF16 = mybir.dt.bfloat16
AF = mybir.ActivationFunctionType
ALU = mybir.AluOpType
AX = mybir.AxisListType
NP_BF16 = ml_dtypes.bfloat16

L1A_LIMIT = {}
L1F_LIMIT = {}

T = 2048
D = 1024
H = 16
HD = 64
KCONV = 4
NTC = 16
NDC = 8
E_MOE = 9
NIC = 22
MAGIC = float(np.float32(12582912.0))
NEG_BIG = -1e30
LN8 = float(np.log(0.125))

ENGS = ("te", "ve", "sc", "gp", "sy")
SEM_OF = {"te": "T", "ve": "V", "sc": "A", "gp": "G", "sy": "L"}


class Prog:
    def __init__(self):
        self.q = {e: [] for e in ENGS}
        self.cnt = {}
        self.last_wait = {e: {} for e in ENGS}
        self.floor = []

    def fence(self, sems=("T", "V", "A")):
        return [(s, self.val(s)) for s in sems if self.val(s) > 0]

    def set_floor(self, toks):
        self.floor = [t for t in toks if t is not None]

    def val(self, s):
        return self.cnt.get(s, 0)

    def emit(self, eng, fn, waits=(), incs=()):
        best = {}
        for tok in list(waits) + self.floor:
            if tok is None:
                continue
            s, v = tok
            assert v <= self.val(s), f"future wait {s}:{v} > {self.val(s)}"
            if v > best.get(s, -1):
                best[s] = v
        w = []
        for s, v in best.items():
            if self.last_wait[eng].get(s, -1) >= v:
                continue
            self.last_wait[eng][s] = v
            w.append((s, v))
        toks = []
        for s, n in incs:
            self.cnt[s] = self.val(s) + n
            toks.append((s, self.cnt[s]))
        self.q[eng].append((fn, tuple(w), tuple(incs)))
        return toks[0] if toks else None

    def te(self, fn, w=()):
        return self.emit("te", fn, w, (("T", 1),))

    def ve(self, fn, w=()):
        return self.emit("ve", fn, w, (("V", 1),))

    def sc(self, fn, w=()):
        return self.emit("sc", fn, w, (("A", 1),))

    def dma(self, eng, out_ap, in_ap, w=(), slow=False, sem=None):
        sem = sem or SEM_OF[eng]
        return self.emit(eng, lambda h, o=out_ap, i=in_ap, sl=slow: h.dma_start(
            out=o, in_=i, allow_slow_non_contiguous=sl),
            w, ((sem, 16),))

    def all(self, sem):
        return (sem, self.val(sem))


class Arena:
    def __init__(self, t, nbytes):
        self.t = t
        self.nbytes = nbytes
        self.off = 0
        self.marks = []
        self.hwm = 0

    def alloc(self, shape, dtype, name=""):
        pdim, nf = shape
        esz = 2 if dtype == BF16 else 4
        nb = nf * esz
        off = (self.off + 63) & ~63
        assert off + nb <= self.nbytes, f"arena OOM {name}: {off}+{nb}>{self.nbytes}"
        self.off = off + nb
        self.hwm = max(self.hwm, self.off)
        if name:
            self.offsets = getattr(self, "offsets", {})
            self.offsets.setdefault(name, (off, nb))
        f32len = (nb + 3) // 4
        ap = self.t[:, off // 4: off // 4 + f32len]
        if dtype == BF16:
            ap = ap.bitcast(BF16)
        return ap[:pdim, :nf]

    def mark(self):
        self.marks.append(self.off)

    def reset(self):
        self.off = self.marks.pop()


def bc64(col_ap):
    k = col_ap.shape[1]
    return col_ap.unsqueeze(2).broadcast_to((col_ap.shape[0], k, 64))


def r3(ap2d, k=16):
    return ap2d.rearrange("p (a b) -> p a b", a=k)


def build_program(tap=None, dbg=None, nocc=False):
    nc = bass.Bass("TRN2", target_bir_lowering=False)
    dr = {}

    def din(name, shape, dtype=F32):
        dr[name] = nc.dram_tensor(name, list(shape), dtype, kind="ExternalInput").ap()
        return dr[name]

    x_in = din("x", (T, D))
    y_out = nc.dram_tensor("y", [T, D], F32, kind="ExternalOutput").ap()
    dbg_out = (nc.dram_tensor("dbg", [128, 16384], F32, kind="ExternalOutput").ap()
               if dbg is not None else None)
    din("wqkv", (24, 128, NDC, 128))
    din("wconv", (24, 128, KCONV))
    din("wz", (NDC, 128, NDC, 128))
    din("wba", (NDC, 128, 48))
    din("wout", (NDC, 128, D))
    din("dtb", (16, 1))
    din("alog", (16, 1))
    din("w0f", (NDC, 128, D))
    din("gam0", (128, NDC))
    din("bet0", (128, NDC))
    din("wq", (NDC, 128, NDC, 128))
    din("wk", (NDC, 128, NDC, 128))
    din("wv", (NDC, 128, NDC, 128))
    din("wo", (NDC, 128, D))
    din("wr", (NDC, 128, E_MOE))
    din("wg", (E_MOE, NIC, 128, NDC, 128))
    din("wu", (E_MOE, NIC, 128, NDC, 128))
    din("wd", (E_MOE, NIC, 128, D))
    din("cid", (128, 128), BF16)
    din("cidf", (128, 128), F32)
    din("cP", (128, 128), BF16)
    din("ctri", (128, 128), F32)
    din("cim", (128, 128), F32)
    din("cbd16", (16, 1024), BF16)
    din("conesf", (1, 128), F32)
    din("ccos", (64, T), BF16)
    din("csin", (64, T), BF16)
    cc_in = nc.dram_tensor("cc_in", [1, 8], F32).ap()
    cc_out = nc.dram_tensor("cc_out", [1, 8], F32).ap()

    p = Prog()
    ctx = []

    def sbuf(name, shape, dtype):
        cm = nc.sbuf_tensor(name, list(shape), dtype)
        t = cm.__enter__()
        ctx.append(cm)
        return t

    arena_t = sbuf("arena", [128, 51904], F32)
    A = Arena(arena_t, 51904 * 4)
    build_program.arena = A
    x_tok = A.alloc([128, 16 * 1024], F32, "x_tok")

    cid = sbuf("sb_cid", [128, 128], BF16)
    cidf = sbuf("sb_cidf", [128, 128], F32)
    cP = sbuf("sb_cP", [128, 128], BF16)
    ctri = sbuf("sb_ctri", [128, 128], F32)
    cim = sbuf("sb_cim", [128, 128], F32)
    cbd16 = sbuf("sb_cbd16", [16, 1024], BF16)
    onesf = sbuf("sb_onesf", [1, 128], F32)
    dtb = sbuf("sb_dtb", [16, 1], F32)
    negea = sbuf("negea", [16, 1], F32)
    smalls = sbuf("smalls", [128, 16], F32)
    st6 = sbuf("st6", [128, 12], F32)
    mv = sbuf("mv", [128, 4], F32)
    cvals = sbuf("cvals", [128, 4], F32)
    nc.const_aps.register(F32, 1e-5, cvals[:, 0:1])
    nc.const_aps.register(F32, 1e-6, cvals[:, 1:2])
    nc.const_aps.register(F32, LN8, cvals[:, 2:3])

    cmA = nc.psum_tensor("psA", [128, 2048], F32)
    pA = cmA.__enter__(); ctx.append(cmA)
    cmB = nc.psum_tensor("psB", [128, 2048], F32)
    pB = cmB.__enter__(); ctx.append(cmB)

    sems = {}
    for s in ("L", "G", "T", "V", "A", "S", "C", "Lx", "Lc", "Lc2", "Lg", "Lr",
              "Lq", "Lo", "Lf", "G0", "G1", "G2", "G3", "G4", "G5", "G6", "G7",
              "Gw", "Go", "Go2", "Gr", "Gcs", "Gd"):
        cm = nc.semaphore(s)
        sems[s] = cm.__enter__()
        ctx.append(cm)

    GD = {}

    def gw(*keys):
        return [GD.get(k) for k in keys]

    def gs(key, tok):
        GD[key] = tok

    def maybe_dbg(label, ap, deps):
        if dbg != label:
            return None
        pdim, fdim = ap.shape[0], int(np.prod(ap.shape[1:]))
        return p.dma("gp", dbg_out[0:pdim, 0:fdim], ap, w=list(deps), sem="Gd")

    tk_cv = None
    for i, v in ((0, 1e-5), (1, 1e-6), (2, LN8)):
        tk_cv = p.ve(lambda h, i=i, v=v: h.memset(cvals[:, i:i + 1], v))
    x_ready = []
    for tc in range(NTC):
        p.dma("sy", x_tok[:, tc * 1024:(tc + 1) * 1024],
              x_in[tc * 128:(tc + 1) * 128, :], sem="Lx")
    x_ready = [p.all("Lx")] * NTC
    for sb_t, name in ((cid, "cid"), (cidf, "cidf"), (cP, "cP"), (ctri, "ctri"),
                       (cim, "cim"), (cbd16, "cbd16"), (onesf, "conesf"),
                       (dtb, "dtb")):
        p.dma("sy", sb_t[:, :], dr[name], sem="Lc")
    p.dma("sy", negea[:, :], dr["alog"], sem="Lc")
    tcons = p.all("Lc")
    tk_al = tcons
    tk_nega = p.sc(lambda h: h.activation(negea[:, :], negea[:, :], AF.Exp,
                                          bias=0.0, scale=1.0), w=[tk_al])
    tk_nega = p.ve(lambda h: h.tensor_scalar_mul(negea[:, :], negea[:, :], -1.0),
                   w=[tk_nega])

    def ln_tile(tc, dep, out_ap, extra=None):
        xsl = x_tok[:, tc * 1024:(tc + 1) * 1024]
        ex = extra or []
        t1 = p.ve(lambda h: h.tensor_scalar(out_ap, xsl, 1.0, 0.0, ALU.mult,
                                            ALU.add, accum_out=mv[:, 0:1]),
                  w=[dep, GD.get("lnstat")] + ex)
        t2 = p.ve(lambda h: h.scalar_tensor_tensor(out_ap, xsl, 1.0, xsl,
                                                   ALU.mult, ALU.mult,
                                                   accum_out=mv[:, 1:2]), w=[t1])
        t3 = p.ve(lambda h: h.tensor_scalar_mul(mv[:, 0:1], mv[:, 0:1],
                                                1.0 / 1024.0), w=[t2])
        t4 = p.ve(lambda h: h.tensor_scalar_mul(mv[:, 1:2], mv[:, 1:2],
                                                1.0 / 1024.0), w=[t3])
        t5 = p.ve(lambda h: h.tensor_scalar(smalls[:, 2:3], mv[:, 0:1], mv[:, 0:1],
                                            None, ALU.mult), w=[t4])
        t6 = p.ve(lambda h: h.tensor_sub(mv[:, 1:2], mv[:, 1:2], smalls[:, 2:3]),
                  w=[t5])
        t7 = p.sc(lambda h: h.activation(smalls[:, 1:2], mv[:, 1:2], AF.Sqrt,
                                         bias=1e-5, scale=1.0), w=[t6, tk_cv])
        t8 = p.ve(lambda h: h.reciprocal(smalls[:, 1:2], smalls[:, 1:2]), w=[t7])
        t9 = p.ve(lambda h: h.tensor_scalar(out_ap, xsl, mv[:, 0:1], smalls[:, 1:2],
                                            ALU.subtract, ALU.mult), w=[t8])
        gs("lnstat", t9)
        return t9

    def ln_to_xnd(tcs, xnd, lnscr, pbank_bf):
        toks = []
        xnd_re = xnd.rearrange("p (dc t) -> p dc t", dc=NDC)
        for i, tc in enumerate(tcs):
            sl = i % 2
            scr = lnscr[:, sl * 1024:(sl + 1) * 1024]
            tln = ln_tile(tc, x_ready[tc], scr, extra=gw(("lnscr", sl)))
            tt = None
            for dc in range(NDC):
                tt = p.te(lambda h, dc=dc, scr=scr: h.transpose(
                    pbank_bf[:, dc * 128:(dc + 1) * 128],
                    scr[:, dc * 128:(dc + 1) * 128], cid[:, :]),
                    w=[tln, tcons, GD.get("lnpsum")])
            gs(("lnscr", sl), tt)
            tev = p.ve(lambda h, i=i: h.tensor_copy(
                xnd_re[:, :, i * 128:(i + 1) * 128],
                pbank_bf[:, :].rearrange("p (dc t) -> p dc t", dc=NDC)), w=[tt])
            gs("lnpsum", tev)
            toks.append(tev)
        return toks

    A.mark()
    z_d = A.alloc([128, 8 * 1024], BF16, "z_d")
    O_ss = A.alloc([128, 8 * 16], F32, "Oss")
    S_sb = A.alloc([128, 8 * 64], F32, "S")
    S_bf = A.alloc([128, 8 * 64], BF16, "Sbf")
    G = A.alloc([16, 1024], F32, "G")
    carry = A.alloc([16, 8], F32, "carry")
    Gb = A.alloc([16, 8], F32, "Gb")
    Gbp = A.alloc([16, 8], F32, "Gbp")
    gblog = A.alloc([16, 8], F32, "gblog")
    glogT = A.alloc([1, 128], F32, "glogT")
    gbT = A.alloc([8, 16], F32, "gbT")
    ccar = A.alloc([128, 24 * 3], BF16, "ccar")
    O_raw = A.alloc([128, 8 * 1024], BF16, "Oraw")
    wconv_sb = A.alloc([128, 24 * KCONV], F32, "wconv")
    sqs = A.alloc([128, 1024], F32, "sqs")
    gcols = A.alloc([128, 16], F32, "gcols")
    q_d = A.alloc([128, 8 * 1024], BF16, "q_d")
    k_d = A.alloc([128, 8 * 1024], BF16, "k_d")
    v_d = A.alloc([128, 8 * 1024], BF16, "v_d")
    rows_st = A.alloc([128, 1024], F32, "rows")
    Grq = A.alloc([16, 1024], F32, "Grq")
    rkc = A.alloc([128, 2 * 16], F32, "rkc")
    mcolB = A.alloc([128, 2 * 16], F32, "mcol")
    colsB = A.alloc([128, 2 * 128], F32, "cols")
    lncB = A.alloc([128, 16], F32, "lnc")

    tk_car0 = p.ve(lambda h: h.memset(carry[:, :], 0.0))
    tk_rows0 = p.ve(lambda h: h.memset(rows_st[:, :], 0.0))
    tk_S0 = p.ve(lambda h: h.memset(S_sb[:, :], 0.0))
    tk_Sb0 = p.ve(lambda h: h.memset(S_bf[:, :], 0.0))
    tk_wconv = None
    tk_wout = [None] * NDC

    def l0a():
        nonlocal tk_wconv
        sil = {}
        fprev = []
        for hf in range(2):
            tcs = list(range(hf * 8, hf * 8 + 8))
            p.set_floor(fprev)
            A.mark()
            xnd = A.alloc([128, NDC * 1024], BF16, "l0a_xnd")
            xnd_re = xnd.rearrange("p (dc t) -> p dc t", dc=NDC)
            lnscr = A.alloc([128, 2 * 1024], BF16, "l0a_lnscr")
            wqs = A.alloc([128, 2 * NDC * 128], BF16, "l0a_wq")
            wba_sb = A.alloc([128, NDC * 48], BF16, "l0a_wba")
            mixed = A.alloc([128, 2 * 1027], BF16, "l0a_mx")
            diag = A.alloc([128, KCONV * 128], BF16, "l0a_dg")
            lnrq = A.alloc([16, 1024], F32, "l0a_lnrq")
            zrow = lnrq
            bdarg = Grq
            qts = lnscr[:, 0:1024]
            tk_zero = p.ve(lambda h: h.memset(lnrq[:, :], 0.0),
                           w=gw("lnrq_c"))
            if hf == 0:
                tk_wconv = p.dma("sy", wconv_sb[:, :],
                                 dr["wconv"].transpose([1, 0, 2]), sem="Lc2")
            tk_wba = p.dma("gp", wba_sb[:, :], dr["wba"].transpose([1, 0, 2]),
                           w=gw(("dmaq", "Gw")), sem="Gw")
            tmx0 = p.ve(lambda h: h.memset(mixed[:, :], 0.0), w=gw("mixed0", "mixed1"))

            pbf = pA[:, 0:512].bitcast(BF16)
            ln_toks = ln_to_xnd(tcs, xnd, lnscr, pbf)
            xnd_all = ln_toks[-1]
            if hf == 0:
                maybe_dbg("xn0", xnd[:, :], [xnd_all])

            pba = pB[0:48, 0:1024]
            tmm = None
            for dc in range(NDC):
                for s2 in range(2):
                    tmm = p.te(lambda h, dc=dc, s2=s2: h.matmul(
                        pba[:, s2 * 512:(s2 + 1) * 512],
                        wba_sb[:, dc * 48:(dc + 1) * 48],
                        xnd_re[:, dc, s2 * 512:(s2 + 1) * 512],
                        start=(dc == 0), stop=(dc == NDC - 1)),
                        w=[tk_wba, ln_toks[-1], GD.get("pB01"),
                           GD.get(("dS", 0)), GD.get(("dS", 64)),
                           GD.get("pB67c")])
            tbeta = p.sc(lambda h: h.activation(rows_st[0:16, :], pba[0:16, :],
                                                AF.Sigmoid, bias=0.0, scale=1.0),
                         w=[tmm, GD.get("rows_c")])
            su = p.sc(lambda h: h.activation(G[:, :], pba[32:48, :], AF.Abs,
                                             bias=dtb[:, 0:1], scale=1.0),
                      w=[tmm, tcons, GD.get("G_c")])
            se = p.sc(lambda h: h.activation(G[:, :], G[:, :], AF.Exp,
                                             bias=0.0, scale=-1.0), w=[su])
            slg = p.sc(lambda h: h.activation(G[:, :], G[:, :], AF.Ln,
                                              bias=1.0, scale=1.0), w=[se])
            sr = p.sc(lambda h: h.activation(bdarg[:, :], pba[32:48, :], AF.Relu,
                                             bias=dtb[:, 0:1], scale=1.0), w=[tmm])
            gs("pB01", sr)
            gs(("dmaq", "Gw"), tmm)
            if hf == 0 and dbg == "balog":
                tdbg = p.ve(lambda h: h.tensor_copy(sqs[:, :], pba[0:48, :][0:48, :]),
                            w=[tmm]) if False else None
                tdbg = p.ve(lambda h: h.tensor_copy(sqs[0:48, :], pba[0:48, :]),
                            w=[tmm])
                maybe_dbg("balog", sqs[0:48, :], [tdbg])
            tsp = p.ve(lambda h: h.tensor_add(G[:, :], G[:, :], bdarg[:, :]),
                       w=[slg, sr])
            tg = p.ve(lambda h: h.tensor_scalar_mul(G[:, :], G[:, :], negea[:, 0:1]),
                      w=[tsp, tk_nega])
            tG = p.ve(lambda h: h.tensor_tensor_scan(G[:, :], zrow[:, :], G[:, :],
                                                     carry[:, 0:1], ALU.add, ALU.add),
                      w=[tg, tk_zero, tk_car0])
            if hf == 0:
                maybe_dbg("g0", G[:, :], [tG])
                maybe_dbg("beta0", rows_st[0:16, :], [tbeta])
            tGb = p.ve(lambda h: h.tensor_copy(Gb[:, :], G[:, 127::128]), w=[tG])
            tp1 = p.ve(lambda h: h.tensor_copy(Gbp[:, 0:1], carry[:, 0:1]), w=[tGb])
            tp2 = p.ve(lambda h: h.tensor_copy(Gbp[:, 1:8], Gb[:, 0:7]), w=[tGb])
            tcar = p.ve(lambda h: h.tensor_copy(carry[:, 0:1], G[:, 1023:1024]),
                        w=[tG, tp1])
            tgl = p.ve(lambda h: h.tensor_sub(gblog[:, :], Gb[:, :], Gbp[:, :]),
                       w=[tGb, tp1, tp2])
            tg1 = p.te(lambda h: h.transpose(pB[0:8, 1024:1040], gblog[:, :],
                                             cidf[0:16, 0:16]),
                       w=[tgl, GD.get("pB45")])
            tg2 = p.ve(lambda h: h.tensor_copy(gbT[:, :], pB[0:8, 1024:1040]),
                       w=[tg1, GD.get("gbT_c")])
            gs("pB45", tg2)
            gs("pB67b", tg2)
            tglT = p.dma("sy", glogT[0:1, :].rearrange("o (c hh) -> o c hh", c=8),
                         gbT[:, :], w=[tg2, GD.get(("dmaq", "Lg"))], slow=True,
                         sem="Lg")
            gs("gbT_c", tglT)
            tneg = None
            for c in range(8):
                tneg = p.ve(lambda h, c=c: h.tensor_scalar(
                    G[:, c * 128:(c + 1) * 128], G[:, c * 128:(c + 1) * 128],
                    Gbp[:, c:c + 1], -1.0, ALU.subtract, ALU.mult), w=[tgl, tcar])
            tbd = None
            for c in range(8):
                tbd = p.ve(lambda h, c=c: h.tensor_scalar(
                    rows_st[32:48, c * 128:(c + 1) * 128],
                    G[:, c * 128:(c + 1) * 128],
                    gblog[:, c:c + 1], None, ALU.add), w=[tneg, GD.get("rows_c")])
            tbd = p.sc(lambda h: h.activation(rows_st[32:48, :], rows_st[32:48, :],
                                              AF.Exp, bias=0.0, scale=1.0), w=[tbd])
            tng = p.ve(lambda h: h.tensor_copy(rows_st[64:80, :], G[:, :]), w=[tneg])

            pmxL = [pA[:, 1024:2048], pA[:, 0:1024]]
            pcvL = [pB[:, 0:1024], pB[:, 1024:2048]]
            for cc in range(32):
                sl = cc % 2
                pmx = pmxL[sl]
                pcv = pcvL[sl]
                is_z = cc >= 24
                wsrc = dr["wz"][cc - 24] if is_z else dr["wqkv"][cc]
                wdst = wqs[:, sl * NDC * 128:(sl + 1) * NDC * 128]
                gsem = "G0" if sl == 0 else "G1"
                twl = p.dma("gp", wdst, wsrc.rearrange("p dc c -> p (dc c)"),
                            w=gw(("dmaq", gsem)), sem=gsem)
                tm = None
                for dc in range(NDC):
                    for s2 in range(2):
                        tm = p.te(lambda h, dc=dc, s2=s2, wdst=wdst, pmx=pmx:
                                  h.matmul(
                            pmx[:, s2 * 512:(s2 + 1) * 512],
                            wdst[:, dc * 128:(dc + 1) * 128],
                            xnd_re[:, dc, s2 * 512:(s2 + 1) * 512],
                            start=(dc == 0), stop=(dc == NDC - 1)),
                            w=[twl, xnd_all, GD.get(("pAm", sl))])
                gs(("dmaq", gsem), tm)
                if is_z:
                    dst = z_d[:, (cc - 24) * 1024:(cc - 23) * 1024]
                    tsl = p.sc(lambda h, dst=dst, pmx=pmx: h.activation(
                        dst, pmx[:, :], AF.Silu, bias=0.0, scale=1.0),
                        w=[tm, GD.get(("zd", cc - 24))])
                    gs(("pAm", sl), tsl)
                    sil[("z", cc - 24, hf)] = tsl
                    continue
                msl = mixed[:, sl * 1027:(sl + 1) * 1027]
                tev = p.sc(lambda h, msl=msl, pmx=pmx: h.copy(msl[:, 3:1027],
                                                              pmx[:, :]),
                           w=[tm, tmx0, GD.get(("mix", sl)), GD.get(("mixv", sl))])
                gs(("pAm", sl), tev)
                if hf == 1:
                    tcc2 = p.ve(lambda h, msl=msl, cc=cc: h.tensor_copy(
                        msl[:, 0:3], ccar[:, cc * 3:cc * 3 + 3]),
                        w=[sil.get(("ccar", cc)), tev, GD.get(("mix", sl))])
                else:
                    tcc2 = tev
                tsv = p.ve(lambda h, msl=msl, cc=cc: h.tensor_copy(
                    ccar[:, cc * 3:cc * 3 + 3], msl[:, 1024:1027]),
                    w=[tev, tcc2])
                gs(("mixv", sl), tsv)
                if hf == 0 and cc == 0:
                    maybe_dbg("mx0", msl[:, 0:1027], [tev, tcc2])
                sil[("ccar", cc)] = tsv
                td = None
                for j in range(KCONV):
                    td = p.ve(lambda h, j=j, cc=cc: h.tensor_scalar_mul(
                        diag[:, j * 128:(j + 1) * 128], cim[:, :],
                        wconv_sb[:, cc * KCONV + j:cc * KCONV + j + 1]),
                        w=[tk_wconv, tcons, GD.get("diag")])
                tcv = None
                for j in range(KCONV):
                    for s2 in range(2):
                        tcv = p.te(lambda h, j=j, s2=s2, msl=msl, pcv=pcv: h.matmul(
                            pcv[:, s2 * 512:(s2 + 1) * 512],
                            diag[:, j * 128:(j + 1) * 128],
                            msl[:, j + s2 * 512: j + s2 * 512 + 512],
                            start=(j == 0), stop=(j == KCONV - 1)),
                            w=[td, tcc2, tsv, GD.get(("pBc", sl)),
                               GD.get("gbT_c"), GD.get("pB45")])
                gs("diag", tcv)
                gs(("mix", sl), tcv)
                dst = (q_d if cc < 8 else k_d if cc < 16 else v_d)[
                    :, (cc % 8) * 1024:(cc % 8 + 1) * 1024]
                tsl = p.sc(lambda h, dst=dst, pcv=pcv: h.activation(
                    dst, pcv[:, :], AF.Silu, bias=0.0, scale=1.0),
                    w=[tcv, GD.get(("qkv", cc))])
                gs(("pBc", sl), tsl)
                sil[(cc, hf)] = tsl

            if hf == 0:
                maybe_dbg("q0", q_d[:, :], [sil[(qc2, 0)] for qc2 in range(8)])
                maybe_dbg("k0", k_d[:, :], [sil[(8 + qc2, 0)] for qc2 in range(8)])
                maybe_dbg("v0", v_d[:, :], [sil[(16 + qc2, 0)] for qc2 in range(8)])
                maybe_dbg("z0", z_d[:, :], [sil[("z", zc2, 0)] for zc2 in range(8)])
            gs("lnpsum", GD.get(("pAm", 1)))
            gs("pB45", GD.get(("pBc", 1)))
            gs("pB67b", GD.get(("pBc", 1)))
            gs("pB67a", GD.get(("pBc", 1)))
            gs("pB67c", GD.get(("pBc", 1)))
            gs("pB01", GD.get(("pBc", 0)))
            gs("pA23", GD.get(("pAm", 0)))
            gs("pA0r", GD.get(("pAm", 1)))
            gs("pA1r", GD.get(("pAm", 1)))
            gs("pOr", GD.get(("pAm", 0)))
            gs(("dS", 0), GD.get(("pBc", 0)))
            gs(("dS", 64), GD.get(("pBc", 0)))
            pqt = pA[:, 0:512].bitcast(BF16)
            plr = pB[0:16, 1024:1152]
            tprev = None
            for i in range(8):
                tt = None
                for qc in range(NDC):
                    tt = p.te(lambda h, qc=qc, i=i: h.transpose(
                        pqt[:, qc * 128:(qc + 1) * 128],
                        q_d[:, qc * 1024 + i * 128: qc * 1024 + (i + 1) * 128],
                        cid[:, :]), w=[sil[(qc, hf)], GD.get("lnpsum")])
                t1 = p.ve(lambda h: h.tensor_copy(qts[:, :], pqt[:, :]),
                          w=[tt, tprev, GD.get(("lnscr", 0)), GD.get(("lnscr", 1))])
                gs("lnpsum", t1)
                t2 = p.ve(lambda h: h.tensor_mul(sqs[:, :], qts[:, :], qts[:, :]),
                          w=[t1, GD.get("sqs_c")])
                t3 = p.ve(lambda h: h.tensor_reduce(
                    lncB[:, 0:16], sqs[:, :].rearrange("p (hh d) -> p hh d", d=64),
                    AX.X, ALU.add), w=[t2, GD.get("lnc_c")])
                gs("sqs_c", t3)
                t4 = p.sc(lambda h: h.activation(lncB[:, 0:16], lncB[:, 0:16], AF.Ln,
                                                 bias=1e-6, scale=1.0), w=[t3])
                t5 = p.ve(lambda h: h.tensor_scalar(lncB[:, 0:16], lncB[:, 0:16],
                                                    -0.5, LN8, ALU.mult, ALU.add),
                          w=[t4])
                t6 = p.te(lambda h: h.transpose(plr[:, 0:128], lncB[:, 0:16],
                                                cidf[:, :]), w=[t5, GD.get("pB45")])
                gs("lnc_c", t6)
                tprev = p.ve(lambda h, i=i: h.tensor_copy(
                    lnrq[:, i * 128:(i + 1) * 128], plr[0:16, 0:128]), w=[t6])
                gs("pB45", tprev)
            tgrq = p.ve(lambda h: h.tensor_sub(Grq[:, :], lnrq[:, :], G[:, :]),
                        w=[tprev, tng, GD.get("Grq_c")])
            gs("lnrq_c", tgrq)
            if hf == 0:
                maybe_dbg("lnrq0", lnrq[:, :], [tprev])
                maybe_dbg("negG0", G[:, :], [tng])
                maybe_dbg("grq0", Grq[:, :], [tgrq])
            tgrq = p.ve(lambda h: h.tensor_copy(rows_st[96:112, :], Grq[:, :]),
                        w=[tgrq])
            gs("rows_w", tgrq)

            fA = p.fence()
            A.reset()
            p.set_floor(fA)

            A.mark()
            wout_sb = A.alloc([128, NDC * 1024], BF16, "l0a_wout")
            for dc in range(NDC):
                p.dma("gp", wout_sb[:, dc * 1024:(dc + 1) * 1024],
                      dr["wout"][dc], sem="Go")
            woutall = p.all("Go")
            for dc in range(NDC):
                tk_wout[dc] = woutall

            A.mark()
            ktok = A.alloc([128, 2 * 1024], BF16, "l0a_ktok")
            vbet = A.alloc([128, 2 * 1024], BF16, "l0a_vb")
            ktil = A.alloc([128, 2 * 1024], BF16, "l0a_ktil")
            Msk = A.alloc([128, 128], F32, "l0a_Msk")
            Ebf = A.alloc([128, 128], BF16, "l0a_E")
            Abf = A.alloc([128, 128], BF16, "l0a_A")
            qtil = A.alloc([128, 128], BF16, "l0a_qt")
            abuf = A.alloc([128, 128], F32, "l0a_ab")

            pAt = pA[:, 0:512]
            pMb = pA[:, 512:1024]
            pO = pA[:, 1024:2048]
            pDS = pB[:, 0:512]
            pGcn = pB[:, 512:1024]
            pKT = pB[:, 1024:1536].bitcast(BF16)
            for c in range(8):
                sl = c % 2
                colss = colsB[:, sl * 128:(sl + 1) * 128]
                tcm = p.te(lambda h, c=c: h.transpose(
                    pGcn[:, 0:128], rows_st[:, c * 128:(c + 1) * 128],
                    cidf[:, :]),
                    w=[tbeta, tbd, tng, tgrq, tk_rows0, GD.get("pB67a")])
                tcol = p.ve(lambda h, colss=colss: h.tensor_copy(colss,
                                                                 pGcn[:, 0:128]),
                            w=[tcm, GD.get(("cols", sl))])
                gs("pB67a", tcol)
                ksl = ktok[:, sl * 1024:(sl + 1) * 1024]
                vbsl = vbet[:, sl * 1024:(sl + 1) * 1024]
                ktsl = ktil[:, sl * 1024:(sl + 1) * 1024]
                tt = None
                for qc in range(NDC):
                    tt = p.te(lambda h, qc=qc, c=c: h.transpose(
                        pKT[:, qc * 128:(qc + 1) * 128],
                        k_d[:, qc * 1024 + c * 128: qc * 1024 + (c + 1) * 128],
                        cid[:, :]), w=[sil[(8 + qc, hf)], GD.get("pB67b")])
                tke = p.ve(lambda h, ksl=ksl: h.tensor_copy(ksl, pKT[:, :]),
                           w=[tt, GD.get(("ktok", sl))])
                gs("pB67b", tke)
                tt = None
                for qc in range(NDC):
                    tt = p.te(lambda h, qc=qc, c=c: h.transpose(
                        pKT[:, qc * 128:(qc + 1) * 128],
                        v_d[:, qc * 1024 + c * 128: qc * 1024 + (c + 1) * 128],
                        cid[:, :]), w=[sil[(16 + qc, hf)], tke])
                tve = p.ve(lambda h, vbsl=vbsl, colss=colss: h.tensor_tensor(
                    r3(vbsl), r3(pKT[:, :]), bc64(colss[:, 0:16]), ALU.mult),
                    w=[tt, tcol, GD.get(("vb", sl))])
                gs("pB67b", tve)
                t2 = p.ve(lambda h, ksl=ksl: h.tensor_mul(sqs[:, :], ksl, ksl),
                          w=[tke, GD.get("sqs_c")])
                t3 = p.ve(lambda h, sl=sl: h.tensor_reduce(
                    rkc[:, sl * 16:(sl + 1) * 16],
                    sqs[:, :].rearrange("p (hh d) -> p hh d", d=64), AX.X, ALU.add),
                    w=[t2, GD.get(("rkc", sl))])
                gs("sqs_c", t3)
                t4 = p.sc(lambda h, sl=sl: h.activation(
                    rkc[:, sl * 16:(sl + 1) * 16], rkc[:, sl * 16:(sl + 1) * 16],
                    AF.Sqrt, bias=1e-6, scale=1.0), w=[t3])
                trk = p.ve(lambda h, sl=sl: h.reciprocal(
                    rkc[:, sl * 16:(sl + 1) * 16], rkc[:, sl * 16:(sl + 1) * 16]),
                    w=[t4])
                tm1 = p.ve(lambda h, sl=sl, colss=colss: h.tensor_mul(
                    mcolB[:, sl * 16:(sl + 1) * 16], rkc[:, sl * 16:(sl + 1) * 16],
                    colss[:, 32:48]), w=[trk, tcol, GD.get(("mcol", sl))])
                tkt = p.ve(lambda h, ktsl=ktsl, ksl=ksl, sl=sl: h.tensor_tensor(
                    r3(ktsl), r3(ksl), bc64(mcolB[:, sl * 16:(sl + 1) * 16]),
                    ALU.mult), w=[tm1, GD.get(("ktil", sl))])
                if hf == 0 and c == 0:
                    maybe_dbg("cols0", colss, [tcol])
                    maybe_dbg("ktok0", ksl, [tke])
                    maybe_dbg("vb0", vbsl, [tve])
                    maybe_dbg("ktil0", ktsl, [tkt])
                    maybe_dbg("rkc0", rkc[:, 0:16], [trk])
                tgc1 = p.te(lambda h, c=c: h.matmul(
                    pGcn[:, 128:144], onesf[0:1, 0:128], glogT[0:1, c * 16:(c + 1) * 16],
                    start=True, stop=True), w=[tglT, GD.get("pB67c")])
                tgc = p.sc(lambda h: h.activation(gcols[:, :], pGcn[:, 128:144],
                                                  AF.Exp, bias=0.0, scale=1.0),
                           w=[tgc1, GD.get("gcols_c")])
                gs("pB67c", tgc)
                gs(("dmaq", "Lg"), tgc1)

                oh = []
                for hh in range(16):
                    qc = hh // 2
                    pb0 = (hh % 2) * 64
                    ksrc = k_d[pb0:pb0 + 64,
                               qc * 1024 + c * 128: qc * 1024 + (c + 1) * 128]
                    qsrc = q_d[pb0:pb0 + 64,
                               qc * 1024 + c * 128: qc * 1024 + (c + 1) * 128]
                    tA = p.te(lambda h, ksrc=ksrc, qsrc=qsrc: h.matmul(
                        pAt[:, 0:128], ksrc, qsrc, start=True, stop=True),
                        w=[sil[(qc, hf)], sil[(8 + qc, hf)], GD.get("pA0r")])
                    tMb = p.te(lambda h, hh=hh, colss=colss: h.transpose(
                        pMb[:, 0:128],
                        colss[:, 96 + hh:97 + hh].broadcast_to((128, 128)),
                        cidf[:, :]), w=[tcol, GD.get("pA1r")])
                    tms = p.ve(lambda h: h.tensor_add(Msk[:, :], pMb[:, 0:128],
                                                      ctri[:, :]),
                               w=[tMb, GD.get("Msk_c")])
                    tE = p.sc(lambda h, hh=hh, colss=colss: h.activation(
                        Ebf[:, :], Msk[:, :], AF.Exp,
                        bias=colss[:, 64 + hh:65 + hh], scale=1.0),
                        w=[tms, GD.get("E_c")])
                    gs("Msk_c", tE)
                    tab = p.sc(lambda h, pb0=pb0: h.activation(
                        abuf[pb0:pb0 + 64, :], pMb[pb0:pb0 + 64, 0:128], AF.Exp,
                        bias=LN8, scale=1.0), w=[tMb, tk_cv, GD.get("ab_c")])
                    gs("pA1r", tab)
                    tAt = p.ve(lambda h, sl=sl, hh=hh: h.scalar_tensor_tensor(
                        Abf[:, :], pAt[:, 0:128],
                        rkc[:, sl * 16 + hh:sl * 16 + hh + 1], Ebf[:, :],
                        ALU.mult, ALU.mult), w=[tA, tE, trk, GD.get("Abf_c")])
                    gs("pA0r", tAt)
                    gs("E_c", tAt)
                    tqt = p.ve(lambda h, qsrc=qsrc, pb0=pb0: h.tensor_tensor(
                        qtil[pb0:pb0 + 64, :], qsrc, abuf[pb0:pb0 + 64, :], ALU.mult),
                        w=[tab, GD.get("qtil_c")])
                    gs("ab_c", tqt)
                    if hf == 0 and c == 0 and hh == 0:
                        maybe_dbg("e00", Ebf[:, :], [tE])
                        maybe_dbg("abf00", Abf[:, :], [tAt])
                        maybe_dbg("qt00", qtil[0:64, :], [tqt])
                        maybe_dbg("msk00", Msk[:, :], [tms])
                    to1 = p.te(lambda h, hh=hh, pb0=pb0: h.matmul(
                        pO[:, hh * 64:(hh + 1) * 64], qtil[pb0:pb0 + 64, :],
                        S_bf[pb0:pb0 + 64, (hh // 2) * 64:(hh // 2 + 1) * 64],
                        start=True, stop=False),
                        w=[tqt, GD.get(("Sbf", hh)) or tk_Sb0, GD.get("pOr")])
                    to2 = p.te(lambda h, hh=hh, ktsl=ktsl, vbsl=vbsl: h.matmul(
                        pO[:, hh * 64:(hh + 1) * 64], Abf[:, :],
                        vbsl[:, hh * 64:(hh + 1) * 64], start=False, stop=True),
                        w=[tAt, tve])
                    gs("qtil_c", to1)
                    gs("Abf_c", to2)
                    oh.append(to2)
                    tds = p.te(lambda h, hh=hh, pb0=pb0, ktsl=ktsl, vbsl=vbsl:
                               h.matmul(pDS[pb0:pb0 + 64, 0:64],
                                        ktsl[:, hh * 64:(hh + 1) * 64],
                                        vbsl[:, hh * 64:(hh + 1) * 64],
                                        start=True, stop=True,
                                        tile_position=(0, pb0)),
                               w=[tkt, tve, GD.get(("dS", pb0))])
                    tS = p.ve(lambda h, hh=hh, pb0=pb0: h.scalar_tensor_tensor(
                        S_sb[pb0:pb0 + 64, (hh // 2) * 64:(hh // 2 + 1) * 64],
                        S_sb[pb0:pb0 + 64, (hh // 2) * 64:(hh // 2 + 1) * 64],
                        gcols[pb0:pb0 + 64, hh:hh + 1],
                        pDS[pb0:pb0 + 64, 0:64], ALU.mult, ALU.add),
                        w=[tds, tgc, to1, GD.get(("S", hh)) or tk_S0])
                    gs(("dS", pb0), tS)
                    gs(("S", hh), tS)
                    gs("gcols_c", tS)
                    tSb = p.ve(lambda h, hh=hh, pb0=pb0: h.tensor_copy(
                        S_bf[pb0:pb0 + 64, (hh // 2) * 64:(hh // 2 + 1) * 64],
                        S_sb[pb0:pb0 + 64, (hh // 2) * 64:(hh // 2 + 1) * 64]),
                        w=[tS])
                    gs(("Sbf", hh), tSb)
                tsq_ = p.sc(lambda h: h.activation(sqs[:, :], pO[:, :], AF.Square,
                                                   bias=0.0, scale=1.0),
                            w=oh + gw("sqs_c"))
                tss = p.ve(lambda h, c=c: h.tensor_reduce(
                    O_ss[:, c * 16:(c + 1) * 16],
                    sqs[:, :].rearrange("p (hh d) -> p hh d", d=64), AX.X, ALU.add),
                    w=[tsq_] + gw("Oss_c"))
                gs("sqs_c", tss)
                tor = p.ve(lambda h, c=c: h.tensor_copy(
                    O_raw[:, c * 1024:(c + 1) * 1024], pO[:, :]),
                    w=[tss] + oh + gw(("Oraw", c)))
                gs("pOr", tor)
                if hf == 0 and c == 0:
                    maybe_dbg("o0", O_raw[:, 0:1024], [tor])
                    maybe_dbg("ss0", O_ss[:, 0:16], [tss])
                    maybe_dbg("sbf0", S_bf[:, :], [GD.get(("Sbf", 15))])
                    maybe_dbg("gcols0", gcols[:, :], [tgc])
                gs(("Braw", c), tor)
                gs(("cols", sl), tE)
                gs(("ktok", sl), tkt)
                gs(("vb", sl), tds)
                gs(("ktil", sl), tds)
                gs(("rkc", sl), tAt)
                gs(("mcol", sl), tkt)
            fB = p.fence()
            A.reset()
            p.set_floor(fB)

            A.mark()
            ztok = A.alloc([128, 2 * 1024], BF16, "l0a_zt")
            ofin = A.alloc([128, 2 * 1024], BF16, "l0a_of")
            ofd = A.alloc([128, 2 * 1024], BF16, "l0a_ofd")
            O_raw2 = O_raw
            pzt = pA[:, 0:512].bitcast(BF16)
            pyw = pA[:, 1024:2048]
            pof = pB[:, 0:512].bitcast(BF16)
            for c in range(8):
                tc = hf * 8 + c
                sl = c % 2
                tt = None
                for zc in range(NDC):
                    tt = p.te(lambda h, zc=zc, c=c: h.transpose(
                        pzt[:, zc * 128:(zc + 1) * 128],
                        z_d[:, zc * 1024 + c * 128: zc * 1024 + (c + 1) * 128],
                        cid[:, :]),
                        w=[sil[("z", zc, hf)], GD.get("pA0r"), GD.get("lnpsum")])
                tzs = p.ve(lambda h, sl=sl: h.tensor_copy(
                    ztok[:, sl * 1024:(sl + 1) * 1024], pzt[:, :]),
                    w=[tt, GD.get(("ztok", sl))])
                gs("pA0r", tzs)
                gs("lnpsum", tzs)
                trs = p.sc(lambda h, c=c: h.activation(
                    lncB[:, 0:16], O_ss[:, c * 16:(c + 1) * 16], AF.Sqrt,
                    bias=1e-6, scale=1.0 / 64.0),
                    w=[GD.get(("Braw", c)), GD.get("lnc_c")])
                trs = p.ve(lambda h: h.reciprocal(lncB[:, 0:16], lncB[:, 0:16]),
                           w=[trs])
                gs("Oss_c", trs)
                t1 = p.ve(lambda h, c=c: h.tensor_tensor(
                    r3(sqs[:, :]), r3(O_raw2[:, c * 1024:(c + 1) * 1024]),
                    bc64(lncB[:, 0:16]), ALU.mult),
                    w=[trs, GD.get(("Braw", c)), GD.get("sqs_c")])
                gs("lnc_c", t1)
                gs(("Oraw", c), t1)
                t2 = p.ve(lambda h, sl=sl: h.tensor_tensor(
                    ofin[:, sl * 1024:(sl + 1) * 1024], sqs[:, :],
                    ztok[:, sl * 1024:(sl + 1) * 1024], ALU.mult),
                    w=[t1, tzs, GD.get(("ofin", sl))])
                gs("sqs_c", t2)
                gs(("ztok", sl), t2)
                if hf == 0 and c == 0:
                    maybe_dbg("zt0", ztok[:, 0:1024], [tzs])
                    maybe_dbg("of0", ofin[:, 0:1024], [t2])
                tt = None
                for zc in range(NDC):
                    tt = p.te(lambda h, zc=zc, sl=sl: h.transpose(
                        pof[:, zc * 128:(zc + 1) * 128],
                        ofin[:, sl * 1024 + zc * 128: sl * 1024 + (zc + 1) * 128],
                        cid[:, :]), w=[t2, GD.get("pB01")])
                gs(("ofin", sl), tt)
                tfd = p.ve(lambda h, sl=sl: h.tensor_copy(
                    ofd[:, sl * 1024:(sl + 1) * 1024], pof[:, :]),
                    w=[tt, GD.get(("ofd", sl))])
                gs("pB01", tfd)
                ty = None
                for dc in range(NDC):
                    for s2 in range(2):
                        ty = p.te(lambda h, dc=dc, s2=s2, sl=sl: h.matmul(
                            pyw[:, s2 * 512:(s2 + 1) * 512],
                            ofd[:, sl * 1024 + dc * 128: sl * 1024 + (dc + 1) * 128],
                            wout_sb[:, dc * 1024 + s2 * 512: dc * 1024 + (s2 + 1) * 512],
                            start=(dc == 0), stop=(dc == NDC - 1)),
                            w=[tfd, tk_wout[dc], GD.get("pA23")])
                gs(("ofd", sl), ty)
                tres = p.ve(lambda h, tc=tc: h.scalar_tensor_tensor(
                    x_tok[:, tc * 1024:(tc + 1) * 1024], pyw[:, :], 1.0,
                    x_tok[:, tc * 1024:(tc + 1) * 1024], ALU.mult, ALU.add),
                    w=[ty, x_ready[tc]])
                gs("pA23", tres)
                gs(("zd", c), tres)
                x_ready[tc] = tres
            fprev = p.fence()
            A.reset()
            A.reset()

    l0a()
    p.set_floor(p.fence())
    A.reset()
    if tap == "l0a":
        return _finish(nc, p, sems, x_tok, y_out, x_ready, ctx)

    A.mark()
    w0 = A.alloc([128, NDC * 1024], F32, "l0f_w")
    tern = A.alloc([128, NDC * 1024], BF16, "l0f_t")
    gbrow = A.alloc([128, 1024], F32, "l0f_g")
    bbrow = A.alloc([128, 1024], F32, "l0f_b")
    gcol0 = A.alloc([128, NDC], F32, "l0f_gc")
    bcol0 = A.alloc([128, NDC], F32, "l0f_bc")
    xaf = A.alloc([128, 1024], F32, "l0f_xa")
    scr0 = A.alloc([128, 1024], F32, "l0f_s")
    qbf = A.alloc([128, 1024], BF16, "l0f_q")
    xqd = A.alloc([128, 1024], BF16, "l0f_xqd")
    red0 = A.alloc([128, 16], F32, "l0f_r")
    stat1 = A.alloc([1, 256], F32, "l0f_st")

    def l0f():
        for dc in range(NDC):
            p.dma("sy", w0[:, dc * 1024:(dc + 1) * 1024], dr["w0f"][dc], sem="Lf")
        p.dma("sy", gcol0[:, :], dr["gam0"], sem="Lf")
        p.dma("sy", bcol0[:, :], dr["bet0"], sem="Lf")
        tw = [p.all("Lf")] * NDC
        tg = p.all("Lf")
        tbc = tg
        pg = pB[:, 0:512]
        tpr = None
        for dc in range(NDC):
            t1 = p.te(lambda h, dc=dc: h.transpose(
                pg[:, 0:128], gcol0[:, dc:dc + 1].broadcast_to((128, 128)),
                cidf[:, :]), w=[tg, tcons, tpr, GD.get("pB01"),
                                GD.get("pB0chain")])
            t2 = p.ve(lambda h, dc=dc: h.tensor_copy(
                gbrow[:, dc * 128:(dc + 1) * 128], pg[:, 0:128]), w=[t1])
            t3 = p.te(lambda h, dc=dc: h.transpose(
                pg[:, 128:256], bcol0[:, dc:dc + 1].broadcast_to((128, 128)),
                cidf[:, :]), w=[tbc, t2])
            tpr = p.ve(lambda h, dc=dc: h.tensor_copy(
                bbrow[:, dc * 128:(dc + 1) * 128], pg[:, 128:256]), w=[t3])
            gs("pB0chain", tpr)
        gs("pB01", tpr)

        def xaff(tc):
            t7 = ln_tile(tc, x_ready[tc], xaf[:, :], extra=gw("xa_c"))
            t8 = p.ve(lambda h: h.tensor_mul(xaf[:, :], xaf[:, :], gbrow[:, :]),
                      w=[t7, tpr])
            t9 = p.ve(lambda h: h.tensor_add(xaf[:, :], xaf[:, :], bbrow[:, :]),
                      w=[t8])
            return t9

        tmx = p.ve(lambda h: h.memset(red0[:, 0:1], 0.0))
        for tc in range(NTC):
            t9 = xaff(tc)
            ta = p.ve(lambda h: h.tensor_reduce(red0[:, 1:2], xaf[:, :], AX.X,
                                                ALU.max, apply_absolute_value=True),
                      w=[t9, tmx])
            gs("xa_c", ta)
            tmx = p.ve(lambda h: h.tensor_tensor(red0[:, 0:1], red0[:, 0:1],
                                                 red0[:, 1:2], ALU.max), w=[ta])
        tt = p.te(lambda h: h.transpose(pB[0:1, 0:128], red0[:, 0:1], cidf[:, :]),
                  w=[tmx, GD.get("pB01"), GD.get("pB0chain")])
        tt = p.ve(lambda h: h.tensor_copy(stat1[0:1, 16:144], pB[0:1, 0:128]),
                  w=[tt])
        tt = p.ve(lambda h: h.tensor_reduce(stat1[0:1, 0:1], stat1[0:1, 16:144],
                                            AX.X, ALU.max), w=[tt])
        gs("pB0chain", tt)
        tz8 = p.ve(lambda h: h.memset(stat1[0:1, 8:16], 0.0))
        tb0 = p.dma("gp", cc_in[0:1, :], stat1[0:1, 8:16], w=[tz8], sem="G7")
        tb1 = p.dma("gp", cc_in[0:1, 0:1], stat1[0:1, 0:1], w=[tt, tb0], sem="G7")
        if nocc:
            tcc = p.dma("gp", cc_out[0:1, :], cc_in[0:1, :], w=[tb1], sem="G7")
        else:
            tcc = p.emit("gp", lambda h: h.collective_compute(
                "AllReduce", ALU.max, replica_groups=[list(range(8))],
                ins=[cc_in.opt()], outs=[cc_out.opt()]),
                waits=[tb1], incs=(("C", 1),))
        tgm = p.dma("gp", stat1[0:1, 1:2], cc_out[0:1, 0:1], w=[tcc], sem="G7")
        tsc_ = p.ve(lambda h: h.tensor_scalar(stat1[0:1, 2:3], stat1[0:1, 1:2],
                                              1e-8, 1.0 / 127.0, ALU.max, ALU.mult),
                    w=[tgm])
        trs_ = p.ve(lambda h: h.reciprocal(stat1[0:1, 3:4], stat1[0:1, 2:3]),
                    w=[tsc_])

        tsm = p.ve(lambda h: h.memset(red0[:, 2:3], 0.0))
        for dc in range(NDC):
            tsm1 = p.ve(lambda h, dc=dc: h.tensor_reduce(
                red0[:, 3:4], w0[:, dc * 1024:(dc + 1) * 1024], AX.X, ALU.add,
                apply_absolute_value=True), w=[tw[dc], tsm])
            tsm = p.ve(lambda h: h.tensor_add(red0[:, 2:3], red0[:, 2:3],
                                              red0[:, 3:4]), w=[tsm1])
        t2 = p.te(lambda h: h.transpose(pB[0:1, 128:256], red0[:, 2:3], cidf[:, :]),
                  w=[tsm, GD.get("pB01"), GD.get("pB0chain")])
        t2 = p.ve(lambda h: h.tensor_copy(stat1[0:1, 16:144], pB[0:1, 128:256]),
                  w=[t2])
        t2 = p.ve(lambda h: h.tensor_reduce(stat1[0:1, 4:5], stat1[0:1, 16:144],
                                            AX.X, ALU.add), w=[t2])
        gs("pB0chain", t2)
        tdl = p.ve(lambda h: h.tensor_scalar_mul(stat1[0:1, 5:6], stat1[0:1, 4:5],
                                                 0.7 / (1024.0 * 1024.0)), w=[t2])
        tdb1 = p.te(lambda h: h.transpose(
            pB[:, 256:257], stat1[0:1, 5:6].broadcast_to((1, 128)), cidf[0:1, 0:1]),
            w=[tdl, GD.get("pB0chain")])
        tdb = p.ve(lambda h: h.tensor_copy(smalls[:, 6:7], pB[:, 256:257]),
                   w=[tdb1])
        gs("pB0chain", tdb)
        tnum = p.ve(lambda h: h.memset(red0[:, 4:6], 0.0))
        ttern = tnum
        for dc in range(NDC):
            wsl = w0[:, dc * 1024:(dc + 1) * 1024]
            ta = p.sc(lambda h, wsl=wsl: h.activation(scr0[:, :], wsl, AF.Abs,
                                                      bias=0.0, scale=1.0),
                      w=[tw[dc], ttern, GD.get("scr0_c")])
            tmk = p.ve(lambda h: h.tensor_scalar(xaf[:, :], scr0[:, :],
                                                 smalls[:, 6:7], None, ALU.is_gt),
                       w=[ta, tdb, GD.get("xa_c")])
            tn1 = p.ve(lambda h: h.tensor_mul(scr0[:, :], scr0[:, :], xaf[:, :]),
                       w=[tmk])
            tn2 = p.ve(lambda h: h.tensor_reduce(red0[:, 6:7], scr0[:, :], AX.X,
                                                 ALU.add), w=[tn1])
            tn3 = p.ve(lambda h: h.tensor_add(red0[:, 4:5], red0[:, 4:5],
                                              red0[:, 6:7]), w=[tn2])
            td1 = p.ve(lambda h: h.tensor_reduce(red0[:, 7:8], xaf[:, :], AX.X,
                                                 ALU.add), w=[tmk])
            td2 = p.ve(lambda h: h.tensor_add(red0[:, 5:6], red0[:, 5:6],
                                              red0[:, 7:8]), w=[td1, tn3])
            tsg = p.sc(lambda h, wsl=wsl: h.activation(scr0[:, :], wsl, AF.Sign,
                                                       bias=0.0, scale=1.0),
                       w=[tn1, tn2])
            gs("scr0_c", None)
            ttern = p.ve(lambda h, dc=dc: h.tensor_mul(
                tern[:, dc * 1024:(dc + 1) * 1024], scr0[:, :], xaf[:, :]),
                w=[tsg, td2])
            gs("xa_c", ttern)
        t3 = p.te(lambda h: h.transpose(pB[0:1, 384:512], red0[:, 4:5], cidf[:, :]),
                  w=[ttern, GD.get("pB01"), GD.get("pB0chain")])
        t3a = p.ve(lambda h: h.tensor_copy(stat1[0:1, 16:144], pB[0:1, 384:512]),
                   w=[t3])
        t3b = p.ve(lambda h: h.tensor_reduce(stat1[0:1, 6:7], stat1[0:1, 16:144],
                                             AX.X, ALU.add), w=[t3a])
        gs("pB0chain", t3b)
        t4 = p.te(lambda h: h.transpose(pB[0:1, 512:640], red0[:, 5:6], cidf[:, :]),
                  w=[ttern, t3, t3b])
        t4a = p.ve(lambda h: h.tensor_copy(stat1[0:1, 16:144], pB[0:1, 512:640]),
                   w=[t4, t3b])
        t4b = p.ve(lambda h: h.tensor_reduce(stat1[0:1, 7:8], stat1[0:1, 16:144],
                                             AX.X, ALU.add), w=[t4a])
        t5 = p.ve(lambda h: h.tensor_scalar_max(stat1[0:1, 7:8], stat1[0:1, 7:8],
                                                1.0), w=[t4b])
        t5 = p.ve(lambda h: h.reciprocal(stat1[0:1, 7:8], stat1[0:1, 7:8]), w=[t5])
        t5 = p.ve(lambda h: h.tensor_mul(stat1[0:1, 6:7], stat1[0:1, 6:7],
                                         stat1[0:1, 7:8]), w=[t5])
        tsa = p.ve(lambda h: h.tensor_mul(stat1[0:1, 6:7], stat1[0:1, 6:7],
                                          stat1[0:1, 2:3]), w=[t5])
        tb2 = p.te(lambda h: h.transpose(
            pB[:, 0:1], stat1[0:1, 3:4].broadcast_to((1, 128)), cidf[0:1, 0:1]),
            w=[trs_, t3, t4b])
        tb2v = p.ve(lambda h: h.tensor_copy(smalls[:, 4:5], pB[:, 0:1]),
                    w=[tb2])
        tb3 = p.te(lambda h: h.transpose(
            pB[:, 128:129], stat1[0:1, 6:7].broadcast_to((1, 128)), cidf[0:1, 0:1]),
            w=[tsa, tb2, tb2v])
        tb3v = p.ve(lambda h: h.tensor_copy(smalls[:, 5:6], pB[:, 128:129]),
                    w=[tb3])
        gs("pB01", tb3v)

        maybe_dbg("l0fstat", stat1[0:1, 0:16], [tsa, tb3v, tb2v, tdb])
        maybe_dbg("gb0", gbrow[:, :], [tpr])
        maybe_dbg("bb0", bbrow[:, :], [tpr])
        maybe_dbg("tern0", tern[:, :], [ttern])
        pxq = pA[:, 0:512].bitcast(BF16)
        pyq = pA[:, 1024:2048]
        for tc in range(NTC):
            t9 = xaff(tc)
            tq1 = p.ve(lambda h: h.tensor_scalar(xaf[:, :], xaf[:, :],
                                                 smalls[:, 4:5], -127.0,
                                                 ALU.mult, ALU.max), w=[t9, tb2v])
            tq2 = p.ve(lambda h: h.tensor_scalar(xaf[:, :], xaf[:, :], 127.0,
                                                 MAGIC, ALU.min, ALU.add), w=[tq1])
            tq3 = p.ve(lambda h: h.tensor_scalar(qbf[:, :], xaf[:, :], MAGIC, None,
                                                 ALU.subtract),
                       w=[tq2, GD.get("qbf_c")])
            gs("xa_c", tq3)
            tt = None
            for dc in range(NDC):
                tt = p.te(lambda h, dc=dc: h.transpose(
                    pxq[:, dc * 128:(dc + 1) * 128],
                    qbf[:, dc * 128:(dc + 1) * 128], cid[:, :]),
                    w=[tq3, GD.get("pA0r"), GD.get("lnpsum")])
            gs("qbf_c", tt)
            txe = p.ve(lambda h: h.tensor_copy(xqd[:, :], pxq[:, :]),
                       w=[tt] + gw("xqd_c"))
            if tc == 0:
                maybe_dbg("q0f", qbf[:, :], [tq3])
                maybe_dbg("xq0", xqd[:, :], [txe])
            gs("pA0r", txe)
            gs("lnpsum", txe)
            ty = None
            for dc in range(NDC):
                for s2 in range(2):
                    ty = p.te(lambda h, dc=dc, s2=s2: h.matmul(
                        pyq[:, s2 * 512:(s2 + 1) * 512],
                        xqd[:, dc * 128:(dc + 1) * 128],
                        tern[:, dc * 1024 + s2 * 512: dc * 1024 + (s2 + 1) * 512],
                        start=(dc == 0), stop=(dc == NDC - 1)),
                        w=[txe, ttern, GD.get("pA23")])
            tres = p.ve(lambda h, tc=tc: h.scalar_tensor_tensor(
                x_tok[:, tc * 1024:(tc + 1) * 1024], pyq[:, :], smalls[:, 5:6],
                x_tok[:, tc * 1024:(tc + 1) * 1024], ALU.mult, ALU.add),
                w=[ty, tb3v, x_ready[tc]])
            gs("pA23", tres)
            gs("xqd_c", ty)
            x_ready[tc] = tres

    l0f()
    p.set_floor(p.fence())
    A.reset()
    if tap == "l0f":
        return _finish(nc, p, sems, x_tok, y_out, x_ready, ctx)

    A.mark()
    vext = A.alloc([128, NTC * 16 * 65], BF16, "l1a_vext")
    vext_re = vext.rearrange("p (t f) -> p t f", t=NTC)
    cs = A.alloc([128, 2048], BF16, "l1a_cs")
    qpair = A.alloc([128, 2048], BF16, "l1a_qp")
    kpair = A.alloc([128, 2048], BF16, "l1a_kp")
    qext2 = [A.alloc([65, 2048], BF16, "l1a_qe0"),
             A.alloc([65, 2048], BF16, "l1a_qe1")]
    kext2 = [A.alloc([65, 2048], BF16, "l1a_ke0"),
             A.alloc([65, 2048], BF16, "l1a_ke1")]
    _qe0_off = A.offsets["l1a_qe0"][0]
    vscr = arena_t[:, _qe0_off // 4: _qe0_off // 4 + 1024].bitcast(BF16)
    odma = A.alloc([128, 8 * 2048], BF16, "l1a_od")
    rho = A.alloc([65, 2048], BF16, "l1a_rho")
    _rho_off = A.offsets["l1a_rho"][0]
    sinb2 = arena_t[:, _rho_off // 4: _rho_off // 4 + 1024].bitcast(BF16)[64:128, :]
    mcols = A.alloc([128, 16], BF16, "l1a_mc")
    mrow = A.alloc([16, 128], BF16, "l1a_mr")
    wstr = A.alloc([128, 2 * NDC * 128], BF16, "l1a_ws")
    lnscr1 = A.alloc([128, 2 * 1024], BF16, "l1a_lns")
    pte = lnscr1[:, 0:1024]
    stage = lnscr1[:, 1024:2048]
    A.mark()
    xnd1 = A.alloc([128, NDC * 2048], BF16, "l1a_xnd")
    xnd1_re = xnd1.rearrange("p (dc t) -> p dc t", dc=NDC)

    def l1a():
        p.dma("gp", cs[0:64, :], dr["ccos"], sem="Gcs")
        p.dma("gp", cs[64:128, :], dr["ccos"], sem="Gcs")
        p.dma("gp", sinb2[:, :], dr["csin"], sem="Gcs")
        tk_cos = p.all("Gcs")
        tk_sin = tk_cos
        pbf = pA[:, 0:512].bitcast(BF16)
        ln_toks = ln_to_xnd(list(range(NTC)), xnd1, lnscr1, pbf)
        xnd_all = ln_toks[-1]
        if L1A_LIMIT.get("ones", True):
            tvo = p.ve(lambda h: h.memset(
                vext_re[:, :, :].rearrange("p t (hh f) -> p t hh f", hh=16)[:, :, :, 64:65],
                1.0), w=gw("vext_c"))
        else:
            tvo = None
        pvp = pB[:, 0:2048]
        for cc in range(NDC if L1A_LIMIT.get("vproj", True) else 0):
            sl = cc % 2
            wdst = wstr[:, sl * NDC * 128:(sl + 1) * NDC * 128]
            gsem = "G0" if sl == 0 else "G1"
            twl = p.dma("gp", wdst, dr["wv"][cc].rearrange("p dc c -> p (dc c)"),
                        w=gw(("dmaq", gsem)), sem=gsem)
            tm = None
            for dc in range(NDC):
                for s4 in range(4):
                    tm = p.te(lambda h, dc=dc, s4=s4, wdst=wdst: h.matmul(
                        pvp[:, s4 * 512:(s4 + 1) * 512],
                        wdst[:, dc * 128:(dc + 1) * 128],
                        xnd1_re[:, dc, s4 * 512:(s4 + 1) * 512],
                        start=(dc == 0), stop=(dc == NDC - 1)),
                        w=[twl, xnd_all, GD.get("pBall")])
            gs(("dmaq", gsem), tm)
            tev = p.sc(lambda h: h.activation(vscr[:, :], pvp[:, :], AF.Copy,
                                              bias=0.0, scale=1.0),
                       w=[tm, GD.get("rscr_c")])
            gs("pBall", tev)
            tt0 = None
            if L1A_LIMIT.get("vtrans", True):
                for bat in range(2):
                    pbk = pA[:, bat * 512:(bat + 1) * 512].bitcast(BF16)
                    tt0 = None
                    for ii in range(8):
                        i = bat * 8 + ii
                        tt0 = p.te(lambda h, i=i, ii=ii, pbk=pbk: h.transpose(
                            pbk[:, ii * 128:(ii + 1) * 128],
                            vscr[:, i * 128:(i + 1) * 128], cid[:, :]),
                            w=[tev, GD.get(("vtp", bat)), GD.get("lnpsum")])
                    out_ap = vext_re[:, bat * 8:(bat + 1) * 8, :].rearrange(
                        "p t (hh f) -> p t hh f", hh=16)[:, :, 2 * cc:2 * cc + 2, 0:64]
                    src_ap = pbk[:, :].rearrange("p (t a b) -> p t a b", t=8, a=2)
                    tt1 = p.ve(lambda h, out_ap=out_ap, src_ap=src_ap: h.tensor_copy(
                        out_ap, src_ap), w=[tt0, tvo, GD.get("vext_c")])
                    gs(("vtp", bat), tt1)
                    gs("vext_c", tt1)
                    gs("lnpsum", tt1)
            gs("rscr_c", tt0)

        tko = p.ve(lambda h: h.memset(kext2[0][64:65, :], 1.0), w=gw("kext_c"))
        tko = p.ve(lambda h: h.memset(kext2[1][64:65, :], 1.0), w=[tko])

        pqp = pB[0:64, 0:2048]
        pbld = pB[:, 0:2048]
        psmp = pB[:, 0:2048]
        psH = pA[:, 0:1024]
        pOH = pA[0:65, 1024:2048]

        def build_pair(j, dep_stream):
            toks = {}
            for wi, (which, wnm, dst) in enumerate(
                    (("q", "wq", qpair), ("k", "wk", kpair))):
                sl = wi % 2
                wdst = wstr[:, sl * NDC * 128:(sl + 1) * NDC * 128]
                gsem = "G0" if sl == 0 else "G1"
                twl = p.dma("gp", wdst, dr[wnm][j].rearrange("p dc c -> p (dc c)"),
                            w=gw(("dmaq", gsem)), sem=gsem)
                tm = None
                for dc in range(NDC):
                    for s4 in range(4):
                        tm = p.te(lambda h, dc=dc, s4=s4, wdst=wdst: h.matmul(
                            pbld[:, s4 * 512:(s4 + 1) * 512],
                            wdst[:, dc * 128:(dc + 1) * 128],
                            xnd1_re[:, dc, s4 * 512:(s4 + 1) * 512],
                            start=(dc == 0), stop=(dc == NDC - 1)),
                            w=[twl, xnd_all, GD.get("pBall")])
                gs(("dmaq", gsem), tm)
                tev = p.sc(lambda h, dst=dst, which=which: h.activation(
                    dst[:, :], pbld[:, :], AF.Copy, bias=0.0,
                    scale=0.125 if which == "q" else 1.0),
                    w=[tm, GD.get((which + "pair_c"))] + [dep_stream])
                gs("pBall", tev)
                tr = None
                for s4 in range(4):
                    tr = p.te(lambda h, s4=s4, dst=dst: h.matmul(
                        pbld[:, s4 * 512:(s4 + 1) * 512], cP[:, :],
                        dst[:, s4 * 512:(s4 + 1) * 512], start=True, stop=True),
                        w=[tev, tcons])
                t2a = p.ve(lambda h: h.tensor_tensor(
                    pbld[0:64, :], pbld[0:64, :], sinb2[:, :], ALU.mult),
                    w=[tr, tk_sin])
                t2b = p.ve(lambda h: h.tensor_tensor(
                    pbld[64:128, :], pbld[64:128, :], sinb2[:, :], ALU.mult),
                    w=[t2a])
                t3a = p.ve(lambda h, dst=dst: h.tensor_tensor(
                    dst[0:64, :], dst[0:64, :], cs[0:64, :], ALU.mult),
                    w=[tr, tk_cos, tev])
                t3b = p.ve(lambda h, dst=dst: h.tensor_tensor(
                    dst[64:128, :], dst[64:128, :], cs[64:128, :], ALU.mult),
                    w=[t3a])
                t4 = p.ve(lambda h, dst=dst: h.tensor_tensor(
                    dst[:, :], dst[:, :], pbld[:, :], ALU.add),
                    w=[t2b, t3b])
                gs("pBall", t4)
                toks[which] = t4
            return toks

        def extract_head(hh, btoks):
            pb0 = (hh % 2) * 64
            par = hh % 2
            qext = qext2[par]
            kext = kext2[par]
            dep_rd = GD.get(("qk_rd", par))
            tqe = p.dma("sy", qext[0:64, :], qpair[pb0:pb0 + 64, :],
                        w=[btoks["q"], dep_rd, GD.get("rscr_c")], sem="Lq")
            tke_ = p.dma("sy", kext[0:64, :], kpair[pb0:pb0 + 64, :],
                         w=[btoks["k"], dep_rd, tko], sem="Lq")
            gs("qpair_c", tke_)
            gs("kpair_c", tke_)
            tsm = None
            for qc in range(NTC):
                bank = qc % 2
                ts1 = p.te(lambda h, qc=qc, bank=bank, qext=qext, kext=kext:
                    h.matmul(
                    pbld[:, bank * 512:bank * 512 + 256],
                    qext[0:64, qc * 128:(qc + 1) * 128],
                    kext[0:64, :][:, 7::8], start=True, stop=True),
                    w=[tqe, tke_, GD.get(("smp", bank)), GD.get("pBall")])
                tsm = p.ve(lambda h, qc=qc, bank=bank: h.reduce_max(
                    mcols[:, qc:qc + 1], pbld[:, bank * 512:bank * 512 + 256],
                    AX.X, negate=True), w=[ts1, GD.get("mcols_c")])
                gs(("smp", bank), tsm)
            pmr = pB[0:16, 1024:1088].bitcast(BF16)
            tmt = p.te(lambda h: h.transpose(pmr, mcols[:, :], cid[:, :]),
                       w=[tsm, GD.get("pBall")])
            tmv = p.ve(lambda h: h.tensor_copy(mrow[:, :], pmr),
                       w=[tmt, GD.get("mrow_c")])
            gs("pBall", tmv)
            tmr = p.dma("sy", qext[64:65, :].rearrange("o (qc t) -> o qc t", qc=16),
                        mrow[:, :], w=[tmv], slow=True, sem="Lq")
            gs("mcols_c", tmr)
            gs("mrow_c", tmr)
            return tqe, tke_, tmr

        LIM = dict(L1A_LIMIT)
        nheads = LIM.get("heads", 16)
        do_div = LIM.get("div", True)
        do_wo = LIM.get("wo", True)
        if nheads == 0:
            return
        btoks = build_pair(0, None)
        ex_cur = extract_head(0, btoks)
        ostage_dep = None
        for hh in range(nheads):
            pb0 = (hh % 2) * 64
            cc = hh // 2
            if hh % 2 == 1 and hh < nheads - 1:
                btoks = build_pair(hh // 2 + 1, GD.get("stream_done"))
            ex_nxt = extract_head(hh + 1, btoks) if hh < nheads - 1 else None
            exq, exk, exm = ex_cur
            qext = qext2[hh % 2]
            kext = kext2[hh % 2]
            t_qk_last = None
            for th in range(2):
                tav = None
                for tsc in range(NTC):
                    tmma = p.te(lambda h, tsc=tsc, th=th, qext=qext, kext=kext:
                        h.matmul(
                        psH[:, 0:512], kext[:, tsc * 128:(tsc + 1) * 128],
                        qext[:, th * 1024:th * 1024 + 512],
                        start=True, stop=True),
                        w=[exq, exk, exm, GD.get(("psH", 0))])
                    tmmb = p.te(lambda h, tsc=tsc, th=th, qext=qext, kext=kext:
                        h.matmul(
                        psH[:, 512:1024], kext[:, tsc * 128:(tsc + 1) * 128],
                        qext[:, th * 1024 + 512:th * 1024 + 1024],
                        start=True, stop=True), w=[GD.get(("psH", 1))])
                    t_qk_last = tmmb
                    texpa = p.sc(lambda h: h.activation(
                        pte[:, 0:512], psH[:, 0:512], AF.Exp,
                        bias=0.0, scale=1.0),
                        w=[tmma, GD.get(("pte", 0))])
                    gs(("psH", 0), texpa)
                    texpb = p.sc(lambda h: h.activation(
                        pte[:, 512:1024], psH[:, 512:1024], AF.Exp,
                        bias=0.0, scale=1.0),
                        w=[tmmb, GD.get(("pte", 1))])
                    gs(("psH", 1), texpb)
                    for s2 in range(2):
                        tav = p.te(lambda h, tsc=tsc, s2=s2, cc2=hh: h.matmul(
                            pOH[:, s2 * 512:(s2 + 1) * 512],
                            vext_re[:, tsc, :].rearrange("p (hh f) -> p hh f", hh=16)[
                                :, cc2:cc2 + 1, :].rearrange("p a b -> p (a b)"),
                            pte[:, s2 * 512:(s2 + 1) * 512],
                            start=(tsc == 0), stop=(tsc == NTC - 1)),
                            w=[texpa if s2 == 0 else texpb,
                               GD.get("vext_c") if tsc == 0 else None,
                               GD.get("psO_c") if tsc == 0 else None])
                        gs(("pte", s2), tav)
                trr = p.ve(lambda h: h.reciprocal(pOH[64:65, :], pOH[64:65, :]),
                           w=[tav])
                trs = p.ve(lambda h, hh=hh, th=th: h.tensor_copy(
                    rho[32:33, th * 1024:(th + 1) * 1024], pOH[64:65, :]),
                    w=[trr, GD.get("rho_st")])
                trd = p.dma("sy", rho[hh:hh + 1, th * 1024:(th + 1) * 1024],
                            rho[32:33, th * 1024:(th + 1) * 1024], w=[trs],
                            sem="Lr")
                gs("rho_st", trd)
                gs(("rho", hh), trd)
                if pb0 == 0:
                    toe = p.ve(lambda h, cc=cc, th=th: h.tensor_copy(
                        odma[0:64, cc * 2048 + th * 1024: cc * 2048 + (th + 1) * 1024],
                        pOH[0:64, :]), w=[tav, trr, GD.get(("odiv", cc))])
                    gs("psO_c", toe)
                else:
                    tst = p.ve(lambda h: h.tensor_copy(stage[0:64, :], pOH[0:64, :]),
                               w=[tav, trr, ostage_dep])
                    toe = p.dma("sy", odma[64:128, cc * 2048 + th * 1024:
                                           cc * 2048 + (th + 1) * 1024],
                                stage[0:64, :], w=[tst], sem="Lo")
                    ostage_dep = toe
                    gs("psO_c", tst)
                gs(("oev", hh, th), toe)
                gs("stream_done", toe)
            gs(("qk_rd", hh % 2), t_qk_last)
            ex_cur = ex_nxt
        if not do_div:
            return
        pexp = pB[:, 0:2048]
        for j in range(nheads // 2):
            deps = [GD.get(("rho", 2 * j)), GD.get(("rho", 2 * j + 1)),
                    GD.get(("oev", 2 * j, 0)), GD.get(("oev", 2 * j, 1)),
                    GD.get(("oev", 2 * j + 1, 0)), GD.get(("oev", 2 * j + 1, 1))]
            tex = None
            for s4 in range(4):
                tex = p.te(lambda h, j=j, s4=s4: h.matmul(
                    pexp[:, s4 * 512:(s4 + 1) * 512],
                    cbd16[:, j * 128:(j + 1) * 128],
                    rho[0:16, s4 * 512:(s4 + 1) * 512],
                    start=True, stop=True),
                    w=deps + [tcons, GD.get("pBall"), GD.get("psO_c"),
                              p.all("Lr"), p.all("Lo")])
            tdv = p.ve(lambda h, j=j: h.tensor_tensor(
                odma[:, j * 2048:(j + 1) * 2048], odma[:, j * 2048:(j + 1) * 2048],
                pexp[:, :], ALU.mult), w=[tex] + deps)
            gs("pBall", tdv)
            gs(("odiv", j), tdv)
        if not do_wo:
            return
        p.set_floor(p.fence())
        A.reset()
        A.mark()
        wo_sb = A.alloc([128, NDC * 1024], BF16, "l1a_wo")
        for dc in range(NDC):
            p.dma("gp", wo_sb[:, dc * 1024:(dc + 1) * 1024], dr["wo"][dc],
                  sem="Go2")
        two = [p.all("Go2")] * NDC
        pyw = pA[:, 0:1024]
        for tc in range(NTC):
            ty = None
            for dc in range(NDC):
                for s2 in range(2):
                    ty = p.te(lambda h, dc=dc, s2=s2, tc=tc: h.matmul(
                        pyw[:, s2 * 512:(s2 + 1) * 512],
                        odma[:, dc * 2048 + tc * 128: dc * 2048 + (tc + 1) * 128],
                        wo_sb[:, dc * 1024 + s2 * 512: dc * 1024 + (s2 + 1) * 512],
                        start=(dc == 0), stop=(dc == NDC - 1)),
                        w=[GD.get(("odiv", dc)), two[dc], GD.get("pyw_c"),
                           GD.get(("psH", 0)), GD.get(("psH", 1))])
            tres = p.ve(lambda h, tc=tc: h.scalar_tensor_tensor(
                x_tok[:, tc * 1024:(tc + 1) * 1024], pyw[:, :], 1.0,
                x_tok[:, tc * 1024:(tc + 1) * 1024], ALU.mult, ALU.add),
                w=[ty, x_ready[tc]])
            gs("pyw_c", tres)
            x_ready[tc] = tres

    l1a()
    p.set_floor(p.fence())
    A.reset()
    A.reset()
    if tap == "l1a":
        return _finish(nc, p, sems, x_tok, y_out, x_ready, ctx)

    A.mark()
    xnd2 = A.alloc([128, NDC * 2048], BF16, "l1f_xnd")
    xnd2_re = xnd2.rearrange("p (dc t) -> p dc t", dc=NDC)
    hbuf = A.alloc([128, NIC * 1024], BF16, "l1f_h")
    wdall = A.alloc([128, NIC * 1024], BF16, "l1f_wd")
    wgs = A.alloc([128, 2 * NDC * 128], BF16, "l1f_wg")
    wus = A.alloc([128, 2 * NDC * 128], BF16, "l1f_wu")
    gate = A.alloc([128, 1024], BF16, "l1f_gate")
    cwc = A.alloc([128, NTC * 16], F32, "l1f_cw")
    wr_sb = A.alloc([128, NDC * 16], BF16, "l1f_wr")
    lnscr2 = A.alloc([128, 2 * 1024], BF16, "l1f_lns")
    lgr = lnscr2[0:16, 0:2048]
    prb = A.alloc([128, 32], F32, "l1f_p")
    prm = A.alloc([128, 8], F32, "l1f_pm")

    def l1f():
        pbf = pA[:, 0:512].bitcast(BF16)
        ln_toks = ln_to_xnd(list(range(NTC)), xnd2, lnscr2, pbf)
        xnd_all = ln_toks[-1]
        twr = p.dma("gp",
                    wr_sb[:, :].rearrange("p (dc e) -> p dc e", dc=NDC)[:, :, 0:E_MOE],
                    dr["wr"].transpose([1, 0, 2]), sem="Gr")
        plg = pB[0:E_MOE, 0:2048]
        tmm = None
        for dc in range(NDC):
            for s4 in range(4):
                tmm = p.te(lambda h, dc=dc, s4=s4: h.matmul(
                    plg[:, s4 * 512:(s4 + 1) * 512],
                    wr_sb[:, :].rearrange("p (dc e) -> p dc e", dc=NDC)[:, dc, 0:E_MOE],
                    xnd2_re[:, dc, s4 * 512:(s4 + 1) * 512],
                    start=(dc == 0), stop=(dc == NDC - 1)),
                    w=[twr, xnd_all, GD.get("pBall"), GD.get("pB01")])
        tlg = p.ve(lambda h: h.tensor_copy(lgr[0:E_MOE, :], plg[:, :]), w=[tmm])
        gs("pBall", tlg)
        gs("pB01", tlg)
        for tc in range(NTC):
            plgt = pB[:, 0:8].bitcast(BF16)[:, 0:E_MOE]
            t1 = p.te(lambda h, tc=tc: h.transpose(
                plgt, lgr[0:E_MOE, tc * 128:(tc + 1) * 128],
                cid[0:E_MOE, 0:E_MOE]),
                w=[tlg, GD.get("pB01"), tcons])
            t2 = p.ve(lambda h: h.reduce_max(prb[:, 9:10], plgt,
                                             AX.X, negate=True),
                      w=[t1, GD.get("prb_c")])
            t3 = p.sc(lambda h: h.activation(prb[:, 0:E_MOE], plgt,
                                             AF.Exp, bias=prb[:, 9:10], scale=1.0,
                                             accum_out=prb[:, 10:11]), w=[t2])
            gs("pB01", t3)
            t4 = p.ve(lambda h: h.reciprocal(prb[:, 10:11], prb[:, 10:11]), w=[t3])
            t5 = p.ve(lambda h: h.tensor_scalar_mul(prb[:, 0:E_MOE],
                                                    prb[:, 0:E_MOE], prb[:, 10:11]),
                      w=[t4])
            t6 = p.ve(lambda h: h.reduce_max(prm[:, 0:1], prb[:, 0:E_MOE], AX.X),
                      w=[t5, GD.get("prm_c")])
            tmsk = p.ve(lambda h: h.tensor_scalar(prb[:, 16:16 + E_MOE],
                                                  prb[:, 0:E_MOE], prm[:, 0:1],
                                                  NEG_BIG, ALU.is_ge, ALU.mult),
                        w=[t6])
            tpm = p.ve(lambda h: h.tensor_add(prb[:, 16:16 + E_MOE],
                                              prb[:, 16:16 + E_MOE],
                                              prb[:, 0:E_MOE]), w=[tmsk])
            t8 = p.ve(lambda h: h.reduce_max(prm[:, 1:2], prb[:, 16:16 + E_MOE],
                                             AX.X), w=[tpm])
            tind = p.ve(lambda h: h.tensor_scalar(prb[:, 16:16 + E_MOE],
                                                  prb[:, 0:E_MOE], prm[:, 1:2],
                                                  None, ALU.is_ge), w=[t8])
            tcwu = p.ve(lambda h: h.tensor_mul(prb[:, 16:16 + E_MOE],
                                               prb[:, 16:16 + E_MOE],
                                               prb[:, 0:E_MOE]), w=[tind])
            tsum = p.ve(lambda h: h.tensor_reduce(prm[:, 2:3],
                                                  prb[:, 16:16 + E_MOE], AX.X,
                                                  ALU.add), w=[tcwu])
            trc = p.ve(lambda h: h.reciprocal(prm[:, 2:3], prm[:, 2:3]), w=[tsum])
            tcw = p.ve(lambda h, tc=tc: h.tensor_scalar_mul(
                cwc[:, tc * 16:tc * 16 + E_MOE], prb[:, 16:16 + E_MOE],
                prm[:, 2:3]), w=[trc, GD.get(("cw", tc))])
            gs("prb_c", tcw)
            gs("prm_c", tcw)
            gs(("cw", tc), tcw)
        gs("pyd_c", GD.get(("cw", NTC - 1)))
        gs("pB01x", GD.get(("cw", NTC - 1)))
        maybe_dbg("cw0", cwc[:, :], [GD.get(("cw", tc2)) for tc2 in range(NTC)])
        maybe_dbg("lg0", lgr[0:16, :], [tlg])
        pguA = [pA[:, 0:1024], pA[:, 1024:2048]]
        pguB = [pB[:, 0:1024], pB[:, 1024:2048]]
        pyd = pA[:, 0:1024]
        for e in range(L1F_LIMIT.get("experts", E_MOE)):
            pre = {}
            for ic in range(2):
                sl = ic % 2
                gsg = "G2" if sl == 0 else "G3"
                gsu = "G4" if sl == 0 else "G5"
                pre[(ic, "g")] = p.dma(
                    "gp", wgs[:, sl * NDC * 128:(sl + 1) * NDC * 128],
                    dr["wg"][e][ic].rearrange("p dc c -> p (dc c)"),
                    w=gw(("dmaq", gsg)), sem=gsg)
                pre[(ic, "u")] = p.dma(
                    "gp", wus[:, sl * NDC * 128:(sl + 1) * NDC * 128],
                    dr["wu"][e][ic].rearrange("p dc c -> p (dc c)"),
                    w=gw(("dmaq", gsu)), sem=gsu)
            for ic in range(NIC):
                p.dma("gp", wdall[:, ic * 1024:(ic + 1) * 1024],
                      dr["wd"][e][ic], w=gw(("dmaq", "G6")), sem="G6")
            twd = [p.all("G6")] * NIC
            for th in range(2):
                for ic in range(NIC):
                    sl = ic % 2
                    pgu = pguA[sl]
                    pgu2 = pguB[sl]
                    wgd = wgs[:, sl * NDC * 128:(sl + 1) * NDC * 128]
                    wud = wus[:, sl * NDC * 128:(sl + 1) * NDC * 128]
                    gsg = "G2" if sl == 0 else "G3"
                    gsu = "G4" if sl == 0 else "G5"
                    if th == 0 and (ic, "g") in pre:
                        twg = pre[(ic, "g")]
                        twu = pre[(ic, "u")]
                    else:
                        twg = p.dma("gp", wgd,
                                    dr["wg"][e][ic].rearrange("p dc c -> p (dc c)"),
                                    w=gw(("dmaq", gsg)), sem=gsg)
                        twu = p.dma("gp", wud,
                                    dr["wu"][e][ic].rearrange("p dc c -> p (dc c)"),
                                    w=gw(("dmaq", gsu)), sem=gsu)
                    tg_ = None
                    for dc in range(NDC):
                        for s2 in range(2):
                            tg_ = p.te(lambda h, dc=dc, s2=s2, th=th, wgd=wgd,
                                       pgu=pgu:
                                       h.matmul(pgu[:, s2 * 512:(s2 + 1) * 512],
                                                wgd[:, dc * 128:(dc + 1) * 128],
                                                xnd2_re[:, dc, th * 1024 + s2 * 512:
                                                        th * 1024 + (s2 + 1) * 512],
                                                start=(dc == 0),
                                                stop=(dc == NDC - 1)),
                                       w=[twg, xnd_all, GD.get(("pgu_c", sl)),
                                          GD.get("pyd_r"),
                                          GD.get("lnpsum")])
                    gs(("dmaq", gsg), tg_)
                    gs("lnpsum", tg_)
                    tsg = p.sc(lambda h, pgu=pgu: h.activation(
                        gate[:, :], pgu[:, :], AF.Silu, bias=0.0, scale=1.0),
                        w=[tg_, GD.get("gate_c"), GD.get("gate_dbg")])
                    gs(("pgu_c", sl), tsg)
                    tu_ = None
                    for dc in range(NDC):
                        for s2 in range(2):
                            tu_ = p.te(lambda h, dc=dc, s2=s2, th=th, wud=wud,
                                       pgu2=pgu2:
                                       h.matmul(pgu2[:, s2 * 512:(s2 + 1) * 512],
                                                wud[:, dc * 128:(dc + 1) * 128],
                                                xnd2_re[:, dc, th * 1024 + s2 * 512:
                                                        th * 1024 + (s2 + 1) * 512],
                                                start=(dc == 0),
                                                stop=(dc == NDC - 1)),
                                       w=[twu, xnd_all, GD.get(("pgu2_c", sl)),
                                          GD.get("pB01x")])
                    gs(("dmaq", gsu), tu_)
                    th_ = p.ve(lambda h, ic=ic, pgu2=pgu2: h.tensor_tensor(
                        hbuf[:, ic * 1024:(ic + 1) * 1024], pgu2[:, :], gate[:, :],
                        ALU.mult), w=[tu_, tsg, GD.get(("hb", ic)),
                                      GD.get("hb_dbg")])
                    gs(("pgu2_c", sl), th_)
                    gs("gate_c", th_)
                    gs(("hcur", ic), th_)
                    if e == 0 and th == 0 and ic == 0:
                        tdbg1 = maybe_dbg("g00", gate[:, :], [tsg])
                        tdbg2 = maybe_dbg("h00", hbuf[:, 0:1024], [th_])
                        if tdbg1 is not None:
                            gs("gate_dbg", tdbg1)
                        if tdbg2 is not None:
                            gs("hb_dbg", tdbg2)
                for t2c in range(8):
                    tc = th * 8 + t2c
                    ty = None
                    for ic in range(NIC):
                        for s2 in range(2):
                            ty = p.te(lambda h, ic=ic, s2=s2, t2c=t2c: h.matmul(
                                pyd[:, s2 * 512:(s2 + 1) * 512],
                                hbuf[:, ic * 1024 + t2c * 128:
                                     ic * 1024 + (t2c + 1) * 128],
                                wdall[:, ic * 1024 + s2 * 512:
                                      ic * 1024 + (s2 + 1) * 512],
                                start=(ic == 0), stop=(ic == NIC - 1)),
                                w=[GD.get(("hcur", ic)), twd[ic],
                                   GD.get("pyd_c"), GD.get(("pgu_c", 0)),
                                   GD.get(("pgu_c", 1))])
                    tres = p.ve(lambda h, tc=tc, e=e: h.scalar_tensor_tensor(
                        x_tok[:, tc * 1024:(tc + 1) * 1024], pyd[:, :],
                        cwc[:, tc * 16 + e:tc * 16 + e + 1],
                        x_tok[:, tc * 1024:(tc + 1) * 1024], ALU.mult, ALU.add),
                        w=[ty, GD.get(("cw", tc)), x_ready[tc]])
                    gs("pyd_c", tres)
                    gs("pyd_r", tres)
                    x_ready[tc] = tres
                for ic in range(NIC):
                    gs(("hb", ic), GD.get("pyd_c"))
            gs(("dmaq", "G6"), GD.get("pyd_c"))

    l1f()
    A.reset()

    return _finish(nc, p, sems, x_tok, y_out, x_ready, ctx)


def _finish(nc, p, sems, x_tok, y_out, x_ready, ctx):
    last = None
    for tc in range(NTC):
        last = p.emit("gp", lambda h, tc=tc: h.dma_start(
            out=y_out[tc * 128:(tc + 1) * 128, :],
            in_=x_tok[:, tc * 1024:(tc + 1) * 1024]),
            waits=[x_ready[tc]], incs=(("S", 16),))
    p.emit("gp", None, waits=[last])
    with nc.Block() as block:
        @block.tensor
        def _(h):
            _replay(p, "te", h, sems)

        @block.vector
        def _(h):
            _replay(p, "ve", h, sems)

        @block.scalar
        def _(h):
            _replay(p, "sc", h, sems)

        @block.gpsimd
        def _(h):
            _replay(p, "gp", h, sems)

        @block.sync
        def _(h):
            _replay(p, "sy", h, sems)
    for cm in reversed(ctx):
        cm.__exit__(None, None, None)
    return nc


def _replay(p, eng, h, sems):
    for fn, waits, incs in p.q[eng]:
        for s, v in waits:
            h.wait_ge(sems[s], v)
        if fn is None:
            continue
        inst = fn(h)
        for s, n in incs:
            inst.then_inc(sems[s], n)



def _host_weights(inputs):
    f = np.float32
    W = {}
    wqkvT = np.ascontiguousarray(np.asarray(inputs["L0_attn_W_qkv"]).T.astype(f))
    W["wqkv"] = np.ascontiguousarray(
        wqkvT.reshape(8, 128, 24, 128).transpose(2, 1, 0, 3))
    W["wconv"] = np.ascontiguousarray(
        np.asarray(inputs["L0_attn_conv_w"]).astype(f).reshape(24, 128, KCONV))
    wzT = np.ascontiguousarray(np.asarray(inputs["L0_attn_W_z"]).T.astype(f))
    W["wz"] = np.ascontiguousarray(
        wzT.reshape(8, 128, 8, 128).transpose(2, 1, 0, 3))
    wba = np.zeros((48, 1024), f)
    wba[0:16] = np.asarray(inputs["L0_attn_W_b"]).astype(f)
    wba[32:48] = np.asarray(inputs["L0_attn_W_a"]).astype(f)
    W["wba"] = np.ascontiguousarray(wba.T.reshape(8, 128, 48))
    W["wout"] = np.ascontiguousarray(
        np.asarray(inputs["L0_attn_W_out"]).T.astype(f).reshape(8, 128, 1024))
    W["dtb"] = np.ascontiguousarray(
        np.asarray(inputs["L0_attn_dt_bias"]).astype(f).reshape(16, 1))
    W["alog"] = np.ascontiguousarray(
        np.asarray(inputs["L0_attn_A_log"]).astype(f).reshape(16, 1))
    W["w0f"] = np.ascontiguousarray(
        np.asarray(inputs["L0_ffn_weight"]).T.astype(f).reshape(8, 128, 1024))
    W["gam0"] = np.ascontiguousarray(
        np.asarray(inputs["L0_ffn_gamma"]).astype(f).reshape(8, 128).T)
    W["bet0"] = np.ascontiguousarray(
        np.asarray(inputs["L0_ffn_beta"]).astype(f).reshape(8, 128).T)
    for nm, key in (("wq", "L1_attn_Wq"), ("wk", "L1_attn_Wk"), ("wv", "L1_attn_Wv")):
        wT = np.ascontiguousarray(np.asarray(inputs[key]).T.astype(f))
        W[nm] = np.ascontiguousarray(
            wT.reshape(8, 128, 8, 128).transpose(2, 1, 0, 3))
    W["wo"] = np.ascontiguousarray(
        np.asarray(inputs["L1_attn_Wo"]).T.astype(f).reshape(8, 128, 1024))
    W["wr"] = np.ascontiguousarray(
        np.asarray(inputs["L1_ffn_Wr"]).astype(f).reshape(8, 128, E_MOE))
    W["wg"] = np.ascontiguousarray(
        np.asarray(inputs["L1_ffn_Wg"]).astype(f).reshape(9, 8, 128, 22, 128).transpose(0, 3, 2, 1, 4))
    W["wu"] = np.ascontiguousarray(
        np.asarray(inputs["L1_ffn_Wu"]).astype(f).reshape(9, 8, 128, 22, 128).transpose(0, 3, 2, 1, 4))
    W["wd"] = np.ascontiguousarray(
        np.asarray(inputs["L1_ffn_Wd"]).astype(f).reshape(9, 22, 128, 1024))
    W["cid"] = np.eye(128, dtype=NP_BF16)
    W["cidf"] = np.eye(128, dtype=f)
    P64 = np.zeros((64, 64), f)
    for j in range(32):
        P64[j, 2 * j + 1] = -1.0
        P64[32 + j, 2 * j] = 1.0
    cPm = np.zeros((128, 128), f)
    cPm[0:64, 0:64] = P64.T
    cPm[64:128, 64:128] = P64.T
    W["cP"] = cPm.astype(NP_BF16)
    r = np.arange(128)
    W["ctri"] = np.where(r[None, :] >= r[:, None], 0.0, NEG_BIG).astype(f)
    W["cim"] = np.eye(128, dtype=f)
    bd16 = np.zeros((16, 8 * 128), f)
    for j in range(8):
        for pp in range(128):
            bd16[2 * j + pp // 64, j * 128 + pp] = 1.0
    W["cbd16"] = bd16.astype(NP_BF16)
    W["conesf"] = np.ones((1, 128), f)
    inv = 1.0 / 10000.0 ** (np.arange(0, HD, 2, dtype=f) / HD)
    t = np.arange(T, dtype=f)
    jj = np.arange(128) % 64
    fr = inv[jj % 32]
    ang = t[None, :] * fr[:, None]
    W["ccos"] = np.cos(ang[:64]).astype(NP_BF16)
    W["csin"] = np.sin(ang[:64]).astype(NP_BF16)
    return W


def kernel(**inputs):
    X = np.ascontiguousarray(np.asarray(inputs["L0_attn_x"], dtype=np.float32))
    W = _host_weights(inputs)
    nc = build_program(None)
    in_maps = []
    for b in range(8):
        m = dict(W)
        m["x"] = np.ascontiguousarray(X[b])
        in_maps.append(m)
    res = run_bass_kernel_spmd(nc, in_maps, core_ids=list(range(8)))
    return np.stack([res.results[b]["y"] for b in range(8)], axis=0)


# version 14
# speedup vs baseline: 1.0087x; 1.0087x over previous
import numpy as np
import ml_dtypes

import concourse.bass as bass
import concourse.mybir as mybir
from concourse.bass_utils import run_bass_kernel_spmd

F32 = mybir.dt.float32
BF16 = mybir.dt.bfloat16
AF = mybir.ActivationFunctionType
ALU = mybir.AluOpType
AX = mybir.AxisListType
NP_BF16 = ml_dtypes.bfloat16

L1A_LIMIT = {}
L1F_LIMIT = {}

T = 2048
D = 1024
H = 16
HD = 64
KCONV = 4
NTC = 16
NDC = 8
E_MOE = 9
NIC = 22
MAGIC = float(np.float32(12582912.0))
NEG_BIG = -1e30
LN8 = float(np.log(0.125))

ENGS = ("te", "ve", "sc", "gp", "sy")
SEM_OF = {"te": "T", "ve": "V", "sc": "A", "gp": "G", "sy": "L"}


class Prog:
    def __init__(self):
        self.q = {e: [] for e in ENGS}
        self.cnt = {}
        self.last_wait = {e: {} for e in ENGS}
        self.floor = []

    def fence(self, sems=("T", "V", "A")):
        return [(s, self.val(s)) for s in sems if self.val(s) > 0]

    def set_floor(self, toks):
        self.floor = [t for t in toks if t is not None]

    def val(self, s):
        return self.cnt.get(s, 0)

    def emit(self, eng, fn, waits=(), incs=()):
        best = {}
        for tok in list(waits) + self.floor:
            if tok is None:
                continue
            s, v = tok
            assert v <= self.val(s), f"future wait {s}:{v} > {self.val(s)}"
            if v > best.get(s, -1):
                best[s] = v
        w = []
        for s, v in best.items():
            if self.last_wait[eng].get(s, -1) >= v:
                continue
            self.last_wait[eng][s] = v
            w.append((s, v))
        toks = []
        for s, n in incs:
            self.cnt[s] = self.val(s) + n
            toks.append((s, self.cnt[s]))
        self.q[eng].append((fn, tuple(w), tuple(incs)))
        return toks[0] if toks else None

    def te(self, fn, w=()):
        return self.emit("te", fn, w, (("T", 1),))

    def ve(self, fn, w=()):
        return self.emit("ve", fn, w, (("V", 1),))

    def sc(self, fn, w=()):
        return self.emit("sc", fn, w, (("A", 1),))

    def dma(self, eng, out_ap, in_ap, w=(), slow=False, sem=None):
        sem = sem or SEM_OF[eng]
        return self.emit(eng, lambda h, o=out_ap, i=in_ap, sl=slow: h.dma_start(
            out=o, in_=i, allow_slow_non_contiguous=sl),
            w, ((sem, 16),))

    def all(self, sem):
        return (sem, self.val(sem))


class Arena:
    def __init__(self, t, nbytes):
        self.t = t
        self.nbytes = nbytes
        self.off = 0
        self.marks = []
        self.hwm = 0

    def alloc(self, shape, dtype, name=""):
        pdim, nf = shape
        esz = 2 if dtype == BF16 else 4
        nb = nf * esz
        off = (self.off + 63) & ~63
        assert off + nb <= self.nbytes, f"arena OOM {name}: {off}+{nb}>{self.nbytes}"
        self.off = off + nb
        self.hwm = max(self.hwm, self.off)
        if name:
            self.offsets = getattr(self, "offsets", {})
            self.offsets.setdefault(name, (off, nb))
        f32len = (nb + 3) // 4
        ap = self.t[:, off // 4: off // 4 + f32len]
        if dtype == BF16:
            ap = ap.bitcast(BF16)
        return ap[:pdim, :nf]

    def mark(self):
        self.marks.append(self.off)

    def reset(self):
        self.off = self.marks.pop()


def bc64(col_ap):
    k = col_ap.shape[1]
    return col_ap.unsqueeze(2).broadcast_to((col_ap.shape[0], k, 64))


def r3(ap2d, k=16):
    return ap2d.rearrange("p (a b) -> p a b", a=k)


def build_program(tap=None, dbg=None, nocc=False):
    nc = bass.Bass("TRN2", target_bir_lowering=False)
    dr = {}

    def din(name, shape, dtype=F32):
        dr[name] = nc.dram_tensor(name, list(shape), dtype, kind="ExternalInput").ap()
        return dr[name]

    x_in = din("x", (T, D))
    y_out = nc.dram_tensor("y", [T, D], F32, kind="ExternalOutput").ap()
    dbg_out = (nc.dram_tensor("dbg", [128, 16384], F32, kind="ExternalOutput").ap()
               if dbg is not None else None)
    din("wqkv", (24, 128, NDC, 128))
    din("wconv", (24, 128, KCONV))
    din("wz", (NDC, 128, NDC, 128))
    din("wba", (NDC, 128, 48))
    din("wout", (NDC, 128, D))
    din("dtb", (16, 1))
    din("alog", (16, 1))
    din("w0f", (NDC, 128, D))
    din("gam0", (128, NDC))
    din("bet0", (128, NDC))
    din("wq", (NDC, 128, NDC, 128))
    din("wk", (NDC, 128, NDC, 128))
    din("wv", (NDC, 128, NDC, 128))
    din("wo", (NDC, 128, D))
    din("wr", (NDC, 128, E_MOE))
    din("wg", (E_MOE, NIC, 128, NDC, 128))
    din("wu", (E_MOE, NIC, 128, NDC, 128))
    din("wd", (E_MOE, NIC, 128, D))
    din("cid", (128, 128), BF16)
    din("cidf", (128, 128), F32)
    din("cP", (128, 128), BF16)
    din("ctri", (128, 128), F32)
    din("cim", (128, 128), F32)
    din("cbd16", (16, 1024), BF16)
    din("conesf", (1, 128), F32)
    din("ccos", (64, T), BF16)
    din("csin", (64, T), BF16)
    cc_in = nc.dram_tensor("cc_in", [1, 8], F32).ap()
    cc_out = nc.dram_tensor("cc_out", [1, 8], F32).ap()

    p = Prog()
    ctx = []

    def sbuf(name, shape, dtype):
        cm = nc.sbuf_tensor(name, list(shape), dtype)
        t = cm.__enter__()
        ctx.append(cm)
        return t

    arena_t = sbuf("arena", [128, 51904], F32)
    A = Arena(arena_t, 51904 * 4)
    build_program.arena = A
    x_tok = A.alloc([128, 16 * 1024], F32, "x_tok")

    cid = sbuf("sb_cid", [128, 128], BF16)
    cidf = sbuf("sb_cidf", [128, 128], F32)
    cP = sbuf("sb_cP", [128, 128], BF16)
    ctri = sbuf("sb_ctri", [128, 128], F32)
    cim = sbuf("sb_cim", [128, 128], F32)
    cbd16 = sbuf("sb_cbd16", [16, 1024], BF16)
    onesf = sbuf("sb_onesf", [1, 128], F32)
    dtb = sbuf("sb_dtb", [16, 1], F32)
    negea = sbuf("negea", [16, 1], F32)
    smalls = sbuf("smalls", [128, 16], F32)
    st6 = sbuf("st6", [128, 12], F32)
    mv = sbuf("mv", [128, 4], F32)
    cvals = sbuf("cvals", [128, 4], F32)
    nc.const_aps.register(F32, 1e-5, cvals[:, 0:1])
    nc.const_aps.register(F32, 1e-6, cvals[:, 1:2])
    nc.const_aps.register(F32, LN8, cvals[:, 2:3])

    cmA = nc.psum_tensor("psA", [128, 2048], F32)
    pA = cmA.__enter__(); ctx.append(cmA)
    cmB = nc.psum_tensor("psB", [128, 2048], F32)
    pB = cmB.__enter__(); ctx.append(cmB)

    sems = {}
    for s in ("L", "G", "T", "V", "A", "S", "C", "Lx", "Lc", "Lc2", "Lg", "Lr",
              "Lq", "Lo", "Lf", "G0", "G1", "G2", "G3", "G4", "G5", "G6", "G7",
              "Gw", "Go", "Go2", "Gr", "Gcs", "Gd"):
        cm = nc.semaphore(s)
        sems[s] = cm.__enter__()
        ctx.append(cm)

    GD = {}

    def gw(*keys):
        return [GD.get(k) for k in keys]

    def gs(key, tok):
        GD[key] = tok

    def maybe_dbg(label, ap, deps):
        if dbg != label:
            return None
        pdim, fdim = ap.shape[0], int(np.prod(ap.shape[1:]))
        return p.dma("gp", dbg_out[0:pdim, 0:fdim], ap, w=list(deps), sem="Gd")

    tk_cv = None
    for i, v in ((0, 1e-5), (1, 1e-6), (2, LN8)):
        tk_cv = p.ve(lambda h, i=i, v=v: h.memset(cvals[:, i:i + 1], v))
    x_ready = []
    for tc in range(NTC):
        p.dma("sy", x_tok[:, tc * 1024:(tc + 1) * 1024],
              x_in[tc * 128:(tc + 1) * 128, :], sem="Lx")
    x_ready = [p.all("Lx")] * NTC
    for sb_t, name in ((cid, "cid"), (cidf, "cidf"), (cP, "cP"), (ctri, "ctri"),
                       (cim, "cim"), (cbd16, "cbd16"), (onesf, "conesf"),
                       (dtb, "dtb")):
        p.dma("sy", sb_t[:, :], dr[name], sem="Lc")
    p.dma("sy", negea[:, :], dr["alog"], sem="Lc")
    tcons = p.all("Lc")
    tk_al = tcons
    tk_nega = p.sc(lambda h: h.activation(negea[:, :], negea[:, :], AF.Exp,
                                          bias=0.0, scale=1.0), w=[tk_al])
    tk_nega = p.ve(lambda h: h.tensor_scalar_mul(negea[:, :], negea[:, :], -1.0),
                   w=[tk_nega])

    def ln_tile(tc, dep, out_ap, extra=None):
        xsl = x_tok[:, tc * 1024:(tc + 1) * 1024]
        ex = extra or []
        t1 = p.ve(lambda h: h.tensor_scalar(out_ap, xsl, 1.0, 0.0, ALU.mult,
                                            ALU.add, accum_out=mv[:, 0:1]),
                  w=[dep, GD.get("lnstat")] + ex)
        t2 = p.ve(lambda h: h.scalar_tensor_tensor(out_ap, xsl, 1.0, xsl,
                                                   ALU.mult, ALU.mult,
                                                   accum_out=mv[:, 1:2]), w=[t1])
        t3 = p.ve(lambda h: h.tensor_scalar_mul(mv[:, 0:1], mv[:, 0:1],
                                                1.0 / 1024.0), w=[t2])
        t4 = p.ve(lambda h: h.tensor_scalar_mul(mv[:, 1:2], mv[:, 1:2],
                                                1.0 / 1024.0), w=[t3])
        t5 = p.ve(lambda h: h.tensor_scalar(smalls[:, 2:3], mv[:, 0:1], mv[:, 0:1],
                                            None, ALU.mult), w=[t4])
        t6 = p.ve(lambda h: h.tensor_sub(mv[:, 1:2], mv[:, 1:2], smalls[:, 2:3]),
                  w=[t5])
        t7 = p.sc(lambda h: h.activation(smalls[:, 1:2], mv[:, 1:2], AF.Sqrt,
                                         bias=1e-5, scale=1.0), w=[t6, tk_cv])
        t8 = p.ve(lambda h: h.reciprocal(smalls[:, 1:2], smalls[:, 1:2]), w=[t7])
        t9 = p.ve(lambda h: h.tensor_scalar(out_ap, xsl, mv[:, 0:1], smalls[:, 1:2],
                                            ALU.subtract, ALU.mult), w=[t8])
        gs("lnstat", t9)
        return t9

    def ln_to_xnd(tcs, xnd, lnscr, pbank_bf):
        toks = []
        xnd_re = xnd.rearrange("p (dc t) -> p dc t", dc=NDC)
        for i, tc in enumerate(tcs):
            sl = i % 2
            scr = lnscr[:, sl * 1024:(sl + 1) * 1024]
            tln = ln_tile(tc, x_ready[tc], scr, extra=gw(("lnscr", sl)))
            tt = None
            for dc in range(NDC):
                tt = p.te(lambda h, dc=dc, scr=scr: h.transpose(
                    pbank_bf[:, dc * 128:(dc + 1) * 128],
                    scr[:, dc * 128:(dc + 1) * 128], cid[:, :]),
                    w=[tln, tcons, GD.get("lnpsum")])
            gs(("lnscr", sl), tt)
            tev = p.ve(lambda h, i=i: h.tensor_copy(
                xnd_re[:, :, i * 128:(i + 1) * 128],
                pbank_bf[:, :].rearrange("p (dc t) -> p dc t", dc=NDC)), w=[tt])
            gs("lnpsum", tev)
            toks.append(tev)
        return toks

    A.mark()
    z_d = A.alloc([128, 8 * 1024], BF16, "z_d")
    O_ss = A.alloc([128, 8 * 16], F32, "Oss")
    S_sb = A.alloc([128, 8 * 64], F32, "S")
    S_bf = A.alloc([128, 8 * 64], BF16, "Sbf")
    G = A.alloc([16, 1024], F32, "G")
    carry = A.alloc([16, 8], F32, "carry")
    Gb = A.alloc([16, 8], F32, "Gb")
    Gbp = A.alloc([16, 8], F32, "Gbp")
    gblog = A.alloc([16, 8], F32, "gblog")
    glogT = A.alloc([1, 128], F32, "glogT")
    gbT = A.alloc([8, 16], F32, "gbT")
    ccar = A.alloc([128, 24 * 3], BF16, "ccar")
    O_raw = A.alloc([128, 8 * 1024], BF16, "Oraw")
    wconv_sb = A.alloc([128, 24 * KCONV], F32, "wconv")
    sqs = A.alloc([128, 1024], F32, "sqs")
    gcols = A.alloc([128, 16], F32, "gcols")
    q_d = A.alloc([128, 8 * 1024], BF16, "q_d")
    k_d = A.alloc([128, 8 * 1024], BF16, "k_d")
    v_d = A.alloc([128, 8 * 1024], BF16, "v_d")
    rows_st = A.alloc([128, 1024], F32, "rows")
    Grq = A.alloc([16, 1024], F32, "Grq")
    rkc = A.alloc([128, 2 * 16], F32, "rkc")
    mcolB = A.alloc([128, 2 * 16], F32, "mcol")
    colsB = A.alloc([128, 2 * 128], F32, "cols")
    lncB = A.alloc([128, 16], F32, "lnc")

    tk_car0 = p.ve(lambda h: h.memset(carry[:, :], 0.0))
    tk_rows0 = p.ve(lambda h: h.memset(rows_st[:, :], 0.0))
    tk_S0 = p.ve(lambda h: h.memset(S_sb[:, :], 0.0))
    tk_Sb0 = p.ve(lambda h: h.memset(S_bf[:, :], 0.0))
    tk_wconv = None
    tk_wout = [None] * NDC

    def l0a():
        nonlocal tk_wconv
        sil = {}
        fprev = []
        for hf in range(2):
            tcs = list(range(hf * 8, hf * 8 + 8))
            p.set_floor(fprev)
            A.mark()
            xnd = A.alloc([128, NDC * 1024], BF16, "l0a_xnd")
            xnd_re = xnd.rearrange("p (dc t) -> p dc t", dc=NDC)
            lnscr = A.alloc([128, 2 * 1024], BF16, "l0a_lnscr")
            wqs = A.alloc([128, 2 * NDC * 128], BF16, "l0a_wq")
            wba_sb = A.alloc([128, NDC * 48], BF16, "l0a_wba")
            mixed = A.alloc([128, 2 * 1027], BF16, "l0a_mx")
            diag = A.alloc([128, KCONV * 128], BF16, "l0a_dg")
            lnrq = A.alloc([16, 1024], F32, "l0a_lnrq")
            zrow = lnrq
            bdarg = Grq
            qts = lnscr[:, 0:1024]
            tk_zero = p.ve(lambda h: h.memset(lnrq[:, :], 0.0),
                           w=gw("lnrq_c"))
            if hf == 0:
                tk_wconv = p.dma("sy", wconv_sb[:, :],
                                 dr["wconv"].transpose([1, 0, 2]), sem="Lc2")
            tk_wba = p.dma("gp", wba_sb[:, :], dr["wba"].transpose([1, 0, 2]),
                           w=gw(("dmaq", "Gw")), sem="Gw")
            tmx0 = p.ve(lambda h: h.memset(mixed[:, :], 0.0), w=gw("mixed0", "mixed1"))

            pbf = pA[:, 0:512].bitcast(BF16)
            ln_toks = ln_to_xnd(tcs, xnd, lnscr, pbf)
            xnd_all = ln_toks[-1]
            if hf == 0:
                maybe_dbg("xn0", xnd[:, :], [xnd_all])

            pba = pB[0:48, 0:1024]
            tmm = None
            for dc in range(NDC):
                for s2 in range(2):
                    tmm = p.te(lambda h, dc=dc, s2=s2: h.matmul(
                        pba[:, s2 * 512:(s2 + 1) * 512],
                        wba_sb[:, dc * 48:(dc + 1) * 48],
                        xnd_re[:, dc, s2 * 512:(s2 + 1) * 512],
                        start=(dc == 0), stop=(dc == NDC - 1)),
                        w=[tk_wba, ln_toks[-1], GD.get("pB01"),
                           GD.get(("dS", 0)), GD.get(("dS", 64)),
                           GD.get("pB67c")])
            tbeta = p.sc(lambda h: h.activation(rows_st[0:16, :], pba[0:16, :],
                                                AF.Sigmoid, bias=0.0, scale=1.0),
                         w=[tmm, GD.get("rows_c")])
            su = p.sc(lambda h: h.activation(G[:, :], pba[32:48, :], AF.Abs,
                                             bias=dtb[:, 0:1], scale=1.0),
                      w=[tmm, tcons, GD.get("G_c")])
            se = p.sc(lambda h: h.activation(G[:, :], G[:, :], AF.Exp,
                                             bias=0.0, scale=-1.0), w=[su])
            slg = p.sc(lambda h: h.activation(G[:, :], G[:, :], AF.Ln,
                                              bias=1.0, scale=1.0), w=[se])
            sr = p.sc(lambda h: h.activation(bdarg[:, :], pba[32:48, :], AF.Relu,
                                             bias=dtb[:, 0:1], scale=1.0), w=[tmm])
            gs("pB01", sr)
            gs(("dmaq", "Gw"), tmm)
            if hf == 0 and dbg == "balog":
                tdbg = p.ve(lambda h: h.tensor_copy(sqs[:, :], pba[0:48, :][0:48, :]),
                            w=[tmm]) if False else None
                tdbg = p.ve(lambda h: h.tensor_copy(sqs[0:48, :], pba[0:48, :]),
                            w=[tmm])
                maybe_dbg("balog", sqs[0:48, :], [tdbg])
            tsp = p.ve(lambda h: h.tensor_add(G[:, :], G[:, :], bdarg[:, :]),
                       w=[slg, sr])
            tg = p.ve(lambda h: h.tensor_scalar_mul(G[:, :], G[:, :], negea[:, 0:1]),
                      w=[tsp, tk_nega])
            tG = p.ve(lambda h: h.tensor_tensor_scan(G[:, :], zrow[:, :], G[:, :],
                                                     carry[:, 0:1], ALU.add, ALU.add),
                      w=[tg, tk_zero, tk_car0])
            if hf == 0:
                maybe_dbg("g0", G[:, :], [tG])
                maybe_dbg("beta0", rows_st[0:16, :], [tbeta])
            tGb = p.ve(lambda h: h.tensor_copy(Gb[:, :], G[:, 127::128]), w=[tG])
            tp1 = p.ve(lambda h: h.tensor_copy(Gbp[:, 0:1], carry[:, 0:1]), w=[tGb])
            tp2 = p.ve(lambda h: h.tensor_copy(Gbp[:, 1:8], Gb[:, 0:7]), w=[tGb])
            tcar = p.ve(lambda h: h.tensor_copy(carry[:, 0:1], G[:, 1023:1024]),
                        w=[tG, tp1])
            tgl = p.ve(lambda h: h.tensor_sub(gblog[:, :], Gb[:, :], Gbp[:, :]),
                       w=[tGb, tp1, tp2])
            tg1 = p.te(lambda h: h.transpose(pB[0:8, 1024:1040], gblog[:, :],
                                             cidf[0:16, 0:16]),
                       w=[tgl, GD.get("pB45")])
            tg2 = p.ve(lambda h: h.tensor_copy(gbT[:, :], pB[0:8, 1024:1040]),
                       w=[tg1, GD.get("gbT_c")])
            gs("pB45", tg2)
            gs("pB67b", tg2)
            tglT = p.dma("sy", glogT[0:1, :].rearrange("o (c hh) -> o c hh", c=8),
                         gbT[:, :], w=[tg2, GD.get(("dmaq", "Lg"))], slow=True,
                         sem="Lg")
            gs("gbT_c", tglT)
            tneg = None
            for c in range(8):
                tneg = p.ve(lambda h, c=c: h.tensor_scalar(
                    G[:, c * 128:(c + 1) * 128], G[:, c * 128:(c + 1) * 128],
                    Gbp[:, c:c + 1], -1.0, ALU.subtract, ALU.mult), w=[tgl, tcar])
            tbd = None
            for c in range(8):
                tbd = p.ve(lambda h, c=c: h.tensor_scalar(
                    rows_st[32:48, c * 128:(c + 1) * 128],
                    G[:, c * 128:(c + 1) * 128],
                    gblog[:, c:c + 1], None, ALU.add), w=[tneg, GD.get("rows_c")])
            tbd = p.sc(lambda h: h.activation(rows_st[32:48, :], rows_st[32:48, :],
                                              AF.Exp, bias=0.0, scale=1.0), w=[tbd])
            tng = p.ve(lambda h: h.tensor_copy(rows_st[64:80, :], G[:, :]), w=[tneg])

            pmxL = [pA[:, 1024:2048], pA[:, 0:1024]]
            pcvL = [pB[:, 0:1024], pB[:, 1024:2048]]
            for cc in range(32):
                sl = cc % 2
                pmx = pmxL[sl]
                pcv = pcvL[sl]
                is_z = cc >= 24
                wsrc = dr["wz"][cc - 24] if is_z else dr["wqkv"][cc]
                wdst = wqs[:, sl * NDC * 128:(sl + 1) * NDC * 128]
                gsem = "G0" if sl == 0 else "G1"
                twl = p.dma("gp", wdst, wsrc.rearrange("p dc c -> p (dc c)"),
                            w=gw(("dmaq", gsem)), sem=gsem)
                tm = None
                for dc in range(NDC):
                    for s2 in range(2):
                        tm = p.te(lambda h, dc=dc, s2=s2, wdst=wdst, pmx=pmx:
                                  h.matmul(
                            pmx[:, s2 * 512:(s2 + 1) * 512],
                            wdst[:, dc * 128:(dc + 1) * 128],
                            xnd_re[:, dc, s2 * 512:(s2 + 1) * 512],
                            start=(dc == 0), stop=(dc == NDC - 1)),
                            w=[twl, xnd_all, GD.get(("pAm", sl))])
                gs(("dmaq", gsem), tm)
                if is_z:
                    dst = z_d[:, (cc - 24) * 1024:(cc - 23) * 1024]
                    tsl = p.sc(lambda h, dst=dst, pmx=pmx: h.activation(
                        dst, pmx[:, :], AF.Silu, bias=0.0, scale=1.0),
                        w=[tm, GD.get(("zd", cc - 24))])
                    gs(("pAm", sl), tsl)
                    sil[("z", cc - 24, hf)] = tsl
                    continue
                msl = mixed[:, sl * 1027:(sl + 1) * 1027]
                tev = p.sc(lambda h, msl=msl, pmx=pmx: h.copy(msl[:, 3:1027],
                                                              pmx[:, :]),
                           w=[tm, tmx0, GD.get(("mix", sl)), GD.get(("mixv", sl))])
                gs(("pAm", sl), tev)
                if hf == 1:
                    tcc2 = p.ve(lambda h, msl=msl, cc=cc: h.tensor_copy(
                        msl[:, 0:3], ccar[:, cc * 3:cc * 3 + 3]),
                        w=[sil.get(("ccar", cc)), tev, GD.get(("mix", sl))])
                else:
                    tcc2 = tev
                tsv = p.ve(lambda h, msl=msl, cc=cc: h.tensor_copy(
                    ccar[:, cc * 3:cc * 3 + 3], msl[:, 1024:1027]),
                    w=[tev, tcc2])
                gs(("mixv", sl), tsv)
                if hf == 0 and cc == 0:
                    maybe_dbg("mx0", msl[:, 0:1027], [tev, tcc2])
                sil[("ccar", cc)] = tsv
                td = None
                for j in range(KCONV):
                    td = p.ve(lambda h, j=j, cc=cc: h.tensor_scalar_mul(
                        diag[:, j * 128:(j + 1) * 128], cim[:, :],
                        wconv_sb[:, cc * KCONV + j:cc * KCONV + j + 1]),
                        w=[tk_wconv, tcons, GD.get("diag")])
                tcv = None
                for j in range(KCONV):
                    for s2 in range(2):
                        tcv = p.te(lambda h, j=j, s2=s2, msl=msl, pcv=pcv: h.matmul(
                            pcv[:, s2 * 512:(s2 + 1) * 512],
                            diag[:, j * 128:(j + 1) * 128],
                            msl[:, j + s2 * 512: j + s2 * 512 + 512],
                            start=(j == 0), stop=(j == KCONV - 1)),
                            w=[td, tcc2, tsv, GD.get(("pBc", sl)),
                               GD.get("gbT_c"), GD.get("pB45")])
                gs("diag", tcv)
                gs(("mix", sl), tcv)
                dst = (q_d if cc < 8 else k_d if cc < 16 else v_d)[
                    :, (cc % 8) * 1024:(cc % 8 + 1) * 1024]
                tsl = p.sc(lambda h, dst=dst, pcv=pcv: h.activation(
                    dst, pcv[:, :], AF.Silu, bias=0.0, scale=1.0),
                    w=[tcv, GD.get(("qkv", cc))])
                gs(("pBc", sl), tsl)
                sil[(cc, hf)] = tsl

            if hf == 0:
                maybe_dbg("q0", q_d[:, :], [sil[(qc2, 0)] for qc2 in range(8)])
                maybe_dbg("k0", k_d[:, :], [sil[(8 + qc2, 0)] for qc2 in range(8)])
                maybe_dbg("v0", v_d[:, :], [sil[(16 + qc2, 0)] for qc2 in range(8)])
                maybe_dbg("z0", z_d[:, :], [sil[("z", zc2, 0)] for zc2 in range(8)])
            gs("lnpsum", GD.get(("pAm", 1)))
            gs("pB45", GD.get(("pBc", 1)))
            gs("pB67b", GD.get(("pBc", 1)))
            gs("pB67a", GD.get(("pBc", 1)))
            gs("pB67c", GD.get(("pBc", 1)))
            gs("pB01", GD.get(("pBc", 0)))
            gs("pA23", GD.get(("pAm", 0)))
            gs("pA0r", GD.get(("pAm", 1)))
            gs("pA1r", GD.get(("pAm", 1)))
            gs("pOr", GD.get(("pAm", 0)))
            gs(("dS", 0), GD.get(("pBc", 0)))
            gs(("dS", 64), GD.get(("pBc", 0)))
            pqt = pA[:, 0:512].bitcast(BF16)
            plr = pB[0:16, 1024:1152]
            tprev = None
            for i in range(8):
                tt = None
                for qc in range(NDC):
                    tt = p.te(lambda h, qc=qc, i=i: h.transpose(
                        pqt[:, qc * 128:(qc + 1) * 128],
                        q_d[:, qc * 1024 + i * 128: qc * 1024 + (i + 1) * 128],
                        cid[:, :]), w=[sil[(qc, hf)], GD.get("lnpsum")])
                t1 = p.ve(lambda h: h.tensor_copy(qts[:, :], pqt[:, :]),
                          w=[tt, tprev, GD.get(("lnscr", 0)), GD.get(("lnscr", 1))])
                gs("lnpsum", t1)
                t2 = p.ve(lambda h: h.tensor_mul(sqs[:, :], qts[:, :], qts[:, :]),
                          w=[t1, GD.get("sqs_c")])
                t3 = p.ve(lambda h: h.tensor_reduce(
                    lncB[:, 0:16], sqs[:, :].rearrange("p (hh d) -> p hh d", d=64),
                    AX.X, ALU.add), w=[t2, GD.get("lnc_c")])
                gs("sqs_c", t3)
                t4 = p.sc(lambda h: h.activation(lncB[:, 0:16], lncB[:, 0:16], AF.Ln,
                                                 bias=1e-6, scale=1.0), w=[t3])
                t5 = p.ve(lambda h: h.tensor_scalar(lncB[:, 0:16], lncB[:, 0:16],
                                                    -0.5, LN8, ALU.mult, ALU.add),
                          w=[t4])
                t6 = p.te(lambda h: h.transpose(plr[:, 0:128], lncB[:, 0:16],
                                                cidf[:, :]), w=[t5, GD.get("pB45")])
                gs("lnc_c", t6)
                tprev = p.ve(lambda h, i=i: h.tensor_copy(
                    lnrq[:, i * 128:(i + 1) * 128], plr[0:16, 0:128]), w=[t6])
                gs("pB45", tprev)
            tgrq = p.ve(lambda h: h.tensor_sub(Grq[:, :], lnrq[:, :], G[:, :]),
                        w=[tprev, tng, GD.get("Grq_c")])
            gs("lnrq_c", tgrq)
            if hf == 0:
                maybe_dbg("lnrq0", lnrq[:, :], [tprev])
                maybe_dbg("negG0", G[:, :], [tng])
                maybe_dbg("grq0", Grq[:, :], [tgrq])
            tgrq = p.ve(lambda h: h.tensor_copy(rows_st[96:112, :], Grq[:, :]),
                        w=[tgrq])
            gs("rows_w", tgrq)

            fA = p.fence()
            A.reset()
            p.set_floor(fA)

            A.mark()
            wout_sb = A.alloc([128, NDC * 1024], BF16, "l0a_wout")
            for dc in range(NDC):
                p.dma("gp", wout_sb[:, dc * 1024:(dc + 1) * 1024],
                      dr["wout"][dc], sem="Go")
            woutall = p.all("Go")
            for dc in range(NDC):
                tk_wout[dc] = woutall

            A.mark()
            ktok = A.alloc([128, 2 * 1024], BF16, "l0a_ktok")
            vbet = A.alloc([128, 2 * 1024], BF16, "l0a_vb")
            ktil = A.alloc([128, 2 * 1024], BF16, "l0a_ktil")
            Msk = A.alloc([128, 128], F32, "l0a_Msk")
            Ebf = A.alloc([128, 128], BF16, "l0a_E")
            Abf = A.alloc([128, 128], BF16, "l0a_A")
            qtil = A.alloc([128, 128], BF16, "l0a_qt")
            abuf = A.alloc([128, 128], F32, "l0a_ab")

            pAt = pA[:, 0:512]
            pMb = pA[:, 512:1024]
            pO = pA[:, 1024:2048]
            pDS = pB[:, 0:512]
            pGcn = pB[:, 512:1024]
            pKT = pB[:, 1024:1536].bitcast(BF16)
            for c in range(8):
                sl = c % 2
                colss = colsB[:, sl * 128:(sl + 1) * 128]
                tcm = p.te(lambda h, c=c: h.transpose(
                    pGcn[:, 0:128], rows_st[:, c * 128:(c + 1) * 128],
                    cidf[:, :]),
                    w=[tbeta, tbd, tng, tgrq, tk_rows0, GD.get("pB67a")])
                tcol = p.ve(lambda h, colss=colss: h.tensor_copy(colss,
                                                                 pGcn[:, 0:128]),
                            w=[tcm, GD.get(("cols", sl))])
                gs("pB67a", tcol)
                ksl = ktok[:, sl * 1024:(sl + 1) * 1024]
                vbsl = vbet[:, sl * 1024:(sl + 1) * 1024]
                ktsl = ktil[:, sl * 1024:(sl + 1) * 1024]
                tt = None
                for qc in range(NDC):
                    tt = p.te(lambda h, qc=qc, c=c: h.transpose(
                        pKT[:, qc * 128:(qc + 1) * 128],
                        k_d[:, qc * 1024 + c * 128: qc * 1024 + (c + 1) * 128],
                        cid[:, :]), w=[sil[(8 + qc, hf)], GD.get("pB67b")])
                tke = p.ve(lambda h, ksl=ksl: h.tensor_copy(ksl, pKT[:, :]),
                           w=[tt, GD.get(("ktok", sl))])
                gs("pB67b", tke)
                tt = None
                for qc in range(NDC):
                    tt = p.te(lambda h, qc=qc, c=c: h.transpose(
                        pKT[:, qc * 128:(qc + 1) * 128],
                        v_d[:, qc * 1024 + c * 128: qc * 1024 + (c + 1) * 128],
                        cid[:, :]), w=[sil[(16 + qc, hf)], tke])
                tve = p.ve(lambda h, vbsl=vbsl, colss=colss: h.tensor_tensor(
                    r3(vbsl), r3(pKT[:, :]), bc64(colss[:, 0:16]), ALU.mult),
                    w=[tt, tcol, GD.get(("vb", sl))])
                gs("pB67b", tve)
                t2 = p.ve(lambda h, ksl=ksl: h.tensor_mul(sqs[:, :], ksl, ksl),
                          w=[tke, GD.get("sqs_c")])
                t3 = p.ve(lambda h, sl=sl: h.tensor_reduce(
                    rkc[:, sl * 16:(sl + 1) * 16],
                    sqs[:, :].rearrange("p (hh d) -> p hh d", d=64), AX.X, ALU.add),
                    w=[t2, GD.get(("rkc", sl))])
                gs("sqs_c", t3)
                t4 = p.sc(lambda h, sl=sl: h.activation(
                    rkc[:, sl * 16:(sl + 1) * 16], rkc[:, sl * 16:(sl + 1) * 16],
                    AF.Sqrt, bias=1e-6, scale=1.0), w=[t3])
                trk = p.ve(lambda h, sl=sl: h.reciprocal(
                    rkc[:, sl * 16:(sl + 1) * 16], rkc[:, sl * 16:(sl + 1) * 16]),
                    w=[t4])
                tm1 = p.ve(lambda h, sl=sl, colss=colss: h.tensor_mul(
                    mcolB[:, sl * 16:(sl + 1) * 16], rkc[:, sl * 16:(sl + 1) * 16],
                    colss[:, 32:48]), w=[trk, tcol, GD.get(("mcol", sl))])
                tkt = p.ve(lambda h, ktsl=ktsl, ksl=ksl, sl=sl: h.tensor_tensor(
                    r3(ktsl), r3(ksl), bc64(mcolB[:, sl * 16:(sl + 1) * 16]),
                    ALU.mult), w=[tm1, GD.get(("ktil", sl))])
                if hf == 0 and c == 0:
                    maybe_dbg("cols0", colss, [tcol])
                    maybe_dbg("ktok0", ksl, [tke])
                    maybe_dbg("vb0", vbsl, [tve])
                    maybe_dbg("ktil0", ktsl, [tkt])
                    maybe_dbg("rkc0", rkc[:, 0:16], [trk])
                tgc1 = p.te(lambda h, c=c: h.matmul(
                    pGcn[:, 128:144], onesf[0:1, 0:128], glogT[0:1, c * 16:(c + 1) * 16],
                    start=True, stop=True), w=[tglT, GD.get("pB67c")])
                tgc = p.sc(lambda h: h.activation(gcols[:, :], pGcn[:, 128:144],
                                                  AF.Exp, bias=0.0, scale=1.0),
                           w=[tgc1, GD.get("gcols_c")])
                gs("pB67c", tgc)
                gs(("dmaq", "Lg"), tgc1)

                oh = []
                for hh in range(16):
                    qc = hh // 2
                    pb0 = (hh % 2) * 64
                    ksrc = k_d[pb0:pb0 + 64,
                               qc * 1024 + c * 128: qc * 1024 + (c + 1) * 128]
                    qsrc = q_d[pb0:pb0 + 64,
                               qc * 1024 + c * 128: qc * 1024 + (c + 1) * 128]
                    tA = p.te(lambda h, ksrc=ksrc, qsrc=qsrc: h.matmul(
                        pAt[:, 0:128], ksrc, qsrc, start=True, stop=True),
                        w=[sil[(qc, hf)], sil[(8 + qc, hf)], GD.get("pA0r")])
                    tMb = p.te(lambda h, hh=hh, colss=colss: h.transpose(
                        pMb[:, 0:128],
                        colss[:, 96 + hh:97 + hh].broadcast_to((128, 128)),
                        cidf[:, :]), w=[tcol, GD.get("pA1r")])
                    tms = p.ve(lambda h: h.tensor_add(Msk[:, :], pMb[:, 0:128],
                                                      ctri[:, :]),
                               w=[tMb, GD.get("Msk_c")])
                    tE = p.sc(lambda h, hh=hh, colss=colss: h.activation(
                        Ebf[:, :], Msk[:, :], AF.Exp,
                        bias=colss[:, 64 + hh:65 + hh], scale=1.0),
                        w=[tms, GD.get("E_c")])
                    gs("Msk_c", tE)
                    tab = p.sc(lambda h, pb0=pb0: h.activation(
                        abuf[pb0:pb0 + 64, :], pMb[pb0:pb0 + 64, 0:128], AF.Exp,
                        bias=LN8, scale=1.0), w=[tMb, tk_cv, GD.get("ab_c")])
                    gs("pA1r", tab)
                    tAt = p.ve(lambda h, sl=sl, hh=hh: h.scalar_tensor_tensor(
                        Abf[:, :], pAt[:, 0:128],
                        rkc[:, sl * 16 + hh:sl * 16 + hh + 1], Ebf[:, :],
                        ALU.mult, ALU.mult), w=[tA, tE, trk, GD.get("Abf_c")])
                    gs("pA0r", tAt)
                    gs("E_c", tAt)
                    tqt = p.ve(lambda h, qsrc=qsrc, pb0=pb0: h.tensor_tensor(
                        qtil[pb0:pb0 + 64, :], qsrc, abuf[pb0:pb0 + 64, :], ALU.mult),
                        w=[tab, GD.get("qtil_c")])
                    gs("ab_c", tqt)
                    if hf == 0 and c == 0 and hh == 0:
                        maybe_dbg("e00", Ebf[:, :], [tE])
                        maybe_dbg("abf00", Abf[:, :], [tAt])
                        maybe_dbg("qt00", qtil[0:64, :], [tqt])
                        maybe_dbg("msk00", Msk[:, :], [tms])
                    to1 = p.te(lambda h, hh=hh, pb0=pb0: h.matmul(
                        pO[:, hh * 64:(hh + 1) * 64], qtil[pb0:pb0 + 64, :],
                        S_bf[pb0:pb0 + 64, (hh // 2) * 64:(hh // 2 + 1) * 64],
                        start=True, stop=False),
                        w=[tqt, GD.get(("Sbf", hh)) or tk_Sb0, GD.get("pOr")])
                    to2 = p.te(lambda h, hh=hh, ktsl=ktsl, vbsl=vbsl: h.matmul(
                        pO[:, hh * 64:(hh + 1) * 64], Abf[:, :],
                        vbsl[:, hh * 64:(hh + 1) * 64], start=False, stop=True),
                        w=[tAt, tve])
                    gs("qtil_c", to1)
                    gs("Abf_c", to2)
                    oh.append(to2)
                    tds = p.te(lambda h, hh=hh, pb0=pb0, ktsl=ktsl, vbsl=vbsl:
                               h.matmul(pDS[pb0:pb0 + 64, 0:64],
                                        ktsl[:, hh * 64:(hh + 1) * 64],
                                        vbsl[:, hh * 64:(hh + 1) * 64],
                                        start=True, stop=True,
                                        tile_position=(0, pb0)),
                               w=[tkt, tve, GD.get(("dS", pb0))])
                    tS = p.ve(lambda h, hh=hh, pb0=pb0: h.scalar_tensor_tensor(
                        S_sb[pb0:pb0 + 64, (hh // 2) * 64:(hh // 2 + 1) * 64],
                        S_sb[pb0:pb0 + 64, (hh // 2) * 64:(hh // 2 + 1) * 64],
                        gcols[pb0:pb0 + 64, hh:hh + 1],
                        pDS[pb0:pb0 + 64, 0:64], ALU.mult, ALU.add),
                        w=[tds, tgc, to1, GD.get(("S", hh)) or tk_S0])
                    gs(("dS", pb0), tS)
                    gs(("S", hh), tS)
                    gs("gcols_c", tS)
                    tSb = p.ve(lambda h, hh=hh, pb0=pb0: h.tensor_copy(
                        S_bf[pb0:pb0 + 64, (hh // 2) * 64:(hh // 2 + 1) * 64],
                        S_sb[pb0:pb0 + 64, (hh // 2) * 64:(hh // 2 + 1) * 64]),
                        w=[tS])
                    gs(("Sbf", hh), tSb)
                tsq_ = p.sc(lambda h: h.activation(sqs[:, :], pO[:, :], AF.Square,
                                                   bias=0.0, scale=1.0),
                            w=oh + gw("sqs_c"))
                tss = p.ve(lambda h, c=c: h.tensor_reduce(
                    O_ss[:, c * 16:(c + 1) * 16],
                    sqs[:, :].rearrange("p (hh d) -> p hh d", d=64), AX.X, ALU.add),
                    w=[tsq_] + gw("Oss_c"))
                gs("sqs_c", tss)
                tor = p.ve(lambda h, c=c: h.tensor_copy(
                    O_raw[:, c * 1024:(c + 1) * 1024], pO[:, :]),
                    w=[tss] + oh + gw(("Oraw", c)))
                gs("pOr", tor)
                if hf == 0 and c == 0:
                    maybe_dbg("o0", O_raw[:, 0:1024], [tor])
                    maybe_dbg("ss0", O_ss[:, 0:16], [tss])
                    maybe_dbg("sbf0", S_bf[:, :], [GD.get(("Sbf", 15))])
                    maybe_dbg("gcols0", gcols[:, :], [tgc])
                gs(("Braw", c), tor)
                gs(("cols", sl), tE)
                gs(("ktok", sl), tkt)
                gs(("vb", sl), tds)
                gs(("ktil", sl), tds)
                gs(("rkc", sl), tAt)
                gs(("mcol", sl), tkt)
            fB = p.fence()
            A.reset()
            p.set_floor(fB)

            A.mark()
            ztok = A.alloc([128, 2 * 1024], BF16, "l0a_zt")
            ofin = A.alloc([128, 2 * 1024], BF16, "l0a_of")
            ofd = A.alloc([128, 2 * 1024], BF16, "l0a_ofd")
            O_raw2 = O_raw
            pzt = pA[:, 0:512].bitcast(BF16)
            pyw = pA[:, 1024:2048]
            pof = pB[:, 0:512].bitcast(BF16)
            for c in range(8):
                tc = hf * 8 + c
                sl = c % 2
                tt = None
                for zc in range(NDC):
                    tt = p.te(lambda h, zc=zc, c=c: h.transpose(
                        pzt[:, zc * 128:(zc + 1) * 128],
                        z_d[:, zc * 1024 + c * 128: zc * 1024 + (c + 1) * 128],
                        cid[:, :]),
                        w=[sil[("z", zc, hf)], GD.get("pA0r"), GD.get("lnpsum")])
                tzs = p.ve(lambda h, sl=sl: h.tensor_copy(
                    ztok[:, sl * 1024:(sl + 1) * 1024], pzt[:, :]),
                    w=[tt, GD.get(("ztok", sl))])
                gs("pA0r", tzs)
                gs("lnpsum", tzs)
                trs = p.sc(lambda h, c=c: h.activation(
                    lncB[:, 0:16], O_ss[:, c * 16:(c + 1) * 16], AF.Sqrt,
                    bias=1e-6, scale=1.0 / 64.0),
                    w=[GD.get(("Braw", c)), GD.get("lnc_c")])
                trs = p.ve(lambda h: h.reciprocal(lncB[:, 0:16], lncB[:, 0:16]),
                           w=[trs])
                gs("Oss_c", trs)
                t1 = p.ve(lambda h, c=c: h.tensor_tensor(
                    r3(sqs[:, :]), r3(O_raw2[:, c * 1024:(c + 1) * 1024]),
                    bc64(lncB[:, 0:16]), ALU.mult),
                    w=[trs, GD.get(("Braw", c)), GD.get("sqs_c")])
                gs("lnc_c", t1)
                gs(("Oraw", c), t1)
                t2 = p.ve(lambda h, sl=sl: h.tensor_tensor(
                    ofin[:, sl * 1024:(sl + 1) * 1024], sqs[:, :],
                    ztok[:, sl * 1024:(sl + 1) * 1024], ALU.mult),
                    w=[t1, tzs, GD.get(("ofin", sl))])
                gs("sqs_c", t2)
                gs(("ztok", sl), t2)
                if hf == 0 and c == 0:
                    maybe_dbg("zt0", ztok[:, 0:1024], [tzs])
                    maybe_dbg("of0", ofin[:, 0:1024], [t2])
                tt = None
                for zc in range(NDC):
                    tt = p.te(lambda h, zc=zc, sl=sl: h.transpose(
                        pof[:, zc * 128:(zc + 1) * 128],
                        ofin[:, sl * 1024 + zc * 128: sl * 1024 + (zc + 1) * 128],
                        cid[:, :]), w=[t2, GD.get("pB01")])
                gs(("ofin", sl), tt)
                tfd = p.ve(lambda h, sl=sl: h.tensor_copy(
                    ofd[:, sl * 1024:(sl + 1) * 1024], pof[:, :]),
                    w=[tt, GD.get(("ofd", sl))])
                gs("pB01", tfd)
                ty = None
                for dc in range(NDC):
                    for s2 in range(2):
                        ty = p.te(lambda h, dc=dc, s2=s2, sl=sl: h.matmul(
                            pyw[:, s2 * 512:(s2 + 1) * 512],
                            ofd[:, sl * 1024 + dc * 128: sl * 1024 + (dc + 1) * 128],
                            wout_sb[:, dc * 1024 + s2 * 512: dc * 1024 + (s2 + 1) * 512],
                            start=(dc == 0), stop=(dc == NDC - 1)),
                            w=[tfd, tk_wout[dc], GD.get("pA23")])
                gs(("ofd", sl), ty)
                tres = p.ve(lambda h, tc=tc: h.scalar_tensor_tensor(
                    x_tok[:, tc * 1024:(tc + 1) * 1024], pyw[:, :], 1.0,
                    x_tok[:, tc * 1024:(tc + 1) * 1024], ALU.mult, ALU.add),
                    w=[ty, x_ready[tc]])
                gs("pA23", tres)
                gs(("zd", c), tres)
                x_ready[tc] = tres
            fprev = p.fence()
            A.reset()
            A.reset()

    l0a()
    p.set_floor(p.fence())
    A.reset()
    if tap == "l0a":
        return _finish(nc, p, sems, x_tok, y_out, x_ready, ctx)

    A.mark()
    w0 = A.alloc([128, NDC * 1024], F32, "l0f_w")
    tern = A.alloc([128, NDC * 1024], BF16, "l0f_t")
    gbrow = A.alloc([128, 1024], F32, "l0f_g")
    bbrow = A.alloc([128, 1024], F32, "l0f_b")
    gcol0 = A.alloc([128, NDC], F32, "l0f_gc")
    bcol0 = A.alloc([128, NDC], F32, "l0f_bc")
    xaf = A.alloc([128, 1024], F32, "l0f_xa")
    scr0 = A.alloc([128, 1024], F32, "l0f_s")
    qbf = A.alloc([128, 1024], BF16, "l0f_q")
    xqd = A.alloc([128, 1024], BF16, "l0f_xqd")
    red0 = A.alloc([128, 16], F32, "l0f_r")
    stat1 = A.alloc([1, 256], F32, "l0f_st")

    def l0f():
        for dc in range(NDC):
            p.dma("sy", w0[:, dc * 1024:(dc + 1) * 1024], dr["w0f"][dc], sem="Lf")
        p.dma("sy", gcol0[:, :], dr["gam0"], sem="Lf")
        p.dma("sy", bcol0[:, :], dr["bet0"], sem="Lf")
        tw = [p.all("Lf")] * NDC
        tg = p.all("Lf")
        tbc = tg
        pg = pB[:, 0:512]
        tpr = None
        for dc in range(NDC):
            t1 = p.te(lambda h, dc=dc: h.transpose(
                pg[:, 0:128], gcol0[:, dc:dc + 1].broadcast_to((128, 128)),
                cidf[:, :]), w=[tg, tcons, tpr, GD.get("pB01"),
                                GD.get("pB0chain")])
            t2 = p.ve(lambda h, dc=dc: h.tensor_copy(
                gbrow[:, dc * 128:(dc + 1) * 128], pg[:, 0:128]), w=[t1])
            t3 = p.te(lambda h, dc=dc: h.transpose(
                pg[:, 128:256], bcol0[:, dc:dc + 1].broadcast_to((128, 128)),
                cidf[:, :]), w=[tbc, t2])
            tpr = p.ve(lambda h, dc=dc: h.tensor_copy(
                bbrow[:, dc * 128:(dc + 1) * 128], pg[:, 128:256]), w=[t3])
            gs("pB0chain", tpr)
        gs("pB01", tpr)

        def xaff(tc):
            t7 = ln_tile(tc, x_ready[tc], xaf[:, :], extra=gw("xa_c"))
            t8 = p.ve(lambda h: h.tensor_mul(xaf[:, :], xaf[:, :], gbrow[:, :]),
                      w=[t7, tpr])
            t9 = p.ve(lambda h: h.tensor_add(xaf[:, :], xaf[:, :], bbrow[:, :]),
                      w=[t8])
            return t9

        tmx = p.ve(lambda h: h.memset(red0[:, 0:1], 0.0))
        for tc in range(NTC):
            t9 = xaff(tc)
            ta = p.ve(lambda h: h.tensor_reduce(red0[:, 1:2], xaf[:, :], AX.X,
                                                ALU.max, apply_absolute_value=True),
                      w=[t9, tmx])
            gs("xa_c", ta)
            tmx = p.ve(lambda h: h.tensor_tensor(red0[:, 0:1], red0[:, 0:1],
                                                 red0[:, 1:2], ALU.max), w=[ta])
        tt = p.te(lambda h: h.transpose(pB[0:1, 0:128], red0[:, 0:1], cidf[:, :]),
                  w=[tmx, GD.get("pB01"), GD.get("pB0chain")])
        tt = p.ve(lambda h: h.tensor_copy(stat1[0:1, 16:144], pB[0:1, 0:128]),
                  w=[tt])
        tt = p.ve(lambda h: h.tensor_reduce(stat1[0:1, 0:1], stat1[0:1, 16:144],
                                            AX.X, ALU.max), w=[tt])
        gs("pB0chain", tt)
        tz8 = p.ve(lambda h: h.memset(stat1[0:1, 8:16], 0.0))
        tb0 = p.dma("gp", cc_in[0:1, :], stat1[0:1, 8:16], w=[tz8], sem="G7")
        tb1 = p.dma("gp", cc_in[0:1, 0:1], stat1[0:1, 0:1], w=[tt, tb0], sem="G7")
        if nocc:
            tcc = p.dma("gp", cc_out[0:1, :], cc_in[0:1, :], w=[tb1], sem="G7")
        else:
            tcc = p.emit("gp", lambda h: h.collective_compute(
                "AllReduce", ALU.max, replica_groups=[list(range(8))],
                ins=[cc_in.opt()], outs=[cc_out.opt()]),
                waits=[tb1], incs=(("C", 1),))
        tgm = p.dma("gp", stat1[0:1, 1:2], cc_out[0:1, 0:1], w=[tcc], sem="G7")
        tsc_ = p.ve(lambda h: h.tensor_scalar(stat1[0:1, 2:3], stat1[0:1, 1:2],
                                              1e-8, 1.0 / 127.0, ALU.max, ALU.mult),
                    w=[tgm])
        trs_ = p.ve(lambda h: h.reciprocal(stat1[0:1, 3:4], stat1[0:1, 2:3]),
                    w=[tsc_])

        tsm = p.ve(lambda h: h.memset(red0[:, 2:3], 0.0))
        for dc in range(NDC):
            tsm1 = p.ve(lambda h, dc=dc: h.tensor_reduce(
                red0[:, 3:4], w0[:, dc * 1024:(dc + 1) * 1024], AX.X, ALU.add,
                apply_absolute_value=True), w=[tw[dc], tsm])
            tsm = p.ve(lambda h: h.tensor_add(red0[:, 2:3], red0[:, 2:3],
                                              red0[:, 3:4]), w=[tsm1])
        t2 = p.te(lambda h: h.transpose(pB[0:1, 128:256], red0[:, 2:3], cidf[:, :]),
                  w=[tsm, GD.get("pB01"), GD.get("pB0chain")])
        t2 = p.ve(lambda h: h.tensor_copy(stat1[0:1, 16:144], pB[0:1, 128:256]),
                  w=[t2])
        t2 = p.ve(lambda h: h.tensor_reduce(stat1[0:1, 4:5], stat1[0:1, 16:144],
                                            AX.X, ALU.add), w=[t2])
        gs("pB0chain", t2)
        tdl = p.ve(lambda h: h.tensor_scalar_mul(stat1[0:1, 5:6], stat1[0:1, 4:5],
                                                 0.7 / (1024.0 * 1024.0)), w=[t2])
        tdb1 = p.te(lambda h: h.transpose(
            pB[:, 256:257], stat1[0:1, 5:6].broadcast_to((1, 128)), cidf[0:1, 0:1]),
            w=[tdl, GD.get("pB0chain")])
        tdb = p.ve(lambda h: h.tensor_copy(smalls[:, 6:7], pB[:, 256:257]),
                   w=[tdb1])
        gs("pB0chain", tdb)
        tnum = p.ve(lambda h: h.memset(red0[:, 4:6], 0.0))
        ttern = tnum
        for dc in range(NDC):
            wsl = w0[:, dc * 1024:(dc + 1) * 1024]
            ta = p.sc(lambda h, wsl=wsl: h.activation(scr0[:, :], wsl, AF.Abs,
                                                      bias=0.0, scale=1.0),
                      w=[tw[dc], ttern, GD.get("scr0_c")])
            tmk = p.ve(lambda h: h.tensor_scalar(xaf[:, :], scr0[:, :],
                                                 smalls[:, 6:7], None, ALU.is_gt),
                       w=[ta, tdb, GD.get("xa_c")])
            tn1 = p.ve(lambda h: h.tensor_mul(scr0[:, :], scr0[:, :], xaf[:, :]),
                       w=[tmk])
            tn2 = p.ve(lambda h: h.tensor_reduce(red0[:, 6:7], scr0[:, :], AX.X,
                                                 ALU.add), w=[tn1])
            tn3 = p.ve(lambda h: h.tensor_add(red0[:, 4:5], red0[:, 4:5],
                                              red0[:, 6:7]), w=[tn2])
            td1 = p.ve(lambda h: h.tensor_reduce(red0[:, 7:8], xaf[:, :], AX.X,
                                                 ALU.add), w=[tmk])
            td2 = p.ve(lambda h: h.tensor_add(red0[:, 5:6], red0[:, 5:6],
                                              red0[:, 7:8]), w=[td1, tn3])
            tsg = p.sc(lambda h, wsl=wsl: h.activation(scr0[:, :], wsl, AF.Sign,
                                                       bias=0.0, scale=1.0),
                       w=[tn1, tn2])
            gs("scr0_c", None)
            ttern = p.ve(lambda h, dc=dc: h.tensor_mul(
                tern[:, dc * 1024:(dc + 1) * 1024], scr0[:, :], xaf[:, :]),
                w=[tsg, td2])
            gs("xa_c", ttern)
        t3 = p.te(lambda h: h.transpose(pB[0:1, 384:512], red0[:, 4:5], cidf[:, :]),
                  w=[ttern, GD.get("pB01"), GD.get("pB0chain")])
        t3a = p.ve(lambda h: h.tensor_copy(stat1[0:1, 16:144], pB[0:1, 384:512]),
                   w=[t3])
        t3b = p.ve(lambda h: h.tensor_reduce(stat1[0:1, 6:7], stat1[0:1, 16:144],
                                             AX.X, ALU.add), w=[t3a])
        gs("pB0chain", t3b)
        t4 = p.te(lambda h: h.transpose(pB[0:1, 512:640], red0[:, 5:6], cidf[:, :]),
                  w=[ttern, t3, t3b])
        t4a = p.ve(lambda h: h.tensor_copy(stat1[0:1, 16:144], pB[0:1, 512:640]),
                   w=[t4, t3b])
        t4b = p.ve(lambda h: h.tensor_reduce(stat1[0:1, 7:8], stat1[0:1, 16:144],
                                             AX.X, ALU.add), w=[t4a])
        t5 = p.ve(lambda h: h.tensor_scalar_max(stat1[0:1, 7:8], stat1[0:1, 7:8],
                                                1.0), w=[t4b])
        t5 = p.ve(lambda h: h.reciprocal(stat1[0:1, 7:8], stat1[0:1, 7:8]), w=[t5])
        t5 = p.ve(lambda h: h.tensor_mul(stat1[0:1, 6:7], stat1[0:1, 6:7],
                                         stat1[0:1, 7:8]), w=[t5])
        tsa = p.ve(lambda h: h.tensor_mul(stat1[0:1, 6:7], stat1[0:1, 6:7],
                                          stat1[0:1, 2:3]), w=[t5])
        tb2 = p.te(lambda h: h.transpose(
            pB[:, 0:1], stat1[0:1, 3:4].broadcast_to((1, 128)), cidf[0:1, 0:1]),
            w=[trs_, t3, t4b])
        tb2v = p.ve(lambda h: h.tensor_copy(smalls[:, 4:5], pB[:, 0:1]),
                    w=[tb2])
        tb3 = p.te(lambda h: h.transpose(
            pB[:, 128:129], stat1[0:1, 6:7].broadcast_to((1, 128)), cidf[0:1, 0:1]),
            w=[tsa, tb2, tb2v])
        tb3v = p.ve(lambda h: h.tensor_copy(smalls[:, 5:6], pB[:, 128:129]),
                    w=[tb3])
        gs("pB01", tb3v)

        maybe_dbg("l0fstat", stat1[0:1, 0:16], [tsa, tb3v, tb2v, tdb])
        maybe_dbg("gb0", gbrow[:, :], [tpr])
        maybe_dbg("bb0", bbrow[:, :], [tpr])
        maybe_dbg("tern0", tern[:, :], [ttern])
        pxq = pA[:, 0:512].bitcast(BF16)
        pyq = pA[:, 1024:2048]
        for tc in range(NTC):
            t9 = xaff(tc)
            tq1 = p.ve(lambda h: h.tensor_scalar(xaf[:, :], xaf[:, :],
                                                 smalls[:, 4:5], -127.0,
                                                 ALU.mult, ALU.max), w=[t9, tb2v])
            tq2 = p.ve(lambda h: h.tensor_scalar(xaf[:, :], xaf[:, :], 127.0,
                                                 MAGIC, ALU.min, ALU.add), w=[tq1])
            tq3 = p.ve(lambda h: h.tensor_scalar(qbf[:, :], xaf[:, :], MAGIC, None,
                                                 ALU.subtract),
                       w=[tq2, GD.get("qbf_c")])
            gs("xa_c", tq3)
            tt = None
            for dc in range(NDC):
                tt = p.te(lambda h, dc=dc: h.transpose(
                    pxq[:, dc * 128:(dc + 1) * 128],
                    qbf[:, dc * 128:(dc + 1) * 128], cid[:, :]),
                    w=[tq3, GD.get("pA0r"), GD.get("lnpsum")])
            gs("qbf_c", tt)
            txe = p.ve(lambda h: h.tensor_copy(xqd[:, :], pxq[:, :]),
                       w=[tt] + gw("xqd_c"))
            if tc == 0:
                maybe_dbg("q0f", qbf[:, :], [tq3])
                maybe_dbg("xq0", xqd[:, :], [txe])
            gs("pA0r", txe)
            gs("lnpsum", txe)
            ty = None
            for dc in range(NDC):
                for s2 in range(2):
                    ty = p.te(lambda h, dc=dc, s2=s2: h.matmul(
                        pyq[:, s2 * 512:(s2 + 1) * 512],
                        xqd[:, dc * 128:(dc + 1) * 128],
                        tern[:, dc * 1024 + s2 * 512: dc * 1024 + (s2 + 1) * 512],
                        start=(dc == 0), stop=(dc == NDC - 1)),
                        w=[txe, ttern, GD.get("pA23")])
            tres = p.ve(lambda h, tc=tc: h.scalar_tensor_tensor(
                x_tok[:, tc * 1024:(tc + 1) * 1024], pyq[:, :], smalls[:, 5:6],
                x_tok[:, tc * 1024:(tc + 1) * 1024], ALU.mult, ALU.add),
                w=[ty, tb3v, x_ready[tc]])
            gs("pA23", tres)
            gs("xqd_c", ty)
            x_ready[tc] = tres

    l0f()
    p.set_floor(p.fence())
    A.reset()
    if tap == "l0f":
        return _finish(nc, p, sems, x_tok, y_out, x_ready, ctx)

    A.mark()
    vext = A.alloc([128, NTC * 16 * 65], BF16, "l1a_vext")
    vext_re = vext.rearrange("p (t f) -> p t f", t=NTC)
    cs = A.alloc([128, 2048], BF16, "l1a_cs")
    qpair = A.alloc([128, 2048], BF16, "l1a_qp")
    kpair = A.alloc([128, 2048], BF16, "l1a_kp")
    qext2 = [A.alloc([65, 2048], BF16, "l1a_qe0"),
             A.alloc([65, 2048], BF16, "l1a_qe1")]
    kext2 = [A.alloc([65, 2048], BF16, "l1a_ke0"),
             A.alloc([65, 2048], BF16, "l1a_ke1")]
    _qe0_off = A.offsets["l1a_qe0"][0]
    vscr = arena_t[:, _qe0_off // 4: _qe0_off // 4 + 1024].bitcast(BF16)
    odma = A.alloc([128, 8 * 2048], BF16, "l1a_od")
    rho = A.alloc([65, 2048], BF16, "l1a_rho")
    _rho_off = A.offsets["l1a_rho"][0]
    sinb2 = arena_t[:, _rho_off // 4: _rho_off // 4 + 1024].bitcast(BF16)[64:128, :]
    mcols = A.alloc([128, 16], BF16, "l1a_mc")
    mrow = A.alloc([16, 128], BF16, "l1a_mr")
    wstr = A.alloc([128, 2 * NDC * 128], BF16, "l1a_ws")
    lnscr1 = A.alloc([128, 2 * 1024], BF16, "l1a_lns")
    pte = lnscr1[:, 0:1024]
    stage = lnscr1[:, 1024:2048]
    A.mark()
    xnd1 = A.alloc([128, NDC * 2048], BF16, "l1a_xnd")
    xnd1_re = xnd1.rearrange("p (dc t) -> p dc t", dc=NDC)

    def l1a():
        p.dma("gp", cs[0:64, :], dr["ccos"], sem="Gcs")
        p.dma("gp", cs[64:128, :], dr["ccos"], sem="Gcs")
        p.dma("gp", sinb2[:, :], dr["csin"], sem="Gcs")
        tk_cos = p.all("Gcs")
        tk_sin = tk_cos
        pbf = pA[:, 0:512].bitcast(BF16)
        ln_toks = ln_to_xnd(list(range(NTC)), xnd1, lnscr1, pbf)
        xnd_all = ln_toks[-1]
        if L1A_LIMIT.get("ones", True):
            tvo = p.ve(lambda h: h.memset(
                vext_re[:, :, :].rearrange("p t (hh f) -> p t hh f", hh=16)[:, :, :, 64:65],
                1.0), w=gw("vext_c"))
        else:
            tvo = None
        pvp = pB[:, 0:2048]
        for cc in range(NDC if L1A_LIMIT.get("vproj", True) else 0):
            sl = cc % 2
            wdst = wstr[:, sl * NDC * 128:(sl + 1) * NDC * 128]
            gsem = "G0" if sl == 0 else "G1"
            twl = p.dma("gp", wdst, dr["wv"][cc].rearrange("p dc c -> p (dc c)"),
                        w=gw(("dmaq", gsem)), sem=gsem)
            tm = None
            for dc in range(NDC):
                for s4 in range(4):
                    tm = p.te(lambda h, dc=dc, s4=s4, wdst=wdst: h.matmul(
                        pvp[:, s4 * 512:(s4 + 1) * 512],
                        wdst[:, dc * 128:(dc + 1) * 128],
                        xnd1_re[:, dc, s4 * 512:(s4 + 1) * 512],
                        start=(dc == 0), stop=(dc == NDC - 1)),
                        w=[twl, xnd_all, GD.get("pBall")])
            gs(("dmaq", gsem), tm)
            tev = p.sc(lambda h: h.activation(vscr[:, :], pvp[:, :], AF.Copy,
                                              bias=0.0, scale=1.0),
                       w=[tm, GD.get("rscr_c")])
            gs("pBall", tev)
            tt0 = None
            if L1A_LIMIT.get("vtrans", True):
                for bat in range(2):
                    pbk = pA[:, bat * 512:(bat + 1) * 512].bitcast(BF16)
                    tt0 = None
                    for ii in range(8):
                        i = bat * 8 + ii
                        tt0 = p.te(lambda h, i=i, ii=ii, pbk=pbk: h.transpose(
                            pbk[:, ii * 128:(ii + 1) * 128],
                            vscr[:, i * 128:(i + 1) * 128], cid[:, :]),
                            w=[tev, GD.get(("vtp", bat)), GD.get("lnpsum")])
                    out_ap = vext_re[:, bat * 8:(bat + 1) * 8, :].rearrange(
                        "p t (hh f) -> p t hh f", hh=16)[:, :, 2 * cc:2 * cc + 2, 0:64]
                    src_ap = pbk[:, :].rearrange("p (t a b) -> p t a b", t=8, a=2)
                    tt1 = p.ve(lambda h, out_ap=out_ap, src_ap=src_ap: h.tensor_copy(
                        out_ap, src_ap), w=[tt0, tvo, GD.get("vext_c")])
                    gs(("vtp", bat), tt1)
                    gs("vext_c", tt1)
                    gs("lnpsum", tt1)
            gs("rscr_c", tt0)

        tko = p.ve(lambda h: h.memset(kext2[0][64:65, :], 1.0), w=gw("kext_c"))
        tko = p.ve(lambda h: h.memset(kext2[1][64:65, :], 1.0), w=[tko])

        pqp = pB[0:64, 0:2048]
        pbld = pB[:, 0:2048]
        psmp = pB[:, 0:2048]
        psH = pA[:, 0:1024]
        pOH = pA[0:65, 1024:2048]

        def build_pair(j, dep_stream):
            toks = {}
            for wi, (which, wnm, dst) in enumerate(
                    (("q", "wq", qpair), ("k", "wk", kpair))):
                sl = wi % 2
                wdst = wstr[:, sl * NDC * 128:(sl + 1) * NDC * 128]
                gsem = "G0" if sl == 0 else "G1"
                twl = p.dma("gp", wdst, dr[wnm][j].rearrange("p dc c -> p (dc c)"),
                            w=gw(("dmaq", gsem)), sem=gsem)
                tm = None
                for dc in range(NDC):
                    for s4 in range(4):
                        tm = p.te(lambda h, dc=dc, s4=s4, wdst=wdst: h.matmul(
                            pbld[:, s4 * 512:(s4 + 1) * 512],
                            wdst[:, dc * 128:(dc + 1) * 128],
                            xnd1_re[:, dc, s4 * 512:(s4 + 1) * 512],
                            start=(dc == 0), stop=(dc == NDC - 1)),
                            w=[twl, xnd_all, GD.get("pBall")])
                gs(("dmaq", gsem), tm)
                tev = p.sc(lambda h, dst=dst, which=which: h.activation(
                    dst[:, :], pbld[:, :], AF.Copy, bias=0.0,
                    scale=0.125 if which == "q" else 1.0),
                    w=[tm, GD.get((which + "pair_c"))] + [dep_stream])
                gs("pBall", tev)
                tr = None
                for s4 in range(4):
                    tr = p.te(lambda h, s4=s4, dst=dst: h.matmul(
                        pbld[:, s4 * 512:(s4 + 1) * 512], cP[:, :],
                        dst[:, s4 * 512:(s4 + 1) * 512], start=True, stop=True),
                        w=[tev, tcons])
                t2a = p.ve(lambda h: h.tensor_tensor(
                    pbld[0:64, :], pbld[0:64, :], sinb2[:, :], ALU.mult),
                    w=[tr, tk_sin])
                t2b = p.ve(lambda h: h.tensor_tensor(
                    pbld[64:128, :], pbld[64:128, :], sinb2[:, :], ALU.mult),
                    w=[t2a])
                t3 = p.ve(lambda h, dst=dst: h.tensor_tensor(
                    dst[:, :], dst[:, :], cs[:, :], ALU.mult),
                    w=[tr, tk_cos, tev])
                t4 = p.ve(lambda h, dst=dst: h.tensor_tensor(
                    dst[:, :], dst[:, :], pbld[:, :], ALU.add),
                    w=[t2b, t3])
                gs("pBall", t4)
                toks[which] = t4
            return toks

        def extract_head(hh, btoks):
            pb0 = (hh % 2) * 64
            par = hh % 2
            qext = qext2[par]
            kext = kext2[par]
            dep_rd = GD.get(("qk_rd", par))
            tqe = p.dma("sy", qext[0:64, :], qpair[pb0:pb0 + 64, :],
                        w=[btoks["q"], dep_rd, GD.get("rscr_c")], sem="Lq")
            tke_ = p.dma("sy", kext[0:64, :], kpair[pb0:pb0 + 64, :],
                         w=[btoks["k"], dep_rd, tko], sem="Lq")
            gs("qpair_c", tke_)
            gs("kpair_c", tke_)
            tsm = None
            for qc in range(NTC):
                bank = qc % 2
                ts1 = p.te(lambda h, qc=qc, bank=bank, qext=qext, kext=kext:
                    h.matmul(
                    pbld[:, bank * 512:bank * 512 + 256],
                    qext[0:64, qc * 128:(qc + 1) * 128],
                    kext[0:64, :][:, 7::8], start=True, stop=True),
                    w=[tqe, tke_, GD.get(("smp", bank)), GD.get("pBall")])
                tsm = p.ve(lambda h, qc=qc, bank=bank: h.reduce_max(
                    mcols[:, qc:qc + 1], pbld[:, bank * 512:bank * 512 + 256],
                    AX.X, negate=True), w=[ts1, GD.get("mcols_c")])
                gs(("smp", bank), tsm)
            pmr = pB[0:16, 1024:1088].bitcast(BF16)
            tmt = p.te(lambda h: h.transpose(pmr, mcols[:, :], cid[:, :]),
                       w=[tsm, GD.get("pBall")])
            tmv = p.ve(lambda h: h.tensor_copy(mrow[:, :], pmr),
                       w=[tmt, GD.get("mrow_c")])
            gs("pBall", tmv)
            tmr = p.dma("sy", qext[64:65, :].rearrange("o (qc t) -> o qc t", qc=16),
                        mrow[:, :], w=[tmv], slow=True, sem="Lq")
            gs("mcols_c", tmr)
            gs("mrow_c", tmr)
            return tqe, tke_, tmr

        LIM = dict(L1A_LIMIT)
        nheads = LIM.get("heads", 16)
        do_div = LIM.get("div", True)
        do_wo = LIM.get("wo", True)
        if nheads == 0:
            return
        btoks = build_pair(0, None)
        ex_cur = extract_head(0, btoks)
        ostage_dep = None
        for hh in range(nheads):
            pb0 = (hh % 2) * 64
            cc = hh // 2
            if hh % 2 == 1 and hh < nheads - 1:
                btoks = build_pair(hh // 2 + 1, GD.get("stream_done"))
            ex_nxt = extract_head(hh + 1, btoks) if hh < nheads - 1 else None
            exq, exk, exm = ex_cur
            qext = qext2[hh % 2]
            kext = kext2[hh % 2]
            t_qk_last = None
            for th in range(2):
                tav = None
                for tsc in range(NTC):
                    tmma = p.te(lambda h, tsc=tsc, th=th, qext=qext, kext=kext:
                        h.matmul(
                        psH[:, 0:512], kext[:, tsc * 128:(tsc + 1) * 128],
                        qext[:, th * 1024:th * 1024 + 512],
                        start=True, stop=True),
                        w=[exq, exk, exm, GD.get(("psH", 0))])
                    tmmb = p.te(lambda h, tsc=tsc, th=th, qext=qext, kext=kext:
                        h.matmul(
                        psH[:, 512:1024], kext[:, tsc * 128:(tsc + 1) * 128],
                        qext[:, th * 1024 + 512:th * 1024 + 1024],
                        start=True, stop=True), w=[GD.get(("psH", 1))])
                    t_qk_last = tmmb
                    texpa = p.sc(lambda h: h.activation(
                        pte[:, 0:512], psH[:, 0:512], AF.Exp,
                        bias=0.0, scale=1.0),
                        w=[tmma, GD.get(("pte", 0))])
                    gs(("psH", 0), texpa)
                    texpb = p.sc(lambda h: h.activation(
                        pte[:, 512:1024], psH[:, 512:1024], AF.Exp,
                        bias=0.0, scale=1.0),
                        w=[tmmb, GD.get(("pte", 1))])
                    gs(("psH", 1), texpb)
                    for s2 in range(2):
                        tav = p.te(lambda h, tsc=tsc, s2=s2, cc2=hh: h.matmul(
                            pOH[:, s2 * 512:(s2 + 1) * 512],
                            vext_re[:, tsc, :].rearrange("p (hh f) -> p hh f", hh=16)[
                                :, cc2:cc2 + 1, :].rearrange("p a b -> p (a b)"),
                            pte[:, s2 * 512:(s2 + 1) * 512],
                            start=(tsc == 0), stop=(tsc == NTC - 1)),
                            w=[texpa if s2 == 0 else texpb,
                               GD.get("vext_c") if tsc == 0 else None,
                               GD.get("psO_c") if tsc == 0 else None])
                        gs(("pte", s2), tav)
                trr = p.ve(lambda h: h.reciprocal(pOH[64:65, :], pOH[64:65, :]),
                           w=[tav])
                trs = p.ve(lambda h, hh=hh, th=th: h.tensor_copy(
                    rho[32:33, th * 1024:(th + 1) * 1024], pOH[64:65, :]),
                    w=[trr, GD.get("rho_st")])
                trd = p.dma("sy", rho[hh:hh + 1, th * 1024:(th + 1) * 1024],
                            rho[32:33, th * 1024:(th + 1) * 1024], w=[trs],
                            sem="Lr")
                gs("rho_st", trd)
                gs(("rho", hh), trd)
                if pb0 == 0:
                    toe = p.ve(lambda h, cc=cc, th=th: h.tensor_copy(
                        odma[0:64, cc * 2048 + th * 1024: cc * 2048 + (th + 1) * 1024],
                        pOH[0:64, :]), w=[tav, trr, GD.get(("odiv", cc))])
                    gs("psO_c", toe)
                else:
                    tst = p.ve(lambda h: h.tensor_copy(stage[0:64, :], pOH[0:64, :]),
                               w=[tav, trr, ostage_dep])
                    toe = p.dma("sy", odma[64:128, cc * 2048 + th * 1024:
                                           cc * 2048 + (th + 1) * 1024],
                                stage[0:64, :], w=[tst], sem="Lo")
                    ostage_dep = toe
                    gs("psO_c", tst)
                gs(("oev", hh, th), toe)
                gs("stream_done", toe)
            gs(("qk_rd", hh % 2), t_qk_last)
            ex_cur = ex_nxt
        if not do_div:
            return
        pexp = pB[:, 0:2048]
        for j in range(nheads // 2):
            deps = [GD.get(("rho", 2 * j)), GD.get(("rho", 2 * j + 1)),
                    GD.get(("oev", 2 * j, 0)), GD.get(("oev", 2 * j, 1)),
                    GD.get(("oev", 2 * j + 1, 0)), GD.get(("oev", 2 * j + 1, 1))]
            tex = None
            for s4 in range(4):
                tex = p.te(lambda h, j=j, s4=s4: h.matmul(
                    pexp[:, s4 * 512:(s4 + 1) * 512],
                    cbd16[:, j * 128:(j + 1) * 128],
                    rho[0:16, s4 * 512:(s4 + 1) * 512],
                    start=True, stop=True),
                    w=deps + [tcons, GD.get("pBall"), GD.get("psO_c"),
                              p.all("Lr"), p.all("Lo")])
            tdv = p.ve(lambda h, j=j: h.tensor_tensor(
                odma[:, j * 2048:(j + 1) * 2048], odma[:, j * 2048:(j + 1) * 2048],
                pexp[:, :], ALU.mult), w=[tex] + deps)
            gs("pBall", tdv)
            gs(("odiv", j), tdv)
        if not do_wo:
            return
        p.set_floor(p.fence())
        A.reset()
        A.mark()
        wo_sb = A.alloc([128, NDC * 1024], BF16, "l1a_wo")
        for dc in range(NDC):
            p.dma("gp", wo_sb[:, dc * 1024:(dc + 1) * 1024], dr["wo"][dc],
                  sem="Go2")
        two = [p.all("Go2")] * NDC
        pyw = pA[:, 0:1024]
        for tc in range(NTC):
            ty = None
            for dc in range(NDC):
                for s2 in range(2):
                    ty = p.te(lambda h, dc=dc, s2=s2, tc=tc: h.matmul(
                        pyw[:, s2 * 512:(s2 + 1) * 512],
                        odma[:, dc * 2048 + tc * 128: dc * 2048 + (tc + 1) * 128],
                        wo_sb[:, dc * 1024 + s2 * 512: dc * 1024 + (s2 + 1) * 512],
                        start=(dc == 0), stop=(dc == NDC - 1)),
                        w=[GD.get(("odiv", dc)), two[dc], GD.get("pyw_c"),
                           GD.get(("psH", 0)), GD.get(("psH", 1))])
            tres = p.ve(lambda h, tc=tc: h.scalar_tensor_tensor(
                x_tok[:, tc * 1024:(tc + 1) * 1024], pyw[:, :], 1.0,
                x_tok[:, tc * 1024:(tc + 1) * 1024], ALU.mult, ALU.add),
                w=[ty, x_ready[tc]])
            gs("pyw_c", tres)
            x_ready[tc] = tres

    l1a()
    p.set_floor(p.fence())
    A.reset()
    A.reset()
    if tap == "l1a":
        return _finish(nc, p, sems, x_tok, y_out, x_ready, ctx)

    A.mark()
    xnd2 = A.alloc([128, NDC * 2048], BF16, "l1f_xnd")
    xnd2_re = xnd2.rearrange("p (dc t) -> p dc t", dc=NDC)
    hbuf = A.alloc([128, NIC * 1024], BF16, "l1f_h")
    wdall = A.alloc([128, NIC * 1024], BF16, "l1f_wd")
    wgs = A.alloc([128, 2 * NDC * 128], BF16, "l1f_wg")
    wus = A.alloc([128, 2 * NDC * 128], BF16, "l1f_wu")
    gate = A.alloc([128, 1024], BF16, "l1f_gate")
    cwc = A.alloc([128, NTC * 16], F32, "l1f_cw")
    wr_sb = A.alloc([128, NDC * 16], BF16, "l1f_wr")
    lnscr2 = A.alloc([128, 2 * 1024], BF16, "l1f_lns")
    lgr = lnscr2[0:16, 0:2048]
    prb = A.alloc([128, 32], F32, "l1f_p")
    prm = A.alloc([128, 8], F32, "l1f_pm")

    def l1f():
        pbf = pA[:, 0:512].bitcast(BF16)
        ln_toks = ln_to_xnd(list(range(NTC)), xnd2, lnscr2, pbf)
        xnd_all = ln_toks[-1]
        twr = p.dma("gp",
                    wr_sb[:, :].rearrange("p (dc e) -> p dc e", dc=NDC)[:, :, 0:E_MOE],
                    dr["wr"].transpose([1, 0, 2]), sem="Gr")
        plg = pB[0:E_MOE, 0:2048]
        tmm = None
        for dc in range(NDC):
            for s4 in range(4):
                tmm = p.te(lambda h, dc=dc, s4=s4: h.matmul(
                    plg[:, s4 * 512:(s4 + 1) * 512],
                    wr_sb[:, :].rearrange("p (dc e) -> p dc e", dc=NDC)[:, dc, 0:E_MOE],
                    xnd2_re[:, dc, s4 * 512:(s4 + 1) * 512],
                    start=(dc == 0), stop=(dc == NDC - 1)),
                    w=[twr, xnd_all, GD.get("pBall"), GD.get("pB01")])
        tlg = p.ve(lambda h: h.tensor_copy(lgr[0:E_MOE, :], plg[:, :]), w=[tmm])
        gs("pBall", tlg)
        gs("pB01", tlg)
        for tc in range(NTC):
            plgt = pB[:, 0:8].bitcast(BF16)[:, 0:E_MOE]
            t1 = p.te(lambda h, tc=tc: h.transpose(
                plgt, lgr[0:E_MOE, tc * 128:(tc + 1) * 128],
                cid[0:E_MOE, 0:E_MOE]),
                w=[tlg, GD.get("pB01"), tcons])
            t2 = p.ve(lambda h: h.reduce_max(prb[:, 9:10], plgt,
                                             AX.X, negate=True),
                      w=[t1, GD.get("prb_c")])
            t3 = p.sc(lambda h: h.activation(prb[:, 0:E_MOE], plgt,
                                             AF.Exp, bias=prb[:, 9:10], scale=1.0,
                                             accum_out=prb[:, 10:11]), w=[t2])
            gs("pB01", t3)
            t4 = p.ve(lambda h: h.reciprocal(prb[:, 10:11], prb[:, 10:11]), w=[t3])
            t5 = p.ve(lambda h: h.tensor_scalar_mul(prb[:, 0:E_MOE],
                                                    prb[:, 0:E_MOE], prb[:, 10:11]),
                      w=[t4])
            t6 = p.ve(lambda h: h.reduce_max(prm[:, 0:1], prb[:, 0:E_MOE], AX.X),
                      w=[t5, GD.get("prm_c")])
            tmsk = p.ve(lambda h: h.tensor_scalar(prb[:, 16:16 + E_MOE],
                                                  prb[:, 0:E_MOE], prm[:, 0:1],
                                                  NEG_BIG, ALU.is_ge, ALU.mult),
                        w=[t6])
            tpm = p.ve(lambda h: h.tensor_add(prb[:, 16:16 + E_MOE],
                                              prb[:, 16:16 + E_MOE],
                                              prb[:, 0:E_MOE]), w=[tmsk])
            t8 = p.ve(lambda h: h.reduce_max(prm[:, 1:2], prb[:, 16:16 + E_MOE],
                                             AX.X), w=[tpm])
            tind = p.ve(lambda h: h.tensor_scalar(prb[:, 16:16 + E_MOE],
                                                  prb[:, 0:E_MOE], prm[:, 1:2],
                                                  None, ALU.is_ge), w=[t8])
            tcwu = p.ve(lambda h: h.tensor_mul(prb[:, 16:16 + E_MOE],
                                               prb[:, 16:16 + E_MOE],
                                               prb[:, 0:E_MOE]), w=[tind])
            tsum = p.ve(lambda h: h.tensor_reduce(prm[:, 2:3],
                                                  prb[:, 16:16 + E_MOE], AX.X,
                                                  ALU.add), w=[tcwu])
            trc = p.ve(lambda h: h.reciprocal(prm[:, 2:3], prm[:, 2:3]), w=[tsum])
            tcw = p.ve(lambda h, tc=tc: h.tensor_scalar_mul(
                cwc[:, tc * 16:tc * 16 + E_MOE], prb[:, 16:16 + E_MOE],
                prm[:, 2:3]), w=[trc, GD.get(("cw", tc))])
            gs("prb_c", tcw)
            gs("prm_c", tcw)
            gs(("cw", tc), tcw)
        gs("pyd_c", GD.get(("cw", NTC - 1)))
        gs("pB01x", GD.get(("cw", NTC - 1)))
        maybe_dbg("cw0", cwc[:, :], [GD.get(("cw", tc2)) for tc2 in range(NTC)])
        maybe_dbg("lg0", lgr[0:16, :], [tlg])
        pguA = [pA[:, 0:1024], pA[:, 1024:2048]]
        pguB = [pB[:, 0:1024], pB[:, 1024:2048]]
        pyd = pA[:, 0:1024]
        for e in range(L1F_LIMIT.get("experts", E_MOE)):
            pre = {}
            for ic in range(2):
                sl = ic % 2
                gsg = "G2" if sl == 0 else "G3"
                gsu = "G4" if sl == 0 else "G5"
                pre[(ic, "g")] = p.dma(
                    "gp", wgs[:, sl * NDC * 128:(sl + 1) * NDC * 128],
                    dr["wg"][e][ic].rearrange("p dc c -> p (dc c)"),
                    w=gw(("dmaq", gsg)), sem=gsg)
                pre[(ic, "u")] = p.dma(
                    "gp", wus[:, sl * NDC * 128:(sl + 1) * NDC * 128],
                    dr["wu"][e][ic].rearrange("p dc c -> p (dc c)"),
                    w=gw(("dmaq", gsu)), sem=gsu)
            for ic in range(NIC):
                p.dma("gp", wdall[:, ic * 1024:(ic + 1) * 1024],
                      dr["wd"][e][ic], w=gw(("dmaq", "G6")), sem="G6")
            twd = [p.all("G6")] * NIC
            for th in range(2):
                for ic in range(NIC):
                    sl = ic % 2
                    pgu = pguA[sl]
                    pgu2 = pguB[sl]
                    wgd = wgs[:, sl * NDC * 128:(sl + 1) * NDC * 128]
                    wud = wus[:, sl * NDC * 128:(sl + 1) * NDC * 128]
                    gsg = "G2" if sl == 0 else "G3"
                    gsu = "G4" if sl == 0 else "G5"
                    if th == 0 and (ic, "g") in pre:
                        twg = pre[(ic, "g")]
                        twu = pre[(ic, "u")]
                    else:
                        twg = p.dma("gp", wgd,
                                    dr["wg"][e][ic].rearrange("p dc c -> p (dc c)"),
                                    w=gw(("dmaq", gsg)), sem=gsg)
                        twu = p.dma("gp", wud,
                                    dr["wu"][e][ic].rearrange("p dc c -> p (dc c)"),
                                    w=gw(("dmaq", gsu)), sem=gsu)
                    tg_ = None
                    for dc in range(NDC):
                        for s2 in range(2):
                            tg_ = p.te(lambda h, dc=dc, s2=s2, th=th, wgd=wgd,
                                       pgu=pgu:
                                       h.matmul(pgu[:, s2 * 512:(s2 + 1) * 512],
                                                wgd[:, dc * 128:(dc + 1) * 128],
                                                xnd2_re[:, dc, th * 1024 + s2 * 512:
                                                        th * 1024 + (s2 + 1) * 512],
                                                start=(dc == 0),
                                                stop=(dc == NDC - 1)),
                                       w=[twg, xnd_all, GD.get(("pgu_c", sl)),
                                          GD.get("pyd_r"),
                                          GD.get("lnpsum")])
                    gs(("dmaq", gsg), tg_)
                    gs("lnpsum", tg_)
                    tsg = p.sc(lambda h, pgu=pgu: h.activation(
                        gate[:, :], pgu[:, :], AF.Silu, bias=0.0, scale=1.0),
                        w=[tg_, GD.get("gate_c"), GD.get("gate_dbg")])
                    gs(("pgu_c", sl), tsg)
                    tu_ = None
                    for dc in range(NDC):
                        for s2 in range(2):
                            tu_ = p.te(lambda h, dc=dc, s2=s2, th=th, wud=wud,
                                       pgu2=pgu2:
                                       h.matmul(pgu2[:, s2 * 512:(s2 + 1) * 512],
                                                wud[:, dc * 128:(dc + 1) * 128],
                                                xnd2_re[:, dc, th * 1024 + s2 * 512:
                                                        th * 1024 + (s2 + 1) * 512],
                                                start=(dc == 0),
                                                stop=(dc == NDC - 1)),
                                       w=[twu, xnd_all, GD.get(("pgu2_c", sl)),
                                          GD.get("pB01x")])
                    gs(("dmaq", gsu), tu_)
                    th_ = p.ve(lambda h, ic=ic, pgu2=pgu2: h.tensor_tensor(
                        hbuf[:, ic * 1024:(ic + 1) * 1024], pgu2[:, :], gate[:, :],
                        ALU.mult), w=[tu_, tsg, GD.get(("hb", ic)),
                                      GD.get("hb_dbg")])
                    gs(("pgu2_c", sl), th_)
                    gs("gate_c", th_)
                    gs(("hcur", ic), th_)
                    if e == 0 and th == 0 and ic == 0:
                        tdbg1 = maybe_dbg("g00", gate[:, :], [tsg])
                        tdbg2 = maybe_dbg("h00", hbuf[:, 0:1024], [th_])
                        if tdbg1 is not None:
                            gs("gate_dbg", tdbg1)
                        if tdbg2 is not None:
                            gs("hb_dbg", tdbg2)
                for t2c in range(8):
                    tc = th * 8 + t2c
                    ty = None
                    for ic in range(NIC):
                        for s2 in range(2):
                            ty = p.te(lambda h, ic=ic, s2=s2, t2c=t2c: h.matmul(
                                pyd[:, s2 * 512:(s2 + 1) * 512],
                                hbuf[:, ic * 1024 + t2c * 128:
                                     ic * 1024 + (t2c + 1) * 128],
                                wdall[:, ic * 1024 + s2 * 512:
                                      ic * 1024 + (s2 + 1) * 512],
                                start=(ic == 0), stop=(ic == NIC - 1)),
                                w=[GD.get(("hcur", ic)), twd[ic],
                                   GD.get("pyd_c"), GD.get(("pgu_c", 0)),
                                   GD.get(("pgu_c", 1))])
                    tres = p.ve(lambda h, tc=tc, e=e: h.scalar_tensor_tensor(
                        x_tok[:, tc * 1024:(tc + 1) * 1024], pyd[:, :],
                        cwc[:, tc * 16 + e:tc * 16 + e + 1],
                        x_tok[:, tc * 1024:(tc + 1) * 1024], ALU.mult, ALU.add),
                        w=[ty, GD.get(("cw", tc)), x_ready[tc]])
                    gs("pyd_c", tres)
                    gs("pyd_r", tres)
                    x_ready[tc] = tres
                for ic in range(NIC):
                    gs(("hb", ic), GD.get("pyd_c"))
            gs(("dmaq", "G6"), GD.get("pyd_c"))

    l1f()
    A.reset()

    return _finish(nc, p, sems, x_tok, y_out, x_ready, ctx)


def _finish(nc, p, sems, x_tok, y_out, x_ready, ctx):
    last = None
    for tc in range(NTC):
        last = p.emit("gp", lambda h, tc=tc: h.dma_start(
            out=y_out[tc * 128:(tc + 1) * 128, :],
            in_=x_tok[:, tc * 1024:(tc + 1) * 1024]),
            waits=[x_ready[tc]], incs=(("S", 16),))
    p.emit("gp", None, waits=[last])
    with nc.Block() as block:
        @block.tensor
        def _(h):
            _replay(p, "te", h, sems)

        @block.vector
        def _(h):
            _replay(p, "ve", h, sems)

        @block.scalar
        def _(h):
            _replay(p, "sc", h, sems)

        @block.gpsimd
        def _(h):
            _replay(p, "gp", h, sems)

        @block.sync
        def _(h):
            _replay(p, "sy", h, sems)
    for cm in reversed(ctx):
        cm.__exit__(None, None, None)
    return nc


def _replay(p, eng, h, sems):
    for fn, waits, incs in p.q[eng]:
        for s, v in waits:
            h.wait_ge(sems[s], v)
        if fn is None:
            continue
        inst = fn(h)
        for s, n in incs:
            inst.then_inc(sems[s], n)



def _host_weights(inputs):
    f = np.float32
    W = {}
    wqkvT = np.ascontiguousarray(np.asarray(inputs["L0_attn_W_qkv"]).T.astype(f))
    W["wqkv"] = np.ascontiguousarray(
        wqkvT.reshape(8, 128, 24, 128).transpose(2, 1, 0, 3))
    W["wconv"] = np.ascontiguousarray(
        np.asarray(inputs["L0_attn_conv_w"]).astype(f).reshape(24, 128, KCONV))
    wzT = np.ascontiguousarray(np.asarray(inputs["L0_attn_W_z"]).T.astype(f))
    W["wz"] = np.ascontiguousarray(
        wzT.reshape(8, 128, 8, 128).transpose(2, 1, 0, 3))
    wba = np.zeros((48, 1024), f)
    wba[0:16] = np.asarray(inputs["L0_attn_W_b"]).astype(f)
    wba[32:48] = np.asarray(inputs["L0_attn_W_a"]).astype(f)
    W["wba"] = np.ascontiguousarray(wba.T.reshape(8, 128, 48))
    W["wout"] = np.ascontiguousarray(
        np.asarray(inputs["L0_attn_W_out"]).T.astype(f).reshape(8, 128, 1024))
    W["dtb"] = np.ascontiguousarray(
        np.asarray(inputs["L0_attn_dt_bias"]).astype(f).reshape(16, 1))
    W["alog"] = np.ascontiguousarray(
        np.asarray(inputs["L0_attn_A_log"]).astype(f).reshape(16, 1))
    W["w0f"] = np.ascontiguousarray(
        np.asarray(inputs["L0_ffn_weight"]).T.astype(f).reshape(8, 128, 1024))
    W["gam0"] = np.ascontiguousarray(
        np.asarray(inputs["L0_ffn_gamma"]).astype(f).reshape(8, 128).T)
    W["bet0"] = np.ascontiguousarray(
        np.asarray(inputs["L0_ffn_beta"]).astype(f).reshape(8, 128).T)
    for nm, key in (("wq", "L1_attn_Wq"), ("wk", "L1_attn_Wk"), ("wv", "L1_attn_Wv")):
        wT = np.ascontiguousarray(np.asarray(inputs[key]).T.astype(f))
        W[nm] = np.ascontiguousarray(
            wT.reshape(8, 128, 8, 128).transpose(2, 1, 0, 3))
    W["wo"] = np.ascontiguousarray(
        np.asarray(inputs["L1_attn_Wo"]).T.astype(f).reshape(8, 128, 1024))
    W["wr"] = np.ascontiguousarray(
        np.asarray(inputs["L1_ffn_Wr"]).astype(f).reshape(8, 128, E_MOE))
    W["wg"] = np.ascontiguousarray(
        np.asarray(inputs["L1_ffn_Wg"]).astype(f).reshape(9, 8, 128, 22, 128).transpose(0, 3, 2, 1, 4))
    W["wu"] = np.ascontiguousarray(
        np.asarray(inputs["L1_ffn_Wu"]).astype(f).reshape(9, 8, 128, 22, 128).transpose(0, 3, 2, 1, 4))
    W["wd"] = np.ascontiguousarray(
        np.asarray(inputs["L1_ffn_Wd"]).astype(f).reshape(9, 22, 128, 1024))
    W["cid"] = np.eye(128, dtype=NP_BF16)
    W["cidf"] = np.eye(128, dtype=f)
    P64 = np.zeros((64, 64), f)
    for j in range(32):
        P64[j, 2 * j + 1] = -1.0
        P64[32 + j, 2 * j] = 1.0
    cPm = np.zeros((128, 128), f)
    cPm[0:64, 0:64] = P64.T
    cPm[64:128, 64:128] = P64.T
    W["cP"] = cPm.astype(NP_BF16)
    r = np.arange(128)
    W["ctri"] = np.where(r[None, :] >= r[:, None], 0.0, NEG_BIG).astype(f)
    W["cim"] = np.eye(128, dtype=f)
    bd16 = np.zeros((16, 8 * 128), f)
    for j in range(8):
        for pp in range(128):
            bd16[2 * j + pp // 64, j * 128 + pp] = 1.0
    W["cbd16"] = bd16.astype(NP_BF16)
    W["conesf"] = np.ones((1, 128), f)
    inv = 1.0 / 10000.0 ** (np.arange(0, HD, 2, dtype=f) / HD)
    t = np.arange(T, dtype=f)
    jj = np.arange(128) % 64
    fr = inv[jj % 32]
    ang = t[None, :] * fr[:, None]
    W["ccos"] = np.cos(ang[:64]).astype(NP_BF16)
    W["csin"] = np.sin(ang[:64]).astype(NP_BF16)
    return W


def kernel(**inputs):
    X = np.ascontiguousarray(np.asarray(inputs["L0_attn_x"], dtype=np.float32))
    W = _host_weights(inputs)
    nc = build_program(None)
    in_maps = []
    for b in range(8):
        m = dict(W)
        m["x"] = np.ascontiguousarray(X[b])
        in_maps.append(m)
    res = run_bass_kernel_spmd(nc, in_maps, core_ids=list(range(8)))
    return np.stack([res.results[b]["y"] for b in range(8)], axis=0)


# version 15 (final)
# speedup vs baseline: 1.0125x; 1.0037x over previous
import numpy as np
import ml_dtypes

import concourse.bass as bass
import concourse.mybir as mybir
from concourse.bass_utils import run_bass_kernel_spmd

F32 = mybir.dt.float32
BF16 = mybir.dt.bfloat16
AF = mybir.ActivationFunctionType
ALU = mybir.AluOpType
AX = mybir.AxisListType
NP_BF16 = ml_dtypes.bfloat16

L1A_LIMIT = {}
L1F_LIMIT = {}

T = 2048
D = 1024
H = 16
HD = 64
KCONV = 4
NTC = 16
NDC = 8
E_MOE = 9
NIC = 22
MAGIC = float(np.float32(12582912.0))
NEG_BIG = -1e30
LN8 = float(np.log(0.125))

ENGS = ("te", "ve", "sc", "gp", "sy")
SEM_OF = {"te": "T", "ve": "V", "sc": "A", "gp": "G", "sy": "L"}


class Prog:
    def __init__(self):
        self.q = {e: [] for e in ENGS}
        self.cnt = {}
        self.last_wait = {e: {} for e in ENGS}
        self.floor = []

    def fence(self, sems=("T", "V", "A")):
        return [(s, self.val(s)) for s in sems if self.val(s) > 0]

    def set_floor(self, toks):
        self.floor = [t for t in toks if t is not None]

    def val(self, s):
        return self.cnt.get(s, 0)

    def emit(self, eng, fn, waits=(), incs=()):
        best = {}
        for tok in list(waits) + self.floor:
            if tok is None:
                continue
            s, v = tok
            assert v <= self.val(s), f"future wait {s}:{v} > {self.val(s)}"
            if v > best.get(s, -1):
                best[s] = v
        w = []
        for s, v in best.items():
            if self.last_wait[eng].get(s, -1) >= v:
                continue
            self.last_wait[eng][s] = v
            w.append((s, v))
        toks = []
        for s, n in incs:
            self.cnt[s] = self.val(s) + n
            toks.append((s, self.cnt[s]))
        self.q[eng].append((fn, tuple(w), tuple(incs)))
        return toks[0] if toks else None

    def te(self, fn, w=()):
        return self.emit("te", fn, w, (("T", 1),))

    def ve(self, fn, w=()):
        return self.emit("ve", fn, w, (("V", 1),))

    def sc(self, fn, w=()):
        return self.emit("sc", fn, w, (("A", 1),))

    def dma(self, eng, out_ap, in_ap, w=(), slow=False, sem=None):
        sem = sem or SEM_OF[eng]
        return self.emit(eng, lambda h, o=out_ap, i=in_ap, sl=slow: h.dma_start(
            out=o, in_=i, allow_slow_non_contiguous=sl),
            w, ((sem, 16),))

    def all(self, sem):
        return (sem, self.val(sem))


class Arena:
    def __init__(self, t, nbytes):
        self.t = t
        self.nbytes = nbytes
        self.off = 0
        self.marks = []
        self.hwm = 0

    def alloc(self, shape, dtype, name=""):
        pdim, nf = shape
        esz = 2 if dtype == BF16 else 4
        nb = nf * esz
        off = (self.off + 63) & ~63
        assert off + nb <= self.nbytes, f"arena OOM {name}: {off}+{nb}>{self.nbytes}"
        self.off = off + nb
        self.hwm = max(self.hwm, self.off)
        if name:
            self.offsets = getattr(self, "offsets", {})
            self.offsets.setdefault(name, (off, nb))
        f32len = (nb + 3) // 4
        ap = self.t[:, off // 4: off // 4 + f32len]
        if dtype == BF16:
            ap = ap.bitcast(BF16)
        return ap[:pdim, :nf]

    def mark(self):
        self.marks.append(self.off)

    def reset(self):
        self.off = self.marks.pop()


def bc64(col_ap):
    k = col_ap.shape[1]
    return col_ap.unsqueeze(2).broadcast_to((col_ap.shape[0], k, 64))


def r3(ap2d, k=16):
    return ap2d.rearrange("p (a b) -> p a b", a=k)


def build_program(tap=None, dbg=None, nocc=False):
    nc = bass.Bass("TRN2", target_bir_lowering=False)
    dr = {}

    def din(name, shape, dtype=F32):
        dr[name] = nc.dram_tensor(name, list(shape), dtype, kind="ExternalInput").ap()
        return dr[name]

    x_in = din("x", (T, D))
    y_out = nc.dram_tensor("y", [T, D], F32, kind="ExternalOutput").ap()
    dbg_out = (nc.dram_tensor("dbg", [128, 16384], F32, kind="ExternalOutput").ap()
               if dbg is not None else None)
    din("wqkv", (24, 128, NDC, 128))
    din("wconv", (24, 128, KCONV))
    din("wz", (NDC, 128, NDC, 128))
    din("wba", (NDC, 128, 48))
    din("wout", (NDC, 128, D))
    din("dtb", (16, 1))
    din("alog", (16, 1))
    din("w0f", (NDC, 128, D))
    din("gam0", (128, NDC))
    din("bet0", (128, NDC))
    din("wq", (NDC, 128, NDC, 128))
    din("wk", (NDC, 128, NDC, 128))
    din("wv", (NDC, 128, NDC, 128))
    din("wo", (NDC, 128, D))
    din("wr", (NDC, 128, E_MOE))
    din("wg", (E_MOE, NIC, 128, NDC, 128))
    din("wu", (E_MOE, NIC, 128, NDC, 128))
    din("wd", (E_MOE, NIC, 128, D))
    din("cid", (128, 128), BF16)
    din("cidf", (128, 128), F32)
    din("cP", (128, 128), BF16)
    din("ctri", (128, 128), F32)
    din("cim", (128, 128), F32)
    din("cbd16", (16, 1024), BF16)
    din("conesf", (1, 128), F32)
    din("ccos", (128, T), BF16)
    din("csin", (128, T), BF16)
    cc_in = nc.dram_tensor("cc_in", [1, 8], F32).ap()
    cc_out = nc.dram_tensor("cc_out", [1, 8], F32).ap()

    p = Prog()
    ctx = []

    def sbuf(name, shape, dtype):
        cm = nc.sbuf_tensor(name, list(shape), dtype)
        t = cm.__enter__()
        ctx.append(cm)
        return t

    arena_t = sbuf("arena", [128, 51904], F32)
    A = Arena(arena_t, 51904 * 4)
    build_program.arena = A
    x_tok = A.alloc([128, 16 * 1024], F32, "x_tok")

    cid = sbuf("sb_cid", [128, 128], BF16)
    cidf = sbuf("sb_cidf", [128, 128], F32)
    cP = sbuf("sb_cP", [128, 128], BF16)
    ctri = sbuf("sb_ctri", [128, 128], F32)
    cim = sbuf("sb_cim", [128, 128], F32)
    cbd16 = sbuf("sb_cbd16", [16, 1024], BF16)
    onesf = sbuf("sb_onesf", [1, 128], F32)
    dtb = sbuf("sb_dtb", [16, 1], F32)
    negea = sbuf("negea", [16, 1], F32)
    smalls = sbuf("smalls", [128, 16], F32)
    st6 = sbuf("st6", [128, 12], F32)
    mv = sbuf("mv", [128, 4], F32)
    cvals = sbuf("cvals", [128, 4], F32)
    nc.const_aps.register(F32, 1e-5, cvals[:, 0:1])
    nc.const_aps.register(F32, 1e-6, cvals[:, 1:2])
    nc.const_aps.register(F32, LN8, cvals[:, 2:3])

    cmA = nc.psum_tensor("psA", [128, 2048], F32)
    pA = cmA.__enter__(); ctx.append(cmA)
    cmB = nc.psum_tensor("psB", [128, 2048], F32)
    pB = cmB.__enter__(); ctx.append(cmB)

    sems = {}
    for s in ("L", "G", "T", "V", "A", "S", "C", "Lx", "Lc", "Lc2", "Lg", "Lr",
              "Lq", "Lo", "Lf", "G0", "G1", "G2", "G3", "G4", "G5", "G6", "G7",
              "Gw", "Go", "Go2", "Gr", "Gcs", "Gd"):
        cm = nc.semaphore(s)
        sems[s] = cm.__enter__()
        ctx.append(cm)

    GD = {}

    def gw(*keys):
        return [GD.get(k) for k in keys]

    def gs(key, tok):
        GD[key] = tok

    def maybe_dbg(label, ap, deps):
        if dbg != label:
            return None
        pdim, fdim = ap.shape[0], int(np.prod(ap.shape[1:]))
        return p.dma("gp", dbg_out[0:pdim, 0:fdim], ap, w=list(deps), sem="Gd")

    tk_cv = None
    for i, v in ((0, 1e-5), (1, 1e-6), (2, LN8)):
        tk_cv = p.ve(lambda h, i=i, v=v: h.memset(cvals[:, i:i + 1], v))
    x_ready = []
    for tc in range(NTC):
        p.dma("sy", x_tok[:, tc * 1024:(tc + 1) * 1024],
              x_in[tc * 128:(tc + 1) * 128, :], sem="Lx")
    x_ready = [p.all("Lx")] * NTC
    for sb_t, name in ((cid, "cid"), (cidf, "cidf"), (cP, "cP"), (ctri, "ctri"),
                       (cim, "cim"), (cbd16, "cbd16"), (onesf, "conesf"),
                       (dtb, "dtb")):
        p.dma("sy", sb_t[:, :], dr[name], sem="Lc")
    p.dma("sy", negea[:, :], dr["alog"], sem="Lc")
    tcons = p.all("Lc")
    tk_al = tcons
    tk_nega = p.sc(lambda h: h.activation(negea[:, :], negea[:, :], AF.Exp,
                                          bias=0.0, scale=1.0), w=[tk_al])
    tk_nega = p.ve(lambda h: h.tensor_scalar_mul(negea[:, :], negea[:, :], -1.0),
                   w=[tk_nega])

    def ln_tile(tc, dep, out_ap, extra=None):
        xsl = x_tok[:, tc * 1024:(tc + 1) * 1024]
        ex = extra or []
        t1 = p.ve(lambda h: h.tensor_scalar(out_ap, xsl, 1.0, 0.0, ALU.mult,
                                            ALU.add, accum_out=mv[:, 0:1]),
                  w=[dep, GD.get("lnstat")] + ex)
        t2 = p.ve(lambda h: h.scalar_tensor_tensor(out_ap, xsl, 1.0, xsl,
                                                   ALU.mult, ALU.mult,
                                                   accum_out=mv[:, 1:2]), w=[t1])
        t3 = p.ve(lambda h: h.tensor_scalar_mul(mv[:, 0:1], mv[:, 0:1],
                                                1.0 / 1024.0), w=[t2])
        t4 = p.ve(lambda h: h.tensor_scalar_mul(mv[:, 1:2], mv[:, 1:2],
                                                1.0 / 1024.0), w=[t3])
        t5 = p.ve(lambda h: h.tensor_scalar(smalls[:, 2:3], mv[:, 0:1], mv[:, 0:1],
                                            None, ALU.mult), w=[t4])
        t6 = p.ve(lambda h: h.tensor_sub(mv[:, 1:2], mv[:, 1:2], smalls[:, 2:3]),
                  w=[t5])
        t7 = p.sc(lambda h: h.activation(smalls[:, 1:2], mv[:, 1:2], AF.Sqrt,
                                         bias=1e-5, scale=1.0), w=[t6, tk_cv])
        t8 = p.ve(lambda h: h.reciprocal(smalls[:, 1:2], smalls[:, 1:2]), w=[t7])
        t9 = p.ve(lambda h: h.tensor_scalar(out_ap, xsl, mv[:, 0:1], smalls[:, 1:2],
                                            ALU.subtract, ALU.mult), w=[t8])
        gs("lnstat", t9)
        return t9

    def ln_to_xnd(tcs, xnd, lnscr, pbank_bf):
        toks = []
        xnd_re = xnd.rearrange("p (dc t) -> p dc t", dc=NDC)
        for i, tc in enumerate(tcs):
            sl = i % 2
            scr = lnscr[:, sl * 1024:(sl + 1) * 1024]
            tln = ln_tile(tc, x_ready[tc], scr, extra=gw(("lnscr", sl)))
            tt = None
            for dc in range(NDC):
                tt = p.te(lambda h, dc=dc, scr=scr: h.transpose(
                    pbank_bf[:, dc * 128:(dc + 1) * 128],
                    scr[:, dc * 128:(dc + 1) * 128], cid[:, :]),
                    w=[tln, tcons, GD.get("lnpsum")])
            gs(("lnscr", sl), tt)
            tev = p.ve(lambda h, i=i: h.tensor_copy(
                xnd_re[:, :, i * 128:(i + 1) * 128],
                pbank_bf[:, :].rearrange("p (dc t) -> p dc t", dc=NDC)), w=[tt])
            gs("lnpsum", tev)
            toks.append(tev)
        return toks

    A.mark()
    z_d = A.alloc([128, 8 * 1024], BF16, "z_d")
    O_ss = A.alloc([128, 8 * 16], F32, "Oss")
    S_sb = A.alloc([128, 8 * 64], F32, "S")
    S_bf = A.alloc([128, 8 * 64], BF16, "Sbf")
    G = A.alloc([16, 1024], F32, "G")
    carry = A.alloc([16, 8], F32, "carry")
    Gb = A.alloc([16, 8], F32, "Gb")
    Gbp = A.alloc([16, 8], F32, "Gbp")
    gblog = A.alloc([16, 8], F32, "gblog")
    glogT = A.alloc([1, 128], F32, "glogT")
    gbT = A.alloc([8, 16], F32, "gbT")
    ccar = A.alloc([128, 24 * 3], BF16, "ccar")
    O_raw = A.alloc([128, 8 * 1024], BF16, "Oraw")
    wconv_sb = A.alloc([128, 24 * KCONV], F32, "wconv")
    sqs = A.alloc([128, 1024], F32, "sqs")
    gcols = A.alloc([128, 16], F32, "gcols")
    q_d = A.alloc([128, 8 * 1024], BF16, "q_d")
    k_d = A.alloc([128, 8 * 1024], BF16, "k_d")
    v_d = A.alloc([128, 8 * 1024], BF16, "v_d")
    rows_st = A.alloc([128, 1024], F32, "rows")
    Grq = A.alloc([16, 1024], F32, "Grq")
    rkc = A.alloc([128, 2 * 16], F32, "rkc")
    mcolB = A.alloc([128, 2 * 16], F32, "mcol")
    colsB = A.alloc([128, 2 * 128], F32, "cols")
    lncB = A.alloc([128, 16], F32, "lnc")

    tk_car0 = p.ve(lambda h: h.memset(carry[:, :], 0.0))
    tk_rows0 = p.ve(lambda h: h.memset(rows_st[:, :], 0.0))
    tk_S0 = p.ve(lambda h: h.memset(S_sb[:, :], 0.0))
    tk_Sb0 = p.ve(lambda h: h.memset(S_bf[:, :], 0.0))
    tk_wconv = None
    tk_wout = [None] * NDC

    def l0a():
        nonlocal tk_wconv
        sil = {}
        fprev = []
        for hf in range(2):
            tcs = list(range(hf * 8, hf * 8 + 8))
            p.set_floor(fprev)
            A.mark()
            xnd = A.alloc([128, NDC * 1024], BF16, "l0a_xnd")
            xnd_re = xnd.rearrange("p (dc t) -> p dc t", dc=NDC)
            lnscr = A.alloc([128, 2 * 1024], BF16, "l0a_lnscr")
            wqs = A.alloc([128, 2 * NDC * 128], BF16, "l0a_wq")
            wba_sb = A.alloc([128, NDC * 48], BF16, "l0a_wba")
            mixed = A.alloc([128, 2 * 1027], BF16, "l0a_mx")
            diag = A.alloc([128, KCONV * 128], BF16, "l0a_dg")
            lnrq = A.alloc([16, 1024], F32, "l0a_lnrq")
            zrow = lnrq
            bdarg = Grq
            qts = lnscr[:, 0:1024]
            tk_zero = p.ve(lambda h: h.memset(lnrq[:, :], 0.0),
                           w=gw("lnrq_c"))
            if hf == 0:
                tk_wconv = p.dma("sy", wconv_sb[:, :],
                                 dr["wconv"].transpose([1, 0, 2]), sem="Lc2")
            tk_wba = p.dma("gp", wba_sb[:, :], dr["wba"].transpose([1, 0, 2]),
                           w=gw(("dmaq", "Gw")), sem="Gw")
            tmx0 = p.ve(lambda h: h.memset(mixed[:, :], 0.0), w=gw("mixed0", "mixed1"))

            pbf = pA[:, 0:512].bitcast(BF16)
            ln_toks = ln_to_xnd(tcs, xnd, lnscr, pbf)
            xnd_all = ln_toks[-1]
            if hf == 0:
                maybe_dbg("xn0", xnd[:, :], [xnd_all])

            pba = pB[0:48, 0:1024]
            tmm = None
            for dc in range(NDC):
                for s2 in range(2):
                    tmm = p.te(lambda h, dc=dc, s2=s2: h.matmul(
                        pba[:, s2 * 512:(s2 + 1) * 512],
                        wba_sb[:, dc * 48:(dc + 1) * 48],
                        xnd_re[:, dc, s2 * 512:(s2 + 1) * 512],
                        start=(dc == 0), stop=(dc == NDC - 1)),
                        w=[tk_wba, ln_toks[-1], GD.get("pB01"),
                           GD.get(("dS", 0)), GD.get(("dS", 64)),
                           GD.get("pB67c")])
            tbeta = p.sc(lambda h: h.activation(rows_st[0:16, :], pba[0:16, :],
                                                AF.Sigmoid, bias=0.0, scale=1.0),
                         w=[tmm, GD.get("rows_c")])
            su = p.sc(lambda h: h.activation(G[:, :], pba[32:48, :], AF.Abs,
                                             bias=dtb[:, 0:1], scale=1.0),
                      w=[tmm, tcons, GD.get("G_c")])
            se = p.sc(lambda h: h.activation(G[:, :], G[:, :], AF.Exp,
                                             bias=0.0, scale=-1.0), w=[su])
            slg = p.sc(lambda h: h.activation(G[:, :], G[:, :], AF.Ln,
                                              bias=1.0, scale=1.0), w=[se])
            sr = p.sc(lambda h: h.activation(bdarg[:, :], pba[32:48, :], AF.Relu,
                                             bias=dtb[:, 0:1], scale=1.0), w=[tmm])
            gs("pB01", sr)
            gs(("dmaq", "Gw"), tmm)
            if hf == 0 and dbg == "balog":
                tdbg = p.ve(lambda h: h.tensor_copy(sqs[:, :], pba[0:48, :][0:48, :]),
                            w=[tmm]) if False else None
                tdbg = p.ve(lambda h: h.tensor_copy(sqs[0:48, :], pba[0:48, :]),
                            w=[tmm])
                maybe_dbg("balog", sqs[0:48, :], [tdbg])
            tsp = p.ve(lambda h: h.tensor_add(G[:, :], G[:, :], bdarg[:, :]),
                       w=[slg, sr])
            tg = p.ve(lambda h: h.tensor_scalar_mul(G[:, :], G[:, :], negea[:, 0:1]),
                      w=[tsp, tk_nega])
            tG = p.ve(lambda h: h.tensor_tensor_scan(G[:, :], zrow[:, :], G[:, :],
                                                     carry[:, 0:1], ALU.add, ALU.add),
                      w=[tg, tk_zero, tk_car0])
            if hf == 0:
                maybe_dbg("g0", G[:, :], [tG])
                maybe_dbg("beta0", rows_st[0:16, :], [tbeta])
            tGb = p.ve(lambda h: h.tensor_copy(Gb[:, :], G[:, 127::128]), w=[tG])
            tp1 = p.ve(lambda h: h.tensor_copy(Gbp[:, 0:1], carry[:, 0:1]), w=[tGb])
            tp2 = p.ve(lambda h: h.tensor_copy(Gbp[:, 1:8], Gb[:, 0:7]), w=[tGb])
            tcar = p.ve(lambda h: h.tensor_copy(carry[:, 0:1], G[:, 1023:1024]),
                        w=[tG, tp1])
            tgl = p.ve(lambda h: h.tensor_sub(gblog[:, :], Gb[:, :], Gbp[:, :]),
                       w=[tGb, tp1, tp2])
            tg1 = p.te(lambda h: h.transpose(pB[0:8, 1024:1040], gblog[:, :],
                                             cidf[0:16, 0:16]),
                       w=[tgl, GD.get("pB45")])
            tg2 = p.ve(lambda h: h.tensor_copy(gbT[:, :], pB[0:8, 1024:1040]),
                       w=[tg1, GD.get("gbT_c")])
            gs("pB45", tg2)
            gs("pB67b", tg2)
            tglT = p.dma("sy", glogT[0:1, :].rearrange("o (c hh) -> o c hh", c=8),
                         gbT[:, :], w=[tg2, GD.get(("dmaq", "Lg"))], slow=True,
                         sem="Lg")
            gs("gbT_c", tglT)
            tneg = None
            for c in range(8):
                tneg = p.ve(lambda h, c=c: h.tensor_scalar(
                    G[:, c * 128:(c + 1) * 128], G[:, c * 128:(c + 1) * 128],
                    Gbp[:, c:c + 1], -1.0, ALU.subtract, ALU.mult), w=[tgl, tcar])
            tbd = None
            for c in range(8):
                tbd = p.ve(lambda h, c=c: h.tensor_scalar(
                    rows_st[32:48, c * 128:(c + 1) * 128],
                    G[:, c * 128:(c + 1) * 128],
                    gblog[:, c:c + 1], None, ALU.add), w=[tneg, GD.get("rows_c")])
            tbd = p.sc(lambda h: h.activation(rows_st[32:48, :], rows_st[32:48, :],
                                              AF.Exp, bias=0.0, scale=1.0), w=[tbd])
            tng = p.ve(lambda h: h.tensor_copy(rows_st[64:80, :], G[:, :]), w=[tneg])

            pmxL = [pA[:, 1024:2048], pA[:, 0:1024]]
            pcvL = [pB[:, 0:1024], pB[:, 1024:2048]]
            for cc in range(32):
                sl = cc % 2
                pmx = pmxL[sl]
                pcv = pcvL[sl]
                is_z = cc >= 24
                wsrc = dr["wz"][cc - 24] if is_z else dr["wqkv"][cc]
                wdst = wqs[:, sl * NDC * 128:(sl + 1) * NDC * 128]
                gsem = "G0" if sl == 0 else "G1"
                twl = p.dma("gp", wdst, wsrc.rearrange("p dc c -> p (dc c)"),
                            w=gw(("dmaq", gsem)), sem=gsem)
                tm = None
                for dc in range(NDC):
                    for s2 in range(2):
                        tm = p.te(lambda h, dc=dc, s2=s2, wdst=wdst, pmx=pmx:
                                  h.matmul(
                            pmx[:, s2 * 512:(s2 + 1) * 512],
                            wdst[:, dc * 128:(dc + 1) * 128],
                            xnd_re[:, dc, s2 * 512:(s2 + 1) * 512],
                            start=(dc == 0), stop=(dc == NDC - 1)),
                            w=[twl, xnd_all, GD.get(("pAm", sl))])
                gs(("dmaq", gsem), tm)
                if is_z:
                    dst = z_d[:, (cc - 24) * 1024:(cc - 23) * 1024]
                    tsl = p.sc(lambda h, dst=dst, pmx=pmx: h.activation(
                        dst, pmx[:, :], AF.Silu, bias=0.0, scale=1.0),
                        w=[tm, GD.get(("zd", cc - 24))])
                    gs(("pAm", sl), tsl)
                    sil[("z", cc - 24, hf)] = tsl
                    continue
                msl = mixed[:, sl * 1027:(sl + 1) * 1027]
                tev = p.sc(lambda h, msl=msl, pmx=pmx: h.copy(msl[:, 3:1027],
                                                              pmx[:, :]),
                           w=[tm, tmx0, GD.get(("mix", sl)), GD.get(("mixv", sl))])
                gs(("pAm", sl), tev)
                if hf == 1:
                    tcc2 = p.ve(lambda h, msl=msl, cc=cc: h.tensor_copy(
                        msl[:, 0:3], ccar[:, cc * 3:cc * 3 + 3]),
                        w=[sil.get(("ccar", cc)), tev, GD.get(("mix", sl))])
                else:
                    tcc2 = tev
                tsv = p.ve(lambda h, msl=msl, cc=cc: h.tensor_copy(
                    ccar[:, cc * 3:cc * 3 + 3], msl[:, 1024:1027]),
                    w=[tev, tcc2])
                gs(("mixv", sl), tsv)
                if hf == 0 and cc == 0:
                    maybe_dbg("mx0", msl[:, 0:1027], [tev, tcc2])
                sil[("ccar", cc)] = tsv
                td = None
                for j in range(KCONV):
                    td = p.ve(lambda h, j=j, cc=cc: h.tensor_scalar_mul(
                        diag[:, j * 128:(j + 1) * 128], cim[:, :],
                        wconv_sb[:, cc * KCONV + j:cc * KCONV + j + 1]),
                        w=[tk_wconv, tcons, GD.get("diag")])
                tcv = None
                for j in range(KCONV):
                    for s2 in range(2):
                        tcv = p.te(lambda h, j=j, s2=s2, msl=msl, pcv=pcv: h.matmul(
                            pcv[:, s2 * 512:(s2 + 1) * 512],
                            diag[:, j * 128:(j + 1) * 128],
                            msl[:, j + s2 * 512: j + s2 * 512 + 512],
                            start=(j == 0), stop=(j == KCONV - 1)),
                            w=[td, tcc2, tsv, GD.get(("pBc", sl)),
                               GD.get("gbT_c"), GD.get("pB45")])
                gs("diag", tcv)
                gs(("mix", sl), tcv)
                dst = (q_d if cc < 8 else k_d if cc < 16 else v_d)[
                    :, (cc % 8) * 1024:(cc % 8 + 1) * 1024]
                tsl = p.sc(lambda h, dst=dst, pcv=pcv: h.activation(
                    dst, pcv[:, :], AF.Silu, bias=0.0, scale=1.0),
                    w=[tcv, GD.get(("qkv", cc))])
                gs(("pBc", sl), tsl)
                sil[(cc, hf)] = tsl

            if hf == 0:
                maybe_dbg("q0", q_d[:, :], [sil[(qc2, 0)] for qc2 in range(8)])
                maybe_dbg("k0", k_d[:, :], [sil[(8 + qc2, 0)] for qc2 in range(8)])
                maybe_dbg("v0", v_d[:, :], [sil[(16 + qc2, 0)] for qc2 in range(8)])
                maybe_dbg("z0", z_d[:, :], [sil[("z", zc2, 0)] for zc2 in range(8)])
            gs("lnpsum", GD.get(("pAm", 1)))
            gs("pB45", GD.get(("pBc", 1)))
            gs("pB67b", GD.get(("pBc", 1)))
            gs("pB67a", GD.get(("pBc", 1)))
            gs("pB67c", GD.get(("pBc", 1)))
            gs("pB01", GD.get(("pBc", 0)))
            gs("pA23", GD.get(("pAm", 0)))
            gs("pA0r", GD.get(("pAm", 1)))
            gs("pA1r", GD.get(("pAm", 1)))
            gs("pOr", GD.get(("pAm", 0)))
            gs(("dS", 0), GD.get(("pBc", 0)))
            gs(("dS", 64), GD.get(("pBc", 0)))
            pqt = pA[:, 0:512].bitcast(BF16)
            plr = pB[0:16, 1024:1152]
            tprev = None
            for i in range(8):
                tt = None
                for qc in range(NDC):
                    tt = p.te(lambda h, qc=qc, i=i: h.transpose(
                        pqt[:, qc * 128:(qc + 1) * 128],
                        q_d[:, qc * 1024 + i * 128: qc * 1024 + (i + 1) * 128],
                        cid[:, :]), w=[sil[(qc, hf)], GD.get("lnpsum")])
                t1 = p.ve(lambda h: h.tensor_copy(qts[:, :], pqt[:, :]),
                          w=[tt, tprev, GD.get(("lnscr", 0)), GD.get(("lnscr", 1))])
                gs("lnpsum", t1)
                t2 = p.ve(lambda h: h.tensor_mul(sqs[:, :], qts[:, :], qts[:, :]),
                          w=[t1, GD.get("sqs_c")])
                t3 = p.ve(lambda h: h.tensor_reduce(
                    lncB[:, 0:16], sqs[:, :].rearrange("p (hh d) -> p hh d", d=64),
                    AX.X, ALU.add), w=[t2, GD.get("lnc_c")])
                gs("sqs_c", t3)
                t4 = p.sc(lambda h: h.activation(lncB[:, 0:16], lncB[:, 0:16], AF.Ln,
                                                 bias=1e-6, scale=1.0), w=[t3])
                t5 = p.ve(lambda h: h.tensor_scalar(lncB[:, 0:16], lncB[:, 0:16],
                                                    -0.5, LN8, ALU.mult, ALU.add),
                          w=[t4])
                t6 = p.te(lambda h: h.transpose(plr[:, 0:128], lncB[:, 0:16],
                                                cidf[:, :]), w=[t5, GD.get("pB45")])
                gs("lnc_c", t6)
                tprev = p.ve(lambda h, i=i: h.tensor_copy(
                    lnrq[:, i * 128:(i + 1) * 128], plr[0:16, 0:128]), w=[t6])
                gs("pB45", tprev)
            tgrq = p.ve(lambda h: h.tensor_sub(Grq[:, :], lnrq[:, :], G[:, :]),
                        w=[tprev, tng, GD.get("Grq_c")])
            gs("lnrq_c", tgrq)
            if hf == 0:
                maybe_dbg("lnrq0", lnrq[:, :], [tprev])
                maybe_dbg("negG0", G[:, :], [tng])
                maybe_dbg("grq0", Grq[:, :], [tgrq])
            tgrq = p.ve(lambda h: h.tensor_copy(rows_st[96:112, :], Grq[:, :]),
                        w=[tgrq])
            gs("rows_w", tgrq)

            fA = p.fence()
            A.reset()
            p.set_floor(fA)

            A.mark()
            wout_sb = A.alloc([128, NDC * 1024], BF16, "l0a_wout")
            for dc in range(NDC):
                p.dma("gp", wout_sb[:, dc * 1024:(dc + 1) * 1024],
                      dr["wout"][dc], sem="Go")
            woutall = p.all("Go")
            for dc in range(NDC):
                tk_wout[dc] = woutall

            A.mark()
            ktok = A.alloc([128, 2 * 1024], BF16, "l0a_ktok")
            vbet = A.alloc([128, 2 * 1024], BF16, "l0a_vb")
            ktil = A.alloc([128, 2 * 1024], BF16, "l0a_ktil")
            Msk = A.alloc([128, 128], F32, "l0a_Msk")
            Ebf = A.alloc([128, 128], BF16, "l0a_E")
            Abf = A.alloc([128, 128], BF16, "l0a_A")
            qtil = A.alloc([128, 128], BF16, "l0a_qt")
            abuf = A.alloc([128, 128], F32, "l0a_ab")

            pAt = pA[:, 0:512]
            pMb = pA[:, 512:1024]
            pO = pA[:, 1024:2048]
            pDS = pB[:, 0:512]
            pGcn = pB[:, 512:1024]
            pKT = pB[:, 1024:1536].bitcast(BF16)
            for c in range(8):
                sl = c % 2
                colss = colsB[:, sl * 128:(sl + 1) * 128]
                tcm = p.te(lambda h, c=c: h.transpose(
                    pGcn[:, 0:128], rows_st[:, c * 128:(c + 1) * 128],
                    cidf[:, :]),
                    w=[tbeta, tbd, tng, tgrq, tk_rows0, GD.get("pB67a")])
                tcol = p.ve(lambda h, colss=colss: h.tensor_copy(colss,
                                                                 pGcn[:, 0:128]),
                            w=[tcm, GD.get(("cols", sl))])
                gs("pB67a", tcol)
                ksl = ktok[:, sl * 1024:(sl + 1) * 1024]
                vbsl = vbet[:, sl * 1024:(sl + 1) * 1024]
                ktsl = ktil[:, sl * 1024:(sl + 1) * 1024]
                tt = None
                for qc in range(NDC):
                    tt = p.te(lambda h, qc=qc, c=c: h.transpose(
                        pKT[:, qc * 128:(qc + 1) * 128],
                        k_d[:, qc * 1024 + c * 128: qc * 1024 + (c + 1) * 128],
                        cid[:, :]), w=[sil[(8 + qc, hf)], GD.get("pB67b")])
                tke = p.ve(lambda h, ksl=ksl: h.tensor_copy(ksl, pKT[:, :]),
                           w=[tt, GD.get(("ktok", sl))])
                gs("pB67b", tke)
                tt = None
                for qc in range(NDC):
                    tt = p.te(lambda h, qc=qc, c=c: h.transpose(
                        pKT[:, qc * 128:(qc + 1) * 128],
                        v_d[:, qc * 1024 + c * 128: qc * 1024 + (c + 1) * 128],
                        cid[:, :]), w=[sil[(16 + qc, hf)], tke])
                tve = p.ve(lambda h, vbsl=vbsl, colss=colss: h.tensor_tensor(
                    r3(vbsl), r3(pKT[:, :]), bc64(colss[:, 0:16]), ALU.mult),
                    w=[tt, tcol, GD.get(("vb", sl))])
                gs("pB67b", tve)
                t2 = p.ve(lambda h, ksl=ksl: h.tensor_mul(sqs[:, :], ksl, ksl),
                          w=[tke, GD.get("sqs_c")])
                t3 = p.ve(lambda h, sl=sl: h.tensor_reduce(
                    rkc[:, sl * 16:(sl + 1) * 16],
                    sqs[:, :].rearrange("p (hh d) -> p hh d", d=64), AX.X, ALU.add),
                    w=[t2, GD.get(("rkc", sl))])
                gs("sqs_c", t3)
                t4 = p.sc(lambda h, sl=sl: h.activation(
                    rkc[:, sl * 16:(sl + 1) * 16], rkc[:, sl * 16:(sl + 1) * 16],
                    AF.Sqrt, bias=1e-6, scale=1.0), w=[t3])
                trk = p.ve(lambda h, sl=sl: h.reciprocal(
                    rkc[:, sl * 16:(sl + 1) * 16], rkc[:, sl * 16:(sl + 1) * 16]),
                    w=[t4])
                tm1 = p.ve(lambda h, sl=sl, colss=colss: h.tensor_mul(
                    mcolB[:, sl * 16:(sl + 1) * 16], rkc[:, sl * 16:(sl + 1) * 16],
                    colss[:, 32:48]), w=[trk, tcol, GD.get(("mcol", sl))])
                tkt = p.ve(lambda h, ktsl=ktsl, ksl=ksl, sl=sl: h.tensor_tensor(
                    r3(ktsl), r3(ksl), bc64(mcolB[:, sl * 16:(sl + 1) * 16]),
                    ALU.mult), w=[tm1, GD.get(("ktil", sl))])
                if hf == 0 and c == 0:
                    maybe_dbg("cols0", colss, [tcol])
                    maybe_dbg("ktok0", ksl, [tke])
                    maybe_dbg("vb0", vbsl, [tve])
                    maybe_dbg("ktil0", ktsl, [tkt])
                    maybe_dbg("rkc0", rkc[:, 0:16], [trk])
                tgc1 = p.te(lambda h, c=c: h.matmul(
                    pGcn[:, 128:144], onesf[0:1, 0:128], glogT[0:1, c * 16:(c + 1) * 16],
                    start=True, stop=True), w=[tglT, GD.get("pB67c")])
                tgc = p.sc(lambda h: h.activation(gcols[:, :], pGcn[:, 128:144],
                                                  AF.Exp, bias=0.0, scale=1.0),
                           w=[tgc1, GD.get("gcols_c")])
                gs("pB67c", tgc)
                gs(("dmaq", "Lg"), tgc1)

                oh = []
                for hh in range(16):
                    qc = hh // 2
                    pb0 = (hh % 2) * 64
                    ksrc = k_d[pb0:pb0 + 64,
                               qc * 1024 + c * 128: qc * 1024 + (c + 1) * 128]
                    qsrc = q_d[pb0:pb0 + 64,
                               qc * 1024 + c * 128: qc * 1024 + (c + 1) * 128]
                    tA = p.te(lambda h, ksrc=ksrc, qsrc=qsrc: h.matmul(
                        pAt[:, 0:128], ksrc, qsrc, start=True, stop=True),
                        w=[sil[(qc, hf)], sil[(8 + qc, hf)], GD.get("pA0r")])
                    tMb = p.te(lambda h, hh=hh, colss=colss: h.transpose(
                        pMb[:, 0:128],
                        colss[:, 96 + hh:97 + hh].broadcast_to((128, 128)),
                        cidf[:, :]), w=[tcol, GD.get("pA1r")])
                    tms = p.ve(lambda h: h.tensor_add(Msk[:, :], pMb[:, 0:128],
                                                      ctri[:, :]),
                               w=[tMb, GD.get("Msk_c")])
                    tE = p.sc(lambda h, hh=hh, colss=colss: h.activation(
                        Ebf[:, :], Msk[:, :], AF.Exp,
                        bias=colss[:, 64 + hh:65 + hh], scale=1.0),
                        w=[tms, GD.get("E_c")])
                    gs("Msk_c", tE)
                    tab = p.sc(lambda h, pb0=pb0: h.activation(
                        abuf[pb0:pb0 + 64, :], pMb[pb0:pb0 + 64, 0:128], AF.Exp,
                        bias=LN8, scale=1.0), w=[tMb, tk_cv, GD.get("ab_c")])
                    gs("pA1r", tab)
                    tAt = p.ve(lambda h, sl=sl, hh=hh: h.scalar_tensor_tensor(
                        Abf[:, :], pAt[:, 0:128],
                        rkc[:, sl * 16 + hh:sl * 16 + hh + 1], Ebf[:, :],
                        ALU.mult, ALU.mult), w=[tA, tE, trk, GD.get("Abf_c")])
                    gs("pA0r", tAt)
                    gs("E_c", tAt)
                    tqt = p.ve(lambda h, qsrc=qsrc, pb0=pb0: h.tensor_tensor(
                        qtil[pb0:pb0 + 64, :], qsrc, abuf[pb0:pb0 + 64, :], ALU.mult),
                        w=[tab, GD.get("qtil_c")])
                    gs("ab_c", tqt)
                    if hf == 0 and c == 0 and hh == 0:
                        maybe_dbg("e00", Ebf[:, :], [tE])
                        maybe_dbg("abf00", Abf[:, :], [tAt])
                        maybe_dbg("qt00", qtil[0:64, :], [tqt])
                        maybe_dbg("msk00", Msk[:, :], [tms])
                    to1 = p.te(lambda h, hh=hh, pb0=pb0: h.matmul(
                        pO[:, hh * 64:(hh + 1) * 64], qtil[pb0:pb0 + 64, :],
                        S_bf[pb0:pb0 + 64, (hh // 2) * 64:(hh // 2 + 1) * 64],
                        start=True, stop=False),
                        w=[tqt, GD.get(("Sbf", hh)) or tk_Sb0, GD.get("pOr")])
                    to2 = p.te(lambda h, hh=hh, ktsl=ktsl, vbsl=vbsl: h.matmul(
                        pO[:, hh * 64:(hh + 1) * 64], Abf[:, :],
                        vbsl[:, hh * 64:(hh + 1) * 64], start=False, stop=True),
                        w=[tAt, tve])
                    gs("qtil_c", to1)
                    gs("Abf_c", to2)
                    oh.append(to2)
                    tds = p.te(lambda h, hh=hh, pb0=pb0, ktsl=ktsl, vbsl=vbsl:
                               h.matmul(pDS[pb0:pb0 + 64, 0:64],
                                        ktsl[:, hh * 64:(hh + 1) * 64],
                                        vbsl[:, hh * 64:(hh + 1) * 64],
                                        start=True, stop=True,
                                        tile_position=(0, pb0)),
                               w=[tkt, tve, GD.get(("dS", pb0))])
                    tS = p.ve(lambda h, hh=hh, pb0=pb0: h.scalar_tensor_tensor(
                        S_sb[pb0:pb0 + 64, (hh // 2) * 64:(hh // 2 + 1) * 64],
                        S_sb[pb0:pb0 + 64, (hh // 2) * 64:(hh // 2 + 1) * 64],
                        gcols[pb0:pb0 + 64, hh:hh + 1],
                        pDS[pb0:pb0 + 64, 0:64], ALU.mult, ALU.add),
                        w=[tds, tgc, to1, GD.get(("S", hh)) or tk_S0])
                    gs(("dS", pb0), tS)
                    gs(("S", hh), tS)
                    gs("gcols_c", tS)
                    tSb = p.ve(lambda h, hh=hh, pb0=pb0: h.tensor_copy(
                        S_bf[pb0:pb0 + 64, (hh // 2) * 64:(hh // 2 + 1) * 64],
                        S_sb[pb0:pb0 + 64, (hh // 2) * 64:(hh // 2 + 1) * 64]),
                        w=[tS])
                    gs(("Sbf", hh), tSb)
                tsq_ = p.sc(lambda h: h.activation(sqs[:, :], pO[:, :], AF.Square,
                                                   bias=0.0, scale=1.0),
                            w=oh + gw("sqs_c"))
                tss = p.ve(lambda h, c=c: h.tensor_reduce(
                    O_ss[:, c * 16:(c + 1) * 16],
                    sqs[:, :].rearrange("p (hh d) -> p hh d", d=64), AX.X, ALU.add),
                    w=[tsq_] + gw("Oss_c"))
                gs("sqs_c", tss)
                tor = p.ve(lambda h, c=c: h.tensor_copy(
                    O_raw[:, c * 1024:(c + 1) * 1024], pO[:, :]),
                    w=[tss] + oh + gw(("Oraw", c)))
                gs("pOr", tor)
                if hf == 0 and c == 0:
                    maybe_dbg("o0", O_raw[:, 0:1024], [tor])
                    maybe_dbg("ss0", O_ss[:, 0:16], [tss])
                    maybe_dbg("sbf0", S_bf[:, :], [GD.get(("Sbf", 15))])
                    maybe_dbg("gcols0", gcols[:, :], [tgc])
                gs(("Braw", c), tor)
                gs(("cols", sl), tE)
                gs(("ktok", sl), tkt)
                gs(("vb", sl), tds)
                gs(("ktil", sl), tds)
                gs(("rkc", sl), tAt)
                gs(("mcol", sl), tkt)
            fB = p.fence()
            A.reset()
            p.set_floor(fB)

            A.mark()
            ztok = A.alloc([128, 2 * 1024], BF16, "l0a_zt")
            ofin = A.alloc([128, 2 * 1024], BF16, "l0a_of")
            ofd = A.alloc([128, 2 * 1024], BF16, "l0a_ofd")
            O_raw2 = O_raw
            pzt = pA[:, 0:512].bitcast(BF16)
            pyw = pA[:, 1024:2048]
            pof = pB[:, 0:512].bitcast(BF16)
            for c in range(8):
                tc = hf * 8 + c
                sl = c % 2
                tt = None
                for zc in range(NDC):
                    tt = p.te(lambda h, zc=zc, c=c: h.transpose(
                        pzt[:, zc * 128:(zc + 1) * 128],
                        z_d[:, zc * 1024 + c * 128: zc * 1024 + (c + 1) * 128],
                        cid[:, :]),
                        w=[sil[("z", zc, hf)], GD.get("pA0r"), GD.get("lnpsum")])
                tzs = p.ve(lambda h, sl=sl: h.tensor_copy(
                    ztok[:, sl * 1024:(sl + 1) * 1024], pzt[:, :]),
                    w=[tt, GD.get(("ztok", sl))])
                gs("pA0r", tzs)
                gs("lnpsum", tzs)
                trs = p.sc(lambda h, c=c: h.activation(
                    lncB[:, 0:16], O_ss[:, c * 16:(c + 1) * 16], AF.Sqrt,
                    bias=1e-6, scale=1.0 / 64.0),
                    w=[GD.get(("Braw", c)), GD.get("lnc_c")])
                trs = p.ve(lambda h: h.reciprocal(lncB[:, 0:16], lncB[:, 0:16]),
                           w=[trs])
                gs("Oss_c", trs)
                t1 = p.ve(lambda h, c=c: h.tensor_tensor(
                    r3(sqs[:, :]), r3(O_raw2[:, c * 1024:(c + 1) * 1024]),
                    bc64(lncB[:, 0:16]), ALU.mult),
                    w=[trs, GD.get(("Braw", c)), GD.get("sqs_c")])
                gs("lnc_c", t1)
                gs(("Oraw", c), t1)
                t2 = p.ve(lambda h, sl=sl: h.tensor_tensor(
                    ofin[:, sl * 1024:(sl + 1) * 1024], sqs[:, :],
                    ztok[:, sl * 1024:(sl + 1) * 1024], ALU.mult),
                    w=[t1, tzs, GD.get(("ofin", sl))])
                gs("sqs_c", t2)
                gs(("ztok", sl), t2)
                if hf == 0 and c == 0:
                    maybe_dbg("zt0", ztok[:, 0:1024], [tzs])
                    maybe_dbg("of0", ofin[:, 0:1024], [t2])
                tt = None
                for zc in range(NDC):
                    tt = p.te(lambda h, zc=zc, sl=sl: h.transpose(
                        pof[:, zc * 128:(zc + 1) * 128],
                        ofin[:, sl * 1024 + zc * 128: sl * 1024 + (zc + 1) * 128],
                        cid[:, :]), w=[t2, GD.get("pB01")])
                gs(("ofin", sl), tt)
                tfd = p.ve(lambda h, sl=sl: h.tensor_copy(
                    ofd[:, sl * 1024:(sl + 1) * 1024], pof[:, :]),
                    w=[tt, GD.get(("ofd", sl))])
                gs("pB01", tfd)
                ty = None
                for dc in range(NDC):
                    for s2 in range(2):
                        ty = p.te(lambda h, dc=dc, s2=s2, sl=sl: h.matmul(
                            pyw[:, s2 * 512:(s2 + 1) * 512],
                            ofd[:, sl * 1024 + dc * 128: sl * 1024 + (dc + 1) * 128],
                            wout_sb[:, dc * 1024 + s2 * 512: dc * 1024 + (s2 + 1) * 512],
                            start=(dc == 0), stop=(dc == NDC - 1)),
                            w=[tfd, tk_wout[dc], GD.get("pA23")])
                gs(("ofd", sl), ty)
                tres = p.ve(lambda h, tc=tc: h.scalar_tensor_tensor(
                    x_tok[:, tc * 1024:(tc + 1) * 1024], pyw[:, :], 1.0,
                    x_tok[:, tc * 1024:(tc + 1) * 1024], ALU.mult, ALU.add),
                    w=[ty, x_ready[tc]])
                gs("pA23", tres)
                gs(("zd", c), tres)
                x_ready[tc] = tres
            fprev = p.fence()
            A.reset()
            A.reset()

    l0a()
    p.set_floor(p.fence())
    A.reset()
    if tap == "l0a":
        return _finish(nc, p, sems, x_tok, y_out, x_ready, ctx)

    A.mark()
    w0 = A.alloc([128, NDC * 1024], F32, "l0f_w")
    tern = A.alloc([128, NDC * 1024], BF16, "l0f_t")
    gbrow = A.alloc([128, 1024], F32, "l0f_g")
    bbrow = A.alloc([128, 1024], F32, "l0f_b")
    gcol0 = A.alloc([128, NDC], F32, "l0f_gc")
    bcol0 = A.alloc([128, NDC], F32, "l0f_bc")
    xaf = A.alloc([128, 1024], F32, "l0f_xa")
    scr0 = A.alloc([128, 1024], F32, "l0f_s")
    qbf = A.alloc([128, 1024], BF16, "l0f_q")
    xqd = A.alloc([128, 1024], BF16, "l0f_xqd")
    red0 = A.alloc([128, 16], F32, "l0f_r")
    stat1 = A.alloc([1, 256], F32, "l0f_st")

    def l0f():
        for dc in range(NDC):
            p.dma("sy", w0[:, dc * 1024:(dc + 1) * 1024], dr["w0f"][dc], sem="Lf")
        p.dma("sy", gcol0[:, :], dr["gam0"], sem="Lf")
        p.dma("sy", bcol0[:, :], dr["bet0"], sem="Lf")
        tw = [p.all("Lf")] * NDC
        tg = p.all("Lf")
        tbc = tg
        pg = pB[:, 0:512]
        tpr = None
        for dc in range(NDC):
            t1 = p.te(lambda h, dc=dc: h.transpose(
                pg[:, 0:128], gcol0[:, dc:dc + 1].broadcast_to((128, 128)),
                cidf[:, :]), w=[tg, tcons, tpr, GD.get("pB01"),
                                GD.get("pB0chain")])
            t2 = p.ve(lambda h, dc=dc: h.tensor_copy(
                gbrow[:, dc * 128:(dc + 1) * 128], pg[:, 0:128]), w=[t1])
            t3 = p.te(lambda h, dc=dc: h.transpose(
                pg[:, 128:256], bcol0[:, dc:dc + 1].broadcast_to((128, 128)),
                cidf[:, :]), w=[tbc, t2])
            tpr = p.ve(lambda h, dc=dc: h.tensor_copy(
                bbrow[:, dc * 128:(dc + 1) * 128], pg[:, 128:256]), w=[t3])
            gs("pB0chain", tpr)
        gs("pB01", tpr)

        def xaff(tc):
            t7 = ln_tile(tc, x_ready[tc], xaf[:, :], extra=gw("xa_c"))
            t8 = p.ve(lambda h: h.tensor_mul(xaf[:, :], xaf[:, :], gbrow[:, :]),
                      w=[t7, tpr])
            t9 = p.ve(lambda h: h.tensor_add(xaf[:, :], xaf[:, :], bbrow[:, :]),
                      w=[t8])
            return t9

        tmx = p.ve(lambda h: h.memset(red0[:, 0:1], 0.0))
        for tc in range(NTC):
            t9 = xaff(tc)
            ta = p.ve(lambda h: h.tensor_reduce(red0[:, 1:2], xaf[:, :], AX.X,
                                                ALU.max, apply_absolute_value=True),
                      w=[t9, tmx])
            gs("xa_c", ta)
            tmx = p.ve(lambda h: h.tensor_tensor(red0[:, 0:1], red0[:, 0:1],
                                                 red0[:, 1:2], ALU.max), w=[ta])
        tt = p.te(lambda h: h.transpose(pB[0:1, 0:128], red0[:, 0:1], cidf[:, :]),
                  w=[tmx, GD.get("pB01"), GD.get("pB0chain")])
        tt = p.ve(lambda h: h.tensor_copy(stat1[0:1, 16:144], pB[0:1, 0:128]),
                  w=[tt])
        tt = p.ve(lambda h: h.tensor_reduce(stat1[0:1, 0:1], stat1[0:1, 16:144],
                                            AX.X, ALU.max), w=[tt])
        gs("pB0chain", tt)
        tz8 = p.ve(lambda h: h.memset(stat1[0:1, 8:16], 0.0))
        tb0 = p.dma("gp", cc_in[0:1, :], stat1[0:1, 8:16], w=[tz8], sem="G7")
        tb1 = p.dma("gp", cc_in[0:1, 0:1], stat1[0:1, 0:1], w=[tt, tb0], sem="G7")
        if nocc:
            tcc = p.dma("gp", cc_out[0:1, :], cc_in[0:1, :], w=[tb1], sem="G7")
        else:
            tcc = p.emit("gp", lambda h: h.collective_compute(
                "AllReduce", ALU.max, replica_groups=[list(range(8))],
                ins=[cc_in.opt()], outs=[cc_out.opt()]),
                waits=[tb1], incs=(("C", 1),))
        tgm = p.dma("gp", stat1[0:1, 1:2], cc_out[0:1, 0:1], w=[tcc], sem="G7")
        tsc_ = p.ve(lambda h: h.tensor_scalar(stat1[0:1, 2:3], stat1[0:1, 1:2],
                                              1e-8, 1.0 / 127.0, ALU.max, ALU.mult),
                    w=[tgm])
        trs_ = p.ve(lambda h: h.reciprocal(stat1[0:1, 3:4], stat1[0:1, 2:3]),
                    w=[tsc_])

        tsm = p.ve(lambda h: h.memset(red0[:, 2:3], 0.0))
        for dc in range(NDC):
            tsm1 = p.ve(lambda h, dc=dc: h.tensor_reduce(
                red0[:, 3:4], w0[:, dc * 1024:(dc + 1) * 1024], AX.X, ALU.add,
                apply_absolute_value=True), w=[tw[dc], tsm])
            tsm = p.ve(lambda h: h.tensor_add(red0[:, 2:3], red0[:, 2:3],
                                              red0[:, 3:4]), w=[tsm1])
        t2 = p.te(lambda h: h.transpose(pB[0:1, 128:256], red0[:, 2:3], cidf[:, :]),
                  w=[tsm, GD.get("pB01"), GD.get("pB0chain")])
        t2 = p.ve(lambda h: h.tensor_copy(stat1[0:1, 16:144], pB[0:1, 128:256]),
                  w=[t2])
        t2 = p.ve(lambda h: h.tensor_reduce(stat1[0:1, 4:5], stat1[0:1, 16:144],
                                            AX.X, ALU.add), w=[t2])
        gs("pB0chain", t2)
        tdl = p.ve(lambda h: h.tensor_scalar_mul(stat1[0:1, 5:6], stat1[0:1, 4:5],
                                                 0.7 / (1024.0 * 1024.0)), w=[t2])
        tdb1 = p.te(lambda h: h.transpose(
            pB[:, 256:257], stat1[0:1, 5:6].broadcast_to((1, 128)), cidf[0:1, 0:1]),
            w=[tdl, GD.get("pB0chain")])
        tdb = p.ve(lambda h: h.tensor_copy(smalls[:, 6:7], pB[:, 256:257]),
                   w=[tdb1])
        gs("pB0chain", tdb)
        tnum = p.ve(lambda h: h.memset(red0[:, 4:6], 0.0))
        ttern = tnum
        for dc in range(NDC):
            wsl = w0[:, dc * 1024:(dc + 1) * 1024]
            ta = p.sc(lambda h, wsl=wsl: h.activation(scr0[:, :], wsl, AF.Abs,
                                                      bias=0.0, scale=1.0),
                      w=[tw[dc], ttern, GD.get("scr0_c")])
            tmk = p.ve(lambda h: h.tensor_scalar(xaf[:, :], scr0[:, :],
                                                 smalls[:, 6:7], None, ALU.is_gt),
                       w=[ta, tdb, GD.get("xa_c")])
            tn1 = p.ve(lambda h: h.tensor_mul(scr0[:, :], scr0[:, :], xaf[:, :]),
                       w=[tmk])
            tn2 = p.ve(lambda h: h.tensor_reduce(red0[:, 6:7], scr0[:, :], AX.X,
                                                 ALU.add), w=[tn1])
            tn3 = p.ve(lambda h: h.tensor_add(red0[:, 4:5], red0[:, 4:5],
                                              red0[:, 6:7]), w=[tn2])
            td1 = p.ve(lambda h: h.tensor_reduce(red0[:, 7:8], xaf[:, :], AX.X,
                                                 ALU.add), w=[tmk])
            td2 = p.ve(lambda h: h.tensor_add(red0[:, 5:6], red0[:, 5:6],
                                              red0[:, 7:8]), w=[td1, tn3])
            tsg = p.sc(lambda h, wsl=wsl: h.activation(scr0[:, :], wsl, AF.Sign,
                                                       bias=0.0, scale=1.0),
                       w=[tn1, tn2])
            gs("scr0_c", None)
            ttern = p.ve(lambda h, dc=dc: h.tensor_mul(
                tern[:, dc * 1024:(dc + 1) * 1024], scr0[:, :], xaf[:, :]),
                w=[tsg, td2])
            gs("xa_c", ttern)
        t3 = p.te(lambda h: h.transpose(pB[0:1, 384:512], red0[:, 4:5], cidf[:, :]),
                  w=[ttern, GD.get("pB01"), GD.get("pB0chain")])
        t3a = p.ve(lambda h: h.tensor_copy(stat1[0:1, 16:144], pB[0:1, 384:512]),
                   w=[t3])
        t3b = p.ve(lambda h: h.tensor_reduce(stat1[0:1, 6:7], stat1[0:1, 16:144],
                                             AX.X, ALU.add), w=[t3a])
        gs("pB0chain", t3b)
        t4 = p.te(lambda h: h.transpose(pB[0:1, 512:640], red0[:, 5:6], cidf[:, :]),
                  w=[ttern, t3, t3b])
        t4a = p.ve(lambda h: h.tensor_copy(stat1[0:1, 16:144], pB[0:1, 512:640]),
                   w=[t4, t3b])
        t4b = p.ve(lambda h: h.tensor_reduce(stat1[0:1, 7:8], stat1[0:1, 16:144],
                                             AX.X, ALU.add), w=[t4a])
        t5 = p.ve(lambda h: h.tensor_scalar_max(stat1[0:1, 7:8], stat1[0:1, 7:8],
                                                1.0), w=[t4b])
        t5 = p.ve(lambda h: h.reciprocal(stat1[0:1, 7:8], stat1[0:1, 7:8]), w=[t5])
        t5 = p.ve(lambda h: h.tensor_mul(stat1[0:1, 6:7], stat1[0:1, 6:7],
                                         stat1[0:1, 7:8]), w=[t5])
        tsa = p.ve(lambda h: h.tensor_mul(stat1[0:1, 6:7], stat1[0:1, 6:7],
                                          stat1[0:1, 2:3]), w=[t5])
        tb2 = p.te(lambda h: h.transpose(
            pB[:, 0:1], stat1[0:1, 3:4].broadcast_to((1, 128)), cidf[0:1, 0:1]),
            w=[trs_, t3, t4b])
        tb2v = p.ve(lambda h: h.tensor_copy(smalls[:, 4:5], pB[:, 0:1]),
                    w=[tb2])
        tb3 = p.te(lambda h: h.transpose(
            pB[:, 128:129], stat1[0:1, 6:7].broadcast_to((1, 128)), cidf[0:1, 0:1]),
            w=[tsa, tb2, tb2v])
        tb3v = p.ve(lambda h: h.tensor_copy(smalls[:, 5:6], pB[:, 128:129]),
                    w=[tb3])
        gs("pB01", tb3v)

        maybe_dbg("l0fstat", stat1[0:1, 0:16], [tsa, tb3v, tb2v, tdb])
        maybe_dbg("gb0", gbrow[:, :], [tpr])
        maybe_dbg("bb0", bbrow[:, :], [tpr])
        maybe_dbg("tern0", tern[:, :], [ttern])
        pxq = pA[:, 0:512].bitcast(BF16)
        pyq = pA[:, 1024:2048]
        for tc in range(NTC):
            t9 = xaff(tc)
            tq1 = p.ve(lambda h: h.tensor_scalar(xaf[:, :], xaf[:, :],
                                                 smalls[:, 4:5], -127.0,
                                                 ALU.mult, ALU.max), w=[t9, tb2v])
            tq2 = p.ve(lambda h: h.tensor_scalar(xaf[:, :], xaf[:, :], 127.0,
                                                 MAGIC, ALU.min, ALU.add), w=[tq1])
            tq3 = p.ve(lambda h: h.tensor_scalar(qbf[:, :], xaf[:, :], MAGIC, None,
                                                 ALU.subtract),
                       w=[tq2, GD.get("qbf_c")])
            gs("xa_c", tq3)
            tt = None
            for dc in range(NDC):
                tt = p.te(lambda h, dc=dc: h.transpose(
                    pxq[:, dc * 128:(dc + 1) * 128],
                    qbf[:, dc * 128:(dc + 1) * 128], cid[:, :]),
                    w=[tq3, GD.get("pA0r"), GD.get("lnpsum")])
            gs("qbf_c", tt)
            txe = p.ve(lambda h: h.tensor_copy(xqd[:, :], pxq[:, :]),
                       w=[tt] + gw("xqd_c"))
            if tc == 0:
                maybe_dbg("q0f", qbf[:, :], [tq3])
                maybe_dbg("xq0", xqd[:, :], [txe])
            gs("pA0r", txe)
            gs("lnpsum", txe)
            ty = None
            for dc in range(NDC):
                for s2 in range(2):
                    ty = p.te(lambda h, dc=dc, s2=s2: h.matmul(
                        pyq[:, s2 * 512:(s2 + 1) * 512],
                        xqd[:, dc * 128:(dc + 1) * 128],
                        tern[:, dc * 1024 + s2 * 512: dc * 1024 + (s2 + 1) * 512],
                        start=(dc == 0), stop=(dc == NDC - 1)),
                        w=[txe, ttern, GD.get("pA23")])
            tres = p.ve(lambda h, tc=tc: h.scalar_tensor_tensor(
                x_tok[:, tc * 1024:(tc + 1) * 1024], pyq[:, :], smalls[:, 5:6],
                x_tok[:, tc * 1024:(tc + 1) * 1024], ALU.mult, ALU.add),
                w=[ty, tb3v, x_ready[tc]])
            gs("pA23", tres)
            gs("xqd_c", ty)
            x_ready[tc] = tres

    l0f()
    p.set_floor(p.fence())
    A.reset()
    if tap == "l0f":
        return _finish(nc, p, sems, x_tok, y_out, x_ready, ctx)

    A.mark()
    vext = A.alloc([128, NTC * 16 * 65], BF16, "l1a_vext")
    vext_re = vext.rearrange("p (t f) -> p t f", t=NTC)
    cosb = A.alloc([128, 2048], BF16, "l1a_cos")
    sinb = A.alloc([128, 2048], BF16, "l1a_sin")
    qpair = A.alloc([128, 2048], BF16, "l1a_qp")
    kpair = A.alloc([128, 2048], BF16, "l1a_kp")
    rscr = A.alloc([128, 2048], BF16, "l1a_rs")
    qext = A.alloc([65, 2048], BF16, "l1a_qe")
    kext = A.alloc([65, 2048], BF16, "l1a_ke")
    odma = A.alloc([128, 8 * 2048], BF16, "l1a_od")
    rho = A.alloc([65, 2048], BF16, "l1a_rho")
    mcols = A.alloc([128, 16], BF16, "l1a_mc")
    mrow = A.alloc([16, 128], BF16, "l1a_mr")
    wstr = A.alloc([128, 2 * NDC * 128], BF16, "l1a_ws")
    lnscr1 = A.alloc([128, 2 * 1024], BF16, "l1a_lns")
    pte = lnscr1[:, 0:1024]
    stage = lnscr1[:, 1024:2048]
    A.mark()
    xnd1 = A.alloc([128, NDC * 2048], BF16, "l1a_xnd")
    xnd1_re = xnd1.rearrange("p (dc t) -> p dc t", dc=NDC)

    def l1a():
        p.dma("gp", cosb[:, :], dr["ccos"], sem="Gcs")
        p.dma("gp", sinb[:, :], dr["csin"], sem="Gcs")
        tk_cos = p.all("Gcs")
        tk_sin = tk_cos
        pbf = pA[:, 0:512].bitcast(BF16)
        ln_toks = ln_to_xnd(list(range(NTC)), xnd1, lnscr1, pbf)
        xnd_all = ln_toks[-1]
        if L1A_LIMIT.get("ones", True):
            tvo = p.ve(lambda h: h.memset(
                vext_re[:, :, :].rearrange("p t (hh f) -> p t hh f", hh=16)[:, :, :, 64:65],
                1.0), w=gw("vext_c"))
        else:
            tvo = None
        pvp = pB[:, 0:2048]
        for cc in range(NDC if L1A_LIMIT.get("vproj", True) else 0):
            sl = cc % 2
            wdst = wstr[:, sl * NDC * 128:(sl + 1) * NDC * 128]
            gsem = "G0" if sl == 0 else "G1"
            twl = p.dma("gp", wdst, dr["wv"][cc].rearrange("p dc c -> p (dc c)"),
                        w=gw(("dmaq", gsem)), sem=gsem)
            tm = None
            for dc in range(NDC):
                for s4 in range(4):
                    tm = p.te(lambda h, dc=dc, s4=s4, wdst=wdst: h.matmul(
                        pvp[:, s4 * 512:(s4 + 1) * 512],
                        wdst[:, dc * 128:(dc + 1) * 128],
                        xnd1_re[:, dc, s4 * 512:(s4 + 1) * 512],
                        start=(dc == 0), stop=(dc == NDC - 1)),
                        w=[twl, xnd_all, GD.get("pBall")])
            gs(("dmaq", gsem), tm)
            tev = p.sc(lambda h: h.activation(rscr[:, :], pvp[:, :], AF.Copy,
                                              bias=0.0, scale=1.0),
                       w=[tm, GD.get("rscr_c")])
            gs("pBall", tev)
            tt0 = None
            if L1A_LIMIT.get("vtrans", True):
                for bat in range(2):
                    pbk = pA[:, bat * 512:(bat + 1) * 512].bitcast(BF16)
                    tt0 = None
                    for ii in range(8):
                        i = bat * 8 + ii
                        tt0 = p.te(lambda h, i=i, ii=ii, pbk=pbk: h.transpose(
                            pbk[:, ii * 128:(ii + 1) * 128],
                            rscr[:, i * 128:(i + 1) * 128], cid[:, :]),
                            w=[tev, GD.get(("vtp", bat)), GD.get("lnpsum")])
                    out_ap = vext_re[:, bat * 8:(bat + 1) * 8, :].rearrange(
                        "p t (hh f) -> p t hh f", hh=16)[:, :, 2 * cc:2 * cc + 2, 0:64]
                    src_ap = pbk[:, :].rearrange("p (t a b) -> p t a b", t=8, a=2)
                    tt1 = p.ve(lambda h, out_ap=out_ap, src_ap=src_ap: h.tensor_copy(
                        out_ap, src_ap), w=[tt0, tvo, GD.get("vext_c")])
                    gs(("vtp", bat), tt1)
                    gs("vext_c", tt1)
                    gs("lnpsum", tt1)
            gs("rscr_c", tt0)

        tko = p.ve(lambda h: h.memset(kext[64:65, :], 1.0), w=gw("kext_c"))

        pqp = pB[0:64, 0:2048]
        pbld = pB[:, 0:2048]
        psmp = pB[:, 0:2048]
        psH = pA[:, 0:1024]
        pOH = pA[0:65, 1024:2048]

        def build_pair(j, dep_stream):
            toks = {}
            for wi, (which, wnm, dst) in enumerate(
                    (("q", "wq", qpair), ("k", "wk", kpair))):
                sl = wi % 2
                wdst = wstr[:, sl * NDC * 128:(sl + 1) * NDC * 128]
                gsem = "G0" if sl == 0 else "G1"
                twl = p.dma("gp", wdst, dr[wnm][j].rearrange("p dc c -> p (dc c)"),
                            w=gw(("dmaq", gsem)), sem=gsem)
                tm = None
                for dc in range(NDC):
                    for s4 in range(4):
                        tm = p.te(lambda h, dc=dc, s4=s4, wdst=wdst: h.matmul(
                            pbld[:, s4 * 512:(s4 + 1) * 512],
                            wdst[:, dc * 128:(dc + 1) * 128],
                            xnd1_re[:, dc, s4 * 512:(s4 + 1) * 512],
                            start=(dc == 0), stop=(dc == NDC - 1)),
                            w=[twl, xnd_all, GD.get("pBall")])
                gs(("dmaq", gsem), tm)
                tev = p.sc(lambda h, dst=dst, which=which: h.activation(
                    dst[:, :], pbld[:, :], AF.Copy, bias=0.0,
                    scale=0.125 if which == "q" else 1.0),
                    w=[tm, GD.get((which + "pair_c"))] + [dep_stream])
                gs("pBall", tev)
                tr = None
                for s4 in range(4):
                    tr = p.te(lambda h, s4=s4, dst=dst: h.matmul(
                        pbld[:, s4 * 512:(s4 + 1) * 512], cP[:, :],
                        dst[:, s4 * 512:(s4 + 1) * 512], start=True, stop=True),
                        w=[tev, tcons])
                t1 = p.ve(lambda h, dst=dst: h.tensor_mul(rscr[:, :], dst[:, :],
                                                          cosb[:, :]),
                          w=[tev, tk_cos, GD.get("rscr_c")])
                t2 = p.ve(lambda h, dst=dst: h.tensor_tensor(
                    dst[:, :], pbld[:, :], sinb[:, :], ALU.mult),
                    w=[tr, tk_sin, t1])
                gs("pBall", t2)
                t3 = p.ve(lambda h, dst=dst: h.tensor_add(dst[:, :], dst[:, :],
                                                          rscr[:, :]), w=[t2])
                gs("rscr_c", t3)
                toks[which] = t3
            return toks

        def extract_head(hh, btoks, dep_q, dep_k):
            pb0 = (hh % 2) * 64
            tqe = p.dma("sy", qext[0:64, :], qpair[pb0:pb0 + 64, :],
                        w=[btoks["q"], dep_q], sem="Lq")
            tke_ = p.dma("sy", kext[0:64, :], kpair[pb0:pb0 + 64, :],
                         w=[btoks["k"], dep_k, tko], sem="Lq")
            gs("qpair_c", tke_)
            gs("kpair_c", tke_)
            tsm = None
            for qc in range(NTC):
                bank = qc % 2
                ts1 = p.te(lambda h, qc=qc, bank=bank: h.matmul(
                    pbld[:, bank * 512:bank * 512 + 256],
                    qext[0:64, qc * 128:(qc + 1) * 128],
                    kext[0:64, :][:, 7::8], start=True, stop=True),
                    w=[tqe, tke_, GD.get(("smp", bank)), GD.get("pBall")])
                tsm = p.ve(lambda h, qc=qc, bank=bank: h.reduce_max(
                    mcols[:, qc:qc + 1], pbld[:, bank * 512:bank * 512 + 256],
                    AX.X, negate=True), w=[ts1, GD.get("mcols_c")])
                gs(("smp", bank), tsm)
            pmr = pB[0:16, 1024:1088].bitcast(BF16)
            tmt = p.te(lambda h: h.transpose(pmr, mcols[:, :], cid[:, :]),
                       w=[tsm, GD.get("pBall")])
            tmv = p.ve(lambda h: h.tensor_copy(mrow[:, :], pmr),
                       w=[tmt, GD.get("mrow_c")])
            gs("pBall", tmv)
            tmr = p.dma("sy", qext[64:65, :].rearrange("o (qc t) -> o qc t", qc=16),
                        mrow[:, :], w=[tmv], slow=True, sem="Lq")
            gs("mcols_c", tmr)
            gs("mrow_c", tmr)
            return tqe, tke_, tmr

        LIM = dict(L1A_LIMIT)
        nheads = LIM.get("heads", 16)
        do_div = LIM.get("div", True)
        do_wo = LIM.get("wo", True)
        if nheads == 0:
            return
        btoks = build_pair(0, None)
        exq, exk, exm = extract_head(0, btoks, None, None)
        ostage_dep = None
        for hh in range(nheads):
            pb0 = (hh % 2) * 64
            cc = hh // 2
            nxt = None
            if hh % 2 == 1 and hh < nheads - 1:
                btoks = build_pair(hh // 2 + 1, GD.get("stream_done"))
            for th in range(2):
                tav = None
                for tsc in range(NTC):
                    tmma = p.te(lambda h, tsc=tsc, th=th: h.matmul(
                        psH[:, 0:512], kext[:, tsc * 128:(tsc + 1) * 128],
                        qext[:, th * 1024:th * 1024 + 512],
                        start=True, stop=True),
                        w=[exq, exk, exm, GD.get(("psH", 0))])
                    tmmb = p.te(lambda h, tsc=tsc, th=th: h.matmul(
                        psH[:, 512:1024], kext[:, tsc * 128:(tsc + 1) * 128],
                        qext[:, th * 1024 + 512:th * 1024 + 1024],
                        start=True, stop=True), w=[GD.get(("psH", 1))])
                    texpa = p.sc(lambda h: h.activation(
                        pte[:, 0:512], psH[:, 0:512], AF.Exp,
                        bias=0.0, scale=1.0),
                        w=[tmma, GD.get(("pte", 0))])
                    gs(("psH", 0), texpa)
                    texpb = p.sc(lambda h: h.activation(
                        pte[:, 512:1024], psH[:, 512:1024], AF.Exp,
                        bias=0.0, scale=1.0),
                        w=[tmmb, GD.get(("pte", 1))])
                    gs(("psH", 1), texpb)
                    for s2 in range(2):
                        tav = p.te(lambda h, tsc=tsc, s2=s2, cc2=hh: h.matmul(
                            pOH[:, s2 * 512:(s2 + 1) * 512],
                            vext_re[:, tsc, :].rearrange("p (hh f) -> p hh f", hh=16)[
                                :, cc2:cc2 + 1, :].rearrange("p a b -> p (a b)"),
                            pte[:, s2 * 512:(s2 + 1) * 512],
                            start=(tsc == 0), stop=(tsc == NTC - 1)),
                            w=[texpa if s2 == 0 else texpb,
                               GD.get("vext_c") if tsc == 0 else None,
                               GD.get("psO_c") if tsc == 0 else None])
                        gs(("pte", s2), tav)
                trr = p.ve(lambda h: h.reciprocal(pOH[64:65, :], pOH[64:65, :]),
                           w=[tav])
                trs = p.ve(lambda h, hh=hh, th=th: h.tensor_copy(
                    rho[64:65, th * 1024:(th + 1) * 1024], pOH[64:65, :]),
                    w=[trr, GD.get("rho_st")])
                trd = p.dma("sy", rho[hh:hh + 1, th * 1024:(th + 1) * 1024],
                            rho[64:65, th * 1024:(th + 1) * 1024], w=[trs],
                            sem="Lr")
                gs("rho_st", trd)
                gs(("rho", hh), trd)
                if pb0 == 0:
                    toe = p.ve(lambda h, cc=cc, th=th: h.tensor_copy(
                        odma[0:64, cc * 2048 + th * 1024: cc * 2048 + (th + 1) * 1024],
                        pOH[0:64, :]), w=[tav, trr, GD.get(("odiv", cc))])
                    gs("psO_c", toe)
                else:
                    tst = p.ve(lambda h: h.tensor_copy(stage[0:64, :], pOH[0:64, :]),
                               w=[tav, trr, ostage_dep])
                    toe = p.dma("sy", odma[64:128, cc * 2048 + th * 1024:
                                           cc * 2048 + (th + 1) * 1024],
                                stage[0:64, :], w=[tst], sem="Lo")
                    ostage_dep = toe
                    gs("psO_c", tst)
                gs(("oev", hh, th), toe)
                gs("stream_done", toe)
            if hh < nheads - 1:
                exq, exk, exm = extract_head(hh + 1, btoks,
                                             GD.get("stream_done"),
                                             GD.get("stream_done"))
        if not do_div:
            return
        pexp = pB[:, 0:2048]
        for j in range(nheads // 2):
            deps = [GD.get(("rho", 2 * j)), GD.get(("rho", 2 * j + 1)),
                    GD.get(("oev", 2 * j, 0)), GD.get(("oev", 2 * j, 1)),
                    GD.get(("oev", 2 * j + 1, 0)), GD.get(("oev", 2 * j + 1, 1))]
            tex = None
            for s4 in range(4):
                tex = p.te(lambda h, j=j, s4=s4: h.matmul(
                    pexp[:, s4 * 512:(s4 + 1) * 512],
                    cbd16[:, j * 128:(j + 1) * 128],
                    rho[0:16, s4 * 512:(s4 + 1) * 512],
                    start=True, stop=True),
                    w=deps + [tcons, GD.get("pBall"), GD.get("psO_c"),
                              p.all("Lr"), p.all("Lo")])
            tdv = p.ve(lambda h, j=j: h.tensor_tensor(
                odma[:, j * 2048:(j + 1) * 2048], odma[:, j * 2048:(j + 1) * 2048],
                pexp[:, :], ALU.mult), w=[tex] + deps)
            gs("pBall", tdv)
            gs(("odiv", j), tdv)
        if not do_wo:
            return
        p.set_floor(p.fence())
        A.reset()
        A.mark()
        wo_sb = A.alloc([128, NDC * 1024], BF16, "l1a_wo")
        for dc in range(NDC):
            p.dma("gp", wo_sb[:, dc * 1024:(dc + 1) * 1024], dr["wo"][dc],
                  sem="Go2")
        two = [p.all("Go2")] * NDC
        pywp = [pA[:, 0:1024], pA[:, 1024:2048]]
        for tc in range(NTC):
            par = tc % 2
            pyw = pywp[par]
            ty = None
            for dc in range(NDC):
                for s2 in range(2):
                    ty = p.te(lambda h, dc=dc, s2=s2, tc=tc, pyw=pyw: h.matmul(
                        pyw[:, s2 * 512:(s2 + 1) * 512],
                        odma[:, dc * 2048 + tc * 128: dc * 2048 + (tc + 1) * 128],
                        wo_sb[:, dc * 1024 + s2 * 512: dc * 1024 + (s2 + 1) * 512],
                        start=(dc == 0), stop=(dc == NDC - 1)),
                        w=[GD.get(("odiv", dc)), two[dc], GD.get(("pyw_c", par))]
                          + ([GD.get(("psH", 0)), GD.get(("psH", 1))] if par == 0
                             else [GD.get("psO_c")]))
            tres = p.ve(lambda h, tc=tc, pyw=pyw: h.scalar_tensor_tensor(
                x_tok[:, tc * 1024:(tc + 1) * 1024], pyw[:, :], 1.0,
                x_tok[:, tc * 1024:(tc + 1) * 1024], ALU.mult, ALU.add),
                w=[ty, x_ready[tc]])
            gs(("pyw_c", par), tres)
            x_ready[tc] = tres

    l1a()
    p.set_floor(p.fence())
    A.reset()
    A.reset()
    if tap == "l1a":
        return _finish(nc, p, sems, x_tok, y_out, x_ready, ctx)

    A.mark()
    xnd2 = A.alloc([128, NDC * 2048], BF16, "l1f_xnd")
    xnd2_re = xnd2.rearrange("p (dc t) -> p dc t", dc=NDC)
    hbuf = A.alloc([128, NIC * 1024], BF16, "l1f_h")
    wdall = A.alloc([128, NIC * 1024], BF16, "l1f_wd")
    wgs = A.alloc([128, 2 * NDC * 128], BF16, "l1f_wg")
    wus = A.alloc([128, 2 * NDC * 128], BF16, "l1f_wu")
    gate = A.alloc([128, 1024], BF16, "l1f_gate")
    cwc = A.alloc([128, NTC * 16], F32, "l1f_cw")
    wr_sb = A.alloc([128, NDC * 16], BF16, "l1f_wr")
    lnscr2 = A.alloc([128, 2 * 1024], BF16, "l1f_lns")
    lgr = lnscr2[0:16, 0:2048]
    prb = A.alloc([128, 32], F32, "l1f_p")
    prm = A.alloc([128, 8], F32, "l1f_pm")

    def l1f():
        pbf = pA[:, 0:512].bitcast(BF16)
        ln_toks = ln_to_xnd(list(range(NTC)), xnd2, lnscr2, pbf)
        xnd_all = ln_toks[-1]
        twr = p.dma("gp",
                    wr_sb[:, :].rearrange("p (dc e) -> p dc e", dc=NDC)[:, :, 0:E_MOE],
                    dr["wr"].transpose([1, 0, 2]), sem="Gr")
        plg = pB[0:E_MOE, 0:2048]
        tmm = None
        for dc in range(NDC):
            for s4 in range(4):
                tmm = p.te(lambda h, dc=dc, s4=s4: h.matmul(
                    plg[:, s4 * 512:(s4 + 1) * 512],
                    wr_sb[:, :].rearrange("p (dc e) -> p dc e", dc=NDC)[:, dc, 0:E_MOE],
                    xnd2_re[:, dc, s4 * 512:(s4 + 1) * 512],
                    start=(dc == 0), stop=(dc == NDC - 1)),
                    w=[twr, xnd_all, GD.get("pBall"), GD.get("pB01")])
        tlg = p.ve(lambda h: h.tensor_copy(lgr[0:E_MOE, :], plg[:, :]), w=[tmm])
        gs("pBall", tlg)
        gs("pB01", tlg)
        for tc in range(NTC):
            plgt = pB[:, 0:8].bitcast(BF16)[:, 0:E_MOE]
            t1 = p.te(lambda h, tc=tc: h.transpose(
                plgt, lgr[0:E_MOE, tc * 128:(tc + 1) * 128],
                cid[0:E_MOE, 0:E_MOE]),
                w=[tlg, GD.get("pB01"), tcons])
            t2 = p.ve(lambda h: h.reduce_max(prb[:, 9:10], plgt,
                                             AX.X, negate=True),
                      w=[t1, GD.get("prb_c")])
            t3 = p.sc(lambda h: h.activation(prb[:, 0:E_MOE], plgt,
                                             AF.Exp, bias=prb[:, 9:10], scale=1.0,
                                             accum_out=prb[:, 10:11]), w=[t2])
            gs("pB01", t3)
            t4 = p.ve(lambda h: h.reciprocal(prb[:, 10:11], prb[:, 10:11]), w=[t3])
            t5 = p.ve(lambda h: h.tensor_scalar_mul(prb[:, 0:E_MOE],
                                                    prb[:, 0:E_MOE], prb[:, 10:11]),
                      w=[t4])
            t6 = p.ve(lambda h: h.reduce_max(prm[:, 0:1], prb[:, 0:E_MOE], AX.X),
                      w=[t5, GD.get("prm_c")])
            tmsk = p.ve(lambda h: h.tensor_scalar(prb[:, 16:16 + E_MOE],
                                                  prb[:, 0:E_MOE], prm[:, 0:1],
                                                  NEG_BIG, ALU.is_ge, ALU.mult),
                        w=[t6])
            tpm = p.ve(lambda h: h.tensor_add(prb[:, 16:16 + E_MOE],
                                              prb[:, 16:16 + E_MOE],
                                              prb[:, 0:E_MOE]), w=[tmsk])
            t8 = p.ve(lambda h: h.reduce_max(prm[:, 1:2], prb[:, 16:16 + E_MOE],
                                             AX.X), w=[tpm])
            tind = p.ve(lambda h: h.tensor_scalar(prb[:, 16:16 + E_MOE],
                                                  prb[:, 0:E_MOE], prm[:, 1:2],
                                                  None, ALU.is_ge), w=[t8])
            tcwu = p.ve(lambda h: h.tensor_mul(prb[:, 16:16 + E_MOE],
                                               prb[:, 16:16 + E_MOE],
                                               prb[:, 0:E_MOE]), w=[tind])
            tsum = p.ve(lambda h: h.tensor_reduce(prm[:, 2:3],
                                                  prb[:, 16:16 + E_MOE], AX.X,
                                                  ALU.add), w=[tcwu])
            trc = p.ve(lambda h: h.reciprocal(prm[:, 2:3], prm[:, 2:3]), w=[tsum])
            tcw = p.ve(lambda h, tc=tc: h.tensor_scalar_mul(
                cwc[:, tc * 16:tc * 16 + E_MOE], prb[:, 16:16 + E_MOE],
                prm[:, 2:3]), w=[trc, GD.get(("cw", tc))])
            gs("prb_c", tcw)
            gs("prm_c", tcw)
            gs(("cw", tc), tcw)
        gs("pyd_c", GD.get(("cw", NTC - 1)))
        gs("pB01x", GD.get(("cw", NTC - 1)))
        maybe_dbg("cw0", cwc[:, :], [GD.get(("cw", tc2)) for tc2 in range(NTC)])
        maybe_dbg("lg0", lgr[0:16, :], [tlg])
        pguA = [pA[:, 0:1024], pA[:, 1024:2048]]
        pguB = [pB[:, 0:1024], pB[:, 1024:2048]]
        pyd = pA[:, 0:1024]
        for e in range(L1F_LIMIT.get("experts", E_MOE)):
            pre = {}
            for ic in range(2):
                sl = ic % 2
                gsg = "G2" if sl == 0 else "G3"
                gsu = "G4" if sl == 0 else "G5"
                pre[(ic, "g")] = p.dma(
                    "gp", wgs[:, sl * NDC * 128:(sl + 1) * NDC * 128],
                    dr["wg"][e][ic].rearrange("p dc c -> p (dc c)"),
                    w=gw(("dmaq", gsg)), sem=gsg)
                pre[(ic, "u")] = p.dma(
                    "gp", wus[:, sl * NDC * 128:(sl + 1) * NDC * 128],
                    dr["wu"][e][ic].rearrange("p dc c -> p (dc c)"),
                    w=gw(("dmaq", gsu)), sem=gsu)
            for ic in range(NIC):
                p.dma("gp", wdall[:, ic * 1024:(ic + 1) * 1024],
                      dr["wd"][e][ic], w=gw(("dmaq", "G6")), sem="G6")
            twd = [p.all("G6")] * NIC
            for th in range(2):
                for ic in range(NIC):
                    sl = ic % 2
                    pgu = pguA[sl]
                    pgu2 = pguB[sl]
                    wgd = wgs[:, sl * NDC * 128:(sl + 1) * NDC * 128]
                    wud = wus[:, sl * NDC * 128:(sl + 1) * NDC * 128]
                    gsg = "G2" if sl == 0 else "G3"
                    gsu = "G4" if sl == 0 else "G5"
                    if th == 0 and (ic, "g") in pre:
                        twg = pre[(ic, "g")]
                        twu = pre[(ic, "u")]
                    else:
                        twg = p.dma("gp", wgd,
                                    dr["wg"][e][ic].rearrange("p dc c -> p (dc c)"),
                                    w=gw(("dmaq", gsg)), sem=gsg)
                        twu = p.dma("gp", wud,
                                    dr["wu"][e][ic].rearrange("p dc c -> p (dc c)"),
                                    w=gw(("dmaq", gsu)), sem=gsu)
                    tg_ = None
                    for dc in range(NDC):
                        for s2 in range(2):
                            tg_ = p.te(lambda h, dc=dc, s2=s2, th=th, wgd=wgd,
                                       pgu=pgu:
                                       h.matmul(pgu[:, s2 * 512:(s2 + 1) * 512],
                                                wgd[:, dc * 128:(dc + 1) * 128],
                                                xnd2_re[:, dc, th * 1024 + s2 * 512:
                                                        th * 1024 + (s2 + 1) * 512],
                                                start=(dc == 0),
                                                stop=(dc == NDC - 1)),
                                       w=[twg, xnd_all, GD.get(("pgu_c", sl)),
                                          GD.get("pyd_r"),
                                          GD.get("lnpsum")])
                    gs(("dmaq", gsg), tg_)
                    gs("lnpsum", tg_)
                    tsg = p.sc(lambda h, pgu=pgu: h.activation(
                        gate[:, :], pgu[:, :], AF.Silu, bias=0.0, scale=1.0),
                        w=[tg_, GD.get("gate_c"), GD.get("gate_dbg")])
                    gs(("pgu_c", sl), tsg)
                    tu_ = None
                    for dc in range(NDC):
                        for s2 in range(2):
                            tu_ = p.te(lambda h, dc=dc, s2=s2, th=th, wud=wud,
                                       pgu2=pgu2:
                                       h.matmul(pgu2[:, s2 * 512:(s2 + 1) * 512],
                                                wud[:, dc * 128:(dc + 1) * 128],
                                                xnd2_re[:, dc, th * 1024 + s2 * 512:
                                                        th * 1024 + (s2 + 1) * 512],
                                                start=(dc == 0),
                                                stop=(dc == NDC - 1)),
                                       w=[twu, xnd_all, GD.get(("pgu2_c", sl)),
                                          GD.get("pB01x")])
                    gs(("dmaq", gsu), tu_)
                    th_ = p.ve(lambda h, ic=ic, pgu2=pgu2: h.tensor_tensor(
                        hbuf[:, ic * 1024:(ic + 1) * 1024], pgu2[:, :], gate[:, :],
                        ALU.mult), w=[tu_, tsg, GD.get(("hb", ic)),
                                      GD.get("hb_dbg")])
                    gs(("pgu2_c", sl), th_)
                    gs("gate_c", th_)
                    gs(("hcur", ic), th_)
                    if e == 0 and th == 0 and ic == 0:
                        tdbg1 = maybe_dbg("g00", gate[:, :], [tsg])
                        tdbg2 = maybe_dbg("h00", hbuf[:, 0:1024], [th_])
                        if tdbg1 is not None:
                            gs("gate_dbg", tdbg1)
                        if tdbg2 is not None:
                            gs("hb_dbg", tdbg2)
                for t2c in range(8):
                    tc = th * 8 + t2c
                    ty = None
                    for ic in range(NIC):
                        for s2 in range(2):
                            ty = p.te(lambda h, ic=ic, s2=s2, t2c=t2c: h.matmul(
                                pyd[:, s2 * 512:(s2 + 1) * 512],
                                hbuf[:, ic * 1024 + t2c * 128:
                                     ic * 1024 + (t2c + 1) * 128],
                                wdall[:, ic * 1024 + s2 * 512:
                                      ic * 1024 + (s2 + 1) * 512],
                                start=(ic == 0), stop=(ic == NIC - 1)),
                                w=[GD.get(("hcur", ic)), twd[ic],
                                   GD.get("pyd_c"), GD.get(("pgu_c", 0)),
                                   GD.get(("pgu_c", 1))])
                    tres = p.ve(lambda h, tc=tc, e=e: h.scalar_tensor_tensor(
                        x_tok[:, tc * 1024:(tc + 1) * 1024], pyd[:, :],
                        cwc[:, tc * 16 + e:tc * 16 + e + 1],
                        x_tok[:, tc * 1024:(tc + 1) * 1024], ALU.mult, ALU.add),
                        w=[ty, GD.get(("cw", tc)), x_ready[tc]])
                    gs("pyd_c", tres)
                    gs("pyd_r", tres)
                    x_ready[tc] = tres
                for ic in range(NIC):
                    gs(("hb", ic), GD.get("pyd_c"))
            gs(("dmaq", "G6"), GD.get("pyd_c"))

    l1f()
    A.reset()

    return _finish(nc, p, sems, x_tok, y_out, x_ready, ctx)


def _finish(nc, p, sems, x_tok, y_out, x_ready, ctx):
    last = None
    for tc in range(NTC):
        last = p.emit("gp", lambda h, tc=tc: h.dma_start(
            out=y_out[tc * 128:(tc + 1) * 128, :],
            in_=x_tok[:, tc * 1024:(tc + 1) * 1024]),
            waits=[x_ready[tc]], incs=(("S", 16),))
    p.emit("gp", None, waits=[last])
    with nc.Block() as block:
        @block.tensor
        def _(h):
            _replay(p, "te", h, sems)

        @block.vector
        def _(h):
            _replay(p, "ve", h, sems)

        @block.scalar
        def _(h):
            _replay(p, "sc", h, sems)

        @block.gpsimd
        def _(h):
            _replay(p, "gp", h, sems)

        @block.sync
        def _(h):
            _replay(p, "sy", h, sems)
    for cm in reversed(ctx):
        cm.__exit__(None, None, None)
    return nc


def _replay(p, eng, h, sems):
    for fn, waits, incs in p.q[eng]:
        for s, v in waits:
            h.wait_ge(sems[s], v)
        if fn is None:
            continue
        inst = fn(h)
        for s, n in incs:
            inst.then_inc(sems[s], n)



def _host_weights(inputs):
    f = np.float32
    W = {}
    wqkvT = np.ascontiguousarray(np.asarray(inputs["L0_attn_W_qkv"]).T.astype(f))
    W["wqkv"] = np.ascontiguousarray(
        wqkvT.reshape(8, 128, 24, 128).transpose(2, 1, 0, 3))
    W["wconv"] = np.ascontiguousarray(
        np.asarray(inputs["L0_attn_conv_w"]).astype(f).reshape(24, 128, KCONV))
    wzT = np.ascontiguousarray(np.asarray(inputs["L0_attn_W_z"]).T.astype(f))
    W["wz"] = np.ascontiguousarray(
        wzT.reshape(8, 128, 8, 128).transpose(2, 1, 0, 3))
    wba = np.zeros((48, 1024), f)
    wba[0:16] = np.asarray(inputs["L0_attn_W_b"]).astype(f)
    wba[32:48] = np.asarray(inputs["L0_attn_W_a"]).astype(f)
    W["wba"] = np.ascontiguousarray(wba.T.reshape(8, 128, 48))
    W["wout"] = np.ascontiguousarray(
        np.asarray(inputs["L0_attn_W_out"]).T.astype(f).reshape(8, 128, 1024))
    W["dtb"] = np.ascontiguousarray(
        np.asarray(inputs["L0_attn_dt_bias"]).astype(f).reshape(16, 1))
    W["alog"] = np.ascontiguousarray(
        np.asarray(inputs["L0_attn_A_log"]).astype(f).reshape(16, 1))
    W["w0f"] = np.ascontiguousarray(
        np.asarray(inputs["L0_ffn_weight"]).T.astype(f).reshape(8, 128, 1024))
    W["gam0"] = np.ascontiguousarray(
        np.asarray(inputs["L0_ffn_gamma"]).astype(f).reshape(8, 128).T)
    W["bet0"] = np.ascontiguousarray(
        np.asarray(inputs["L0_ffn_beta"]).astype(f).reshape(8, 128).T)
    for nm, key in (("wq", "L1_attn_Wq"), ("wk", "L1_attn_Wk"), ("wv", "L1_attn_Wv")):
        wT = np.ascontiguousarray(np.asarray(inputs[key]).T.astype(f))
        W[nm] = np.ascontiguousarray(
            wT.reshape(8, 128, 8, 128).transpose(2, 1, 0, 3))
    W["wo"] = np.ascontiguousarray(
        np.asarray(inputs["L1_attn_Wo"]).T.astype(f).reshape(8, 128, 1024))
    W["wr"] = np.ascontiguousarray(
        np.asarray(inputs["L1_ffn_Wr"]).astype(f).reshape(8, 128, E_MOE))
    W["wg"] = np.ascontiguousarray(
        np.asarray(inputs["L1_ffn_Wg"]).astype(f).reshape(9, 8, 128, 22, 128).transpose(0, 3, 2, 1, 4))
    W["wu"] = np.ascontiguousarray(
        np.asarray(inputs["L1_ffn_Wu"]).astype(f).reshape(9, 8, 128, 22, 128).transpose(0, 3, 2, 1, 4))
    W["wd"] = np.ascontiguousarray(
        np.asarray(inputs["L1_ffn_Wd"]).astype(f).reshape(9, 22, 128, 1024))
    W["cid"] = np.eye(128, dtype=NP_BF16)
    W["cidf"] = np.eye(128, dtype=f)
    P64 = np.zeros((64, 64), f)
    for j in range(32):
        P64[j, 2 * j + 1] = -1.0
        P64[32 + j, 2 * j] = 1.0
    cPm = np.zeros((128, 128), f)
    cPm[0:64, 0:64] = P64.T
    cPm[64:128, 64:128] = P64.T
    W["cP"] = cPm.astype(NP_BF16)
    r = np.arange(128)
    W["ctri"] = np.where(r[None, :] >= r[:, None], 0.0, NEG_BIG).astype(f)
    W["cim"] = np.eye(128, dtype=f)
    bd16 = np.zeros((16, 8 * 128), f)
    for j in range(8):
        for pp in range(128):
            bd16[2 * j + pp // 64, j * 128 + pp] = 1.0
    W["cbd16"] = bd16.astype(NP_BF16)
    W["conesf"] = np.ones((1, 128), f)
    inv = 1.0 / 10000.0 ** (np.arange(0, HD, 2, dtype=f) / HD)
    t = np.arange(T, dtype=f)
    jj = np.arange(128) % 64
    fr = inv[jj % 32]
    ang = t[None, :] * fr[:, None]
    W["ccos"] = np.cos(ang).astype(NP_BF16)
    W["csin"] = np.sin(ang).astype(NP_BF16)
    return W


def kernel(**inputs):
    X = np.ascontiguousarray(np.asarray(inputs["L0_attn_x"], dtype=np.float32))
    W = _host_weights(inputs)
    nc = build_program(None)
    in_maps = []
    for b in range(8):
        m = dict(W)
        m["x"] = np.ascontiguousarray(X[b])
        in_maps.append(m)
    res = run_bass_kernel_spmd(nc, in_maps, core_ids=list(range(8)))
    return np.stack([res.results[b]["y"] for b in range(8)], axis=0)
